# version 05
# baseline (speedup 1.0000x reference)
import math
from contextlib import ExitStack

import numpy as np
import ml_dtypes
import concourse.bass as bass
import concourse.mybir as mybir
from concourse.bass_utils import run_bass_kernel_spmd

F32 = mybir.dt.float32
BF16 = mybir.dt.bfloat16
AF = mybir.ActivationFunctionType
OP = mybir.AluOpType
AXX = mybir.AxisListType.X

B, T, D = 2, 8192, 2048
NH, DH = 16, 128
GRP = 4
DI = 7168
NE, RK = 8, 64
DL = 512
DHR = 64
SCALING = 16.0 / 64.0
NEG = -30000.0
N_SUB = 4
PHASE_LIMIT = 999

BFD = np.dtype(ml_dtypes.bfloat16)
ENGINES = ("sync", "vector", "scalar", "tensor", "gpsimd")
SELF_SYNC = ("vector", "scalar", "gpsimd")


def _dims():
    HPC = NH // GRP
    TOK = T // GRP
    return dict(HPC=HPC, TOK=TOK, NTIL=TOK // 128, NTG=TOK // 512,
                FT=T // 512, NIC=DI // 128, NQB=T // 128)


class Prog:
    def __init__(self):
        self.ops = {e: [] for e in ENGINES}
        self.sem = {}
        self.handles = None
        self.last = {e: None for e in ENGINES}

    def new_sem(self, name):
        self.sem[name] = 0

    def emit(self, eng, fn, inc=None, by=1):
        if eng in SELF_SYNC:
            if self.last[eng] is not None:
                self.wait(eng, *self.last[eng])
            if inc is None:
                inc, by = "sf_" + eng, 1
        if inc is not None:
            self.sem[inc] += by

            def wrapped(E, fn=fn, s=inc, b=by):
                return fn(E).then_inc(self.handles[s], b)
            self.ops[eng].append(wrapped)
            if eng in SELF_SYNC:
                self.last[eng] = (inc, self.sem[inc])
            return self.sem[inc]
        self.ops[eng].append(fn)
        return None

    def wait(self, eng, sem, val):
        if val is None:
            return
        self.ops[eng].append(
            lambda E, s=sem, v=val: E.wait_ge(self.handles[s], v))

    def val(self, sem):
        return self.sem[sem]

    def dma(self, eng, out, in_, inc="st"):
        self.sem[inc] += 16

        def fn(E, o=out, i=in_, s=inc):
            return E.dma_start(out=o, in_=i).then_inc(self.handles[s], 16)
        self.ops[eng].append(fn)
        return self.sem[inc]

    def barrier(self):
        for s, v in self.sem.items():
            if s != "bar" and v > 0:
                self.wait("sync", s, v)
        self.sem["bar"] += 5
        tgt = self.sem["bar"]
        for e in ENGINES:
            def fin(E):
                n = E.engine_nop() if hasattr(E, "engine_nop") else E.nop()
                return n.then_inc(self.handles["bar"], 1)
            self.ops[e].append(fin)
        for e in ENGINES:
            self.wait(e, "bar", tgt)


class Ring:

    def __init__(self, P, nslots, dma_sems=None):
        self.P = P
        self.n = nslots
        self.tok = [[] for _ in range(nslots)]
        self.i = 0
        self.dma_sems = dma_sems
        if dma_sems is not None:
            assert len(dma_sems) == nslots

    def acquire(self, eng):
        s = self.i % self.n
        self.i += 1
        for sem, val in self.tok[s]:
            self.P.wait(eng, sem, val)
        self.tok[s] = []
        if self.dma_sems is not None:
            self.P.wait(eng, self.dma_sems[s], self.P.val(self.dma_sems[s]))
        return s

    def dma(self, eng, s, out, in_):
        return self.P.dma(eng, out, in_, self.dma_sems[s])

    def consumed(self, s, sem, val):
        self.tok[s].append((sem, val))


def k_tiles_for_qblock(qb):
    q0 = qb * 128
    tiles = [(i * 512, 512, False) for i in range(q0 // 512)]
    base = (q0 // 512) * 512
    for j in range((q0 - base) // 128):
        tiles.append((base + j * 128, 128, False))
    tiles.append((q0, 128, True))
    return tiles


def build_nc():
    dm = _dims()
    HPC, TOK, NTIL, NTG, FT = dm["HPC"], dm["TOK"], dm["NTIL"], dm["NTG"], dm["FT"]
    NIC, NQB = dm["NIC"], dm["NQB"]
    nc = bass.Bass("TRN2", target_bir_lowering=False)

    def inp(name, shape, dtype=F32):
        return nc.dram_tensor(name, shape, dtype, kind="ExternalInput").ap()

    x0 = inp("x0", [TOK, D])
    wq = inp("wq", [D, HPC * DH]); wk = inp("wk", [D, HPC * DH])
    wv = inp("wv", [D, HPC * DH])
    wo = inp("wo", [HPC * DH, D])
    wa = inp("wa", [D, 2 * DL]); ba = inp("ba", [128, 8])
    wb = inp("wb", [2 * DL, DHR]); wbs = inp("wbs", [2 * DL, DHR])
    bb = inp("bb", [DHR, 1]); bbs = inp("bbs", [DHR, 1])
    wc = inp("wc", [2 * DL, HPC * DHR]); wcs = inp("wcs", [2 * DL, HPC * DHR])
    bc = inp("bc", [128, 2]); bcs = inp("bcs", [128, 2])
    wuq = inp("wuq", [DL, HPC * DH]); wuk = inp("wuk", [DL, HPC * DH])
    wuv = inp("wuv", [DL, HPC * DH])
    wz = inp("wz", [HPC * DH, D])
    bzrep = inp("bzrep", [128, D])
    wealth = inp("wealth", [NE, 1]); confw = inp("confw", [D, NE])
    confb = inp("confb", [NE, 1])
    bgate = inp("bgate", [NIC * 128, D]); bup = inp("bup", [NIC * 128, D])
    bdown = inp("bdown", [DI, D])
    gA = inp("gA", [D, NE * RK]); uA = inp("uA", [D, NE * RK])
    gB = inp("gB", [NIC * 128, (NE // 2) * 128])
    uB = inp("uB", [NIC * 128, (NE // 2) * 128])
    dA = inp("dA", [DI, NE * RK]); dB = inp("dB", [NE * RK, D])
    identb = inp("identb", [128, 128], BF16)
    identf = inp("identf", [128, 128])
    onesf = inp("onesf", [1, 128])
    alibi5 = inp("alibi5", [HPC, 128, 512])
    alibid = inp("alibid", [HPC, 128, 128])
    alibic = inp("alibic", [128, HPC * 64])
    diagm = inp("diagm", [128, 128])
    coskr = inp("coskr", [DHR, T]); sinkr = inp("sinkr", [DHR, T])
    cosqr = inp("cosqr", [HPC * DHR, T]); sinqr = inp("sinqr", [HPC * DHR, T])

    yout = nc.dram_tensor("y", [TOK, D], F32, kind="ExternalOutput").ap()

    def scrt(name, shape, dtype=BF16):
        return nc.dram_tensor(name, shape, dtype).ap()

    xres = scrt("xres", [TOK, D], F32)
    wpairs = {}
    for nm, w in [("wq", wq), ("wk", wk), ("wv", wv), ("wa", wa),
                  ("bgate", bgate), ("bup", bup), ("bdown", bdown),
                  ("gA", gA), ("uA", uA), ("gB", gB), ("uB", uB),
                  ("confw", confw)]:
        wpairs[nm] = (w, scrt(nm + "_h", list(w.shape)),
                      scrt(nm + "_l", list(w.shape)))
    wq_h, wq_l = wpairs["wq"][1:]
    wk_h, wk_l = wpairs["wk"][1:]
    wv_h, wv_l = wpairs["wv"][1:]
    wa_h, wa_l = wpairs["wa"][1:]
    bgate_h, bgate_l = wpairs["bgate"][1:]
    bup_h, bup_l = wpairs["bup"][1:]
    bdown_h, bdown_l = wpairs["bdown"][1:]
    gA_h, gA_l = wpairs["gA"][1:]
    uA_h, uA_l = wpairs["uA"][1:]
    gB_h, gB_l = wpairs["gB"][1:]
    uB_h, uB_l = wpairs["uB"][1:]
    confw_h, confw_l = wpairs["confw"][1:]
    xn_ci = scrt("xn_ci", [2 * D, TOK]); xn_co = scrt("xn_co", [GRP * 2 * D, TOK])
    h_ci = scrt("h_ci", [2 * DL, TOK], F32)
    h_co = scrt("h_co", [GRP * 2 * DL, TOK], F32)
    pp_ci = scrt("pp_ci", [T, D], F32); pp_co = scrt("pp_co", [TOK, D], F32)
    qh_d = scrt("qh_d", [HPC, DH, T], F32); kh_d = scrt("kh_d", [HPC, DH, T], F32)
    vt_d = scrt("vt_d", [HPC, T, DH], F32)
    qr_d = scrt("qr_d", [HPC * DHR, T], F32); kr_d = scrt("kr_d", [DHR, T], F32)
    yfm_d = scrt("yfm_d", [HPC * DH, T], F32)
    xnfm_d = scrt("xnfm_d", [2 * D, TOK])
    wt_d = scrt("wt_d", [NE, TOK], F32)
    hmix_d = scrt("hmix_d", [NTG, NIC, 128, 1024])
    z_d = scrt("z_d", [NTG, 128, (NE // 2) * 512], F32)

    P = Prog()
    for s in ["ld", "lds", "st", "bar", "mm", "ve", "se", "ge", "cc",
              "sf_vector", "sf_scalar", "sf_gpsimd"]:
        P.new_sem(s)
    for i in range(24):
        P.new_sem(f"r{i}")

    with ExitStack() as stack:
        en = stack.enter_context
        sb_s = en(nc.sbuf_tensor("sb_s", [128, 8192], F32))
        sb_q = en(nc.sbuf_tensor("sb_q", [128, 8192], BF16))
        sb_k = en(nc.sbuf_tensor("sb_k", [128, 8192], BF16))
        sb_v = en(nc.sbuf_tensor("sb_v", [128, 8192], BF16))
        sb_p = en(nc.sbuf_tensor("sb_p", [128, 8192], BF16))
        sb_pt = en(nc.sbuf_tensor("sb_pt", [128, 2048], BF16))
        sb_zb = en(nc.sbuf_tensor("sb_zb", [128, 4096], BF16))
        sb_xt = en(nc.sbuf_tensor("sb_xt", [128, 2048], F32))
        sb_xn = en(nc.sbuf_tensor("sb_xn", [128, 2048], F32))
        sb_fm = en(nc.sbuf_tensor("sb_fm", [128, 4096], BF16))
        sb_wst = en(nc.sbuf_tensor("sb_wst", [128, 4096], BF16))
        sb_ws2 = en(nc.sbuf_tensor("sb_ws2", [128, 4096], BF16))
        sb_lw = en(nc.sbuf_tensor("sb_lw", [128, 4096], BF16))
        sb_sc = en(nc.sbuf_tensor("sb_sc", [128, 4096], F32))
        sb_sc2 = en(nc.sbuf_tensor("sb_sc2", [128, 2048], F32))
        sb_sm = en(nc.sbuf_tensor("sb_sm", [128, 512], F32))
        sb_st = en(nc.sbuf_tensor("sb_st", [128, 160], F32))
        sb_cb = en(nc.sbuf_tensor("sb_cb", [128, 128], BF16))
        sb_zr = en(nc.sbuf_tensor("sb_zr", [1, 640], BF16))
        sb_or = en(nc.sbuf_tensor("sb_or", [1, 128], F32))
        sb_zf = en(nc.sbuf_tensor("sb_zf", [1, 512], F32))
        sb_cf = en(nc.sbuf_tensor("sb_cf", [128, 128], F32))
        sb_rp = en(nc.sbuf_tensor("sb_rp", [128, 1024], F32))
        sb_bi = en(nc.sbuf_tensor("sb_bi", [128, 20], F32))
        for cval, ccol in ((1e-5, 16), (-0.5, 17), (SCALING, 18)):
            nc.const_aps.register(F32, cval, sb_bi[:, ccol:ccol + 1])
        pb = [en(nc.psum_tensor(f"pb{i}", [128, 512], F32)) for i in range(6)]
        ptbs = [en(nc.psum_tensor(f"ptb{i}", [128, 1024], BF16))
                for i in range(2)]
        pb6 = ptbs[0][:, :].bitcast(F32)
        pb7 = ptbs[1][:, :].bitcast(F32)
        ptf = [ptbs[0][:, :].bitcast(F32), ptbs[1][:, :].bitcast(F32)]

        sems = {s: en(nc.semaphore(s)) for s in P.sem}
        P.handles = sems

        sbf = sb_s[:, :].bitcast(BF16)
        sqf = sb_q[:, :].bitcast(F32)
        skf = sb_k[:, :].bitcast(F32)
        svf = sb_v[:, :].bitcast(F32)
        spf = sb_p[:, :].bitcast(F32)
        sptf = sb_pt[:, :].bitcast(F32)
        swstf = sb_wst[:, :].bitcast(F32)
        sws2f = sb_ws2[:, :].bitcast(F32)
        slwf = sb_lw[:, :].bitcast(F32)
        szbf = sb_zb[:, :].bitcast(F32)
        sfmf = sb_fm[:, :].bitcast(F32)
        svbf = sb_v[:, :]

        sb_co = sb_sc
        AL5, ALD = 0, HPC * 512
        DGM = ALD + HPC * 128
        ALC = DGM + 128
        QRING = 2944
        RRING = 3200
        assert RRING + 512 <= 4096
        BA, BB_, BBS, BC_, BCS, CB_, WE_ = 0, 8, 9, 10, 12, 14, 15

        fm_ring = Ring(P, 8, dma_sems=[f"r{i}" for i in range(8)])
        f32_ring = Ring(P, 4, dma_sems=[f"r{i}" for i in range(8, 12)])
        tr_state = {"i": 0, "tok": [None, None]}

        def t_evac(src, dst, n=128, f32=False):
            bk = tr_state["i"] % 2
            off = ((tr_state["i"] // 2) % (4 if f32 else 4)) * 128
            tr_state["i"] += 1
            if tr_state["tok"][bk] is not None:
                P.wait("tensor", "ve", tr_state["tok"][bk])
            if f32:
                dst_ps = ptf[bk][:, off:off + n]
                ident = sb_cf[0:src.shape[0], 0:src.shape[0]]
            else:
                dst_ps = ptbs[bk][:, off:off + n]
                ident = sb_cb[0:src.shape[0], 0:src.shape[0]]
            mm = P.emit("tensor", lambda E, d=dst_ps, a=src, i=ident:
                        E.transpose(d, a, i), inc="mm")
            P.wait("vector", "mm", mm)
            if f32:
                src_ps = ptf[bk][0:dst.shape[0], off:off + n]
            else:
                src_ps = ptbs[bk][0:dst.shape[0], off:off + n]
            ve = P.emit("vector", lambda E, d=dst, s=src_ps:
                        E.tensor_copy(d, s), inc="ve")
            tr_state["tok"][bk] = ve
            return ve

        def t_guard():
            P.wait("tensor", "ve", P.val("ve"))
            tr_state["tok"][0] = None
            tr_state["tok"][1] = None

        def t_release():
            tr_state["tok"][0] = P.val("ve")
            tr_state["tok"][1] = P.val("ve")

        def mm3(bank_ap, lh, ll, rh, rl, first, last, inc=None):
            P.emit("tensor", lambda E: E.matmul(bank_ap, lh, rh,
                                                start=first, stop=False))
            P.emit("tensor", lambda E: E.matmul(bank_ap, lh, rl,
                                                start=False, stop=False))
            return P.emit("tensor", lambda E: E.matmul(
                bank_ap, ll, rh, start=False, stop=last), inc=inc)

        def p_consts():
            P.emit("vector", lambda E: E.memset(sb_zr[:, :], 0.0))
            P.emit("vector", lambda E: E.memset(sb_zf[:, :], 0.0))
            for cval, ccol in ((1e-5, 16), (-0.5, 17), (SCALING, 18)):
                P.emit("vector", lambda E, v=cval, c=ccol:
                       E.memset(sb_bi[:, c:c + 1], v))
            P.dma("sync", sb_cb[:, 0:128], identb, "ld")
            P.dma("sync", sb_cf[:, 0:128], identf, "ld")
            P.dma("sync", sb_bi[:, BA:BA + 8], ba, "ld")
            P.dma("sync", sb_bi[0:DHR, BB_:BB_ + 1], bb, "ld")
            P.dma("sync", sb_bi[0:DHR, BBS:BBS + 1], bbs, "ld")
            P.dma("sync", sb_bi[:, BC_:BC_ + 2], bc, "ld")
            P.dma("sync", sb_bi[:, BCS:BCS + 2], bcs, "ld")
            P.dma("sync", sb_bi[0:NE, CB_:CB_ + 1], confb, "ld")
            P.dma("sync", sb_bi[0:NE, WE_:WE_ + 1], wealth, "ld")
            P.dma("sync", sb_or[0:1, 0:128], onesf, "ld")

        def p_cast_weights():
            ldr = Ring(P, 2)
            stw = Ring(P, 2, dma_sems=["r12", "r13"])
            low = Ring(P, 2, dma_sems=["r14", "r15"])
            for nm, (w32, whi, wlo) in wpairs.items():
                rows, cols = w32.shape
                assert cols <= 4096, (nm, cols)
                for r0 in range(0, rows, 128):
                    rr = min(128, rows - r0)
                    sl = ldr.acquire("sync")
                    src = sb_s[0:rr, sl * 4096:sl * 4096 + cols]
                    ld = P.dma("sync", src, w32[r0:r0 + rr, :], "ld")
                    P.wait("scalar", "ld", ld)
                    so = stw.acquire("scalar")
                    hdst = sb_zb[0:rr, so * 2048:so * 2048 + cols]
                    se = P.emit("scalar", lambda E, d=hdst, s=src: E.copy(d, s))
                    P.wait("sync", "se" if False else "sf_scalar", se)
                    stw.dma("sync", so, whi[r0:r0 + rr, :], hdst)
                    lso = low.acquire("vector")
                    ldst = sb_p[0:rr, lso * 2048:lso * 2048 + cols]
                    P.wait("vector", "sf_scalar", se)
                    ve = P.emit("vector", lambda E, d=ldst, h=hdst, s=src:
                                E.scalar_tensor_tensor(
                                    d, h, -1.0, s, OP.mult, OP.add))
                    ldr.consumed(sl, "sf_vector", ve)
                    stw.consumed(so, "sf_vector", ve)
                    P.wait("sync", "sf_vector", ve)
                    low.dma("sync", lso, wlo[r0:r0 + rr, :], ldst)

        def p_ln_fm(src_ap, fm_dram):
            for tile in range(NTIL):
                t0 = tile * 128
                xt = sb_xt[:, :]
                ld = P.dma("sync", xt, src_ap[t0:t0 + 128, :], "ld")
                P.wait("vector", "ld", ld)
                P.wait("scalar", "ld", ld)
                mcol = sb_st[:, 64:65]
                vcol = sb_st[:, 65:66]
                rcol = sb_st[:, 66:67]
                se = P.emit("scalar", lambda E: E.activation(
                    sb_xn[:, :], xt, AF.Square, accum_out=vcol))
                P.emit("vector", lambda E: E.reduce_sum(mcol, xt, axis=AXX))
                P.emit("vector",
                       lambda E: E.tensor_scalar_mul(mcol, mcol, 1.0 / D))
                P.wait("vector", "sf_scalar", se)
                P.emit("vector",
                       lambda E: E.tensor_scalar_mul(vcol, vcol, 1.0 / D))
                P.emit("vector",
                       lambda E: E.tensor_tensor(rcol, mcol, mcol, OP.mult))
                ve = P.emit("vector", lambda E: E.tensor_tensor(
                    vcol, vcol, rcol, OP.subtract))
                P.wait("scalar", "sf_vector", ve)
                P.emit("scalar",
                       lambda E: E.activation(rcol, vcol, AF.Ln, bias=1e-5))
                se = P.emit("scalar", lambda E: E.activation(
                    rcol, rcol, AF.Exp, scale=-0.5))
                P.wait("vector", "sf_scalar", se)
                P.emit("vector", lambda E: E.tensor_scalar(
                    sb_xn[:, :], xt, mcol, rcol, OP.subtract, OP.mult))
                for half in range(2):
                    if half == 0:
                        ve = P.emit("vector", lambda E: E.tensor_copy(
                            sb_fm[:, 0:2048], sb_xn[:, :]))
                    else:
                        ve = P.emit("vector", lambda E: E.scalar_tensor_tensor(
                            sb_fm[:, 0:2048], sb_fm[:, 0:2048], -1.0,
                            sb_xn[:, :], OP.mult, OP.add))
                    P.wait("tensor", "sf_vector", ve)
                    last = None
                    for c in range(16):
                        last = t_evac(sb_fm[:, c * 128:(c + 1) * 128],
                                      sb_fm[:, 2048 + c * 128:
                                            2048 + (c + 1) * 128])
                    P.wait("sync", "ve", last)
                    st = P.dma("sync",
                               fm_dram[half * D:(half + 1) * D, t0:t0 + 128]
                               .rearrange("(c p) n -> p c n", p=128),
                               sb_fm[:, 2048:4096]
                               .rearrange("p (c n) -> p c n", n=128), "st")
                    P.wait("sync", "st", st)
                    P.wait("vector", "st", st)

        def p_coll(kind, cin, cout):
            op = OP.add if kind == "ReduceScatter" else OP.bypass
            cc = P.emit("gpsimd", lambda E, a=cin, b=cout, k=kind, o=op:
                        E.collective_compute(
                            k, o, replica_groups=[[0, 1, 2, 3], [4, 5, 6, 7]],
                            ins=[a[:, :]], outs=[b[:, :]]), inc="cc")
            P.wait("gpsimd", "cc", cc)

        def p_attn_a_qkv():
            qsc = 1.0 / math.sqrt(DH)
            for wi, (whi, wlo, dstd, sc) in enumerate(
                    ((wq_h, wq_l, qh_d, qsc), (wk_h, wk_l, kh_d, 1.0),
                     (wv_h, wv_l, vt_d, 1.0))):
                P.wait("sync", "mm", P.val("mm"))
                lw = None
                for kc in range(16):
                    lw = P.dma("sync", sb_q[:, kc * 512:(kc + 1) * 512],
                               whi[kc * 128:(kc + 1) * 128, :], "ld")
                    lw = P.dma("sync", sb_k[:, kc * 512:(kc + 1) * 512],
                               wlo[kc * 128:(kc + 1) * 128, :], "ld")
                P.wait("tensor", "ld", lw)
                xguard = [None]
                bank_guard = {}
                for tg in range(FT):
                    rank, tloc = tg // NTG, (tg % NTG) * 512
                    P.wait("sync", "mm", xguard[0])
                    lx = None
                    for kc in range(16):
                        lx = P.dma("sync",
                                   sbf[:, kc * 512:(kc + 1) * 512],
                                   xn_co[rank * 2 * D + kc * 128:
                                         rank * 2 * D + (kc + 1) * 128,
                                         tloc:tloc + 512], "ld")
                        lx = P.dma("sync",
                                   sbf[:, 8192 + kc * 512:8192 + (kc + 1) * 512],
                                   xn_co[rank * 2 * D + D + kc * 128:
                                         rank * 2 * D + D + (kc + 1) * 128,
                                         tloc:tloc + 512], "ld")
                    P.wait("tensor", "ld", lx)
                    lastmm = None
                    for m in range(HPC):
                        bi = m % 2
                        bank = pb[bi]
                        P.wait("tensor", "sf_vector", bank_guard.get(bi))
                        mm = None
                        for kc in range(16):
                            mm = mm3(bank[:, :],
                                     sb_q[:, kc * 512 + m * 128:
                                          kc * 512 + (m + 1) * 128],
                                     sb_k[:, kc * 512 + m * 128:
                                          kc * 512 + (m + 1) * 128],
                                     sbf[:, kc * 512:(kc + 1) * 512],
                                     sbf[:, 8192 + kc * 512:
                                         8192 + (kc + 1) * 512],
                                     first=(kc == 0), last=(kc == 15),
                                     inc="mm" if kc == 15 else None)
                        lastmm = mm
                        P.wait("vector", "mm", mm)
                        so = f32_ring.acquire("vector")
                        dst = sfmf[:, so * 512:(so + 1) * 512]
                        ve = P.emit("vector", lambda E, d=dst, b=bank, s=sc:
                                    E.tensor_scalar_mul(d, b[:, :], s))
                        bank_guard[bi] = ve
                        if wi < 2:
                            P.wait("sync", "sf_vector", ve)
                            f32_ring.dma("sync", so,
                                         dstd[m, :, tg * 512:(tg + 1) * 512],
                                         dst)
                        else:
                            P.wait("tensor", "sf_vector", ve)
                            tvs = None
                            for c in range(4):
                                tvs = t_evac(dst[:, c * 128:(c + 1) * 128],
                                             sptf[:, (tg % 2) * 512 + c * 128:
                                                  (tg % 2) * 512 + (c + 1) * 128],
                                             f32=True)
                            f32_ring.consumed(so, "mm", P.val("mm"))
                            P.wait("sync", "ve", tvs)
                            st = P.dma("sync",
                                       vt_d[m, tg * 512:(tg + 1) * 512, :]
                                       .rearrange("(c p) n -> p c n", p=128),
                                       sptf[:, (tg % 2) * 512:
                                            (tg % 2) * 512 + 512]
                                       .rearrange("p (c n) -> p c n", n=128),
                                       "st")
                            P.wait("vector", "st", st)
                    xguard[0] = lastmm

        def p_attn_heads(layer):
            krq = [swstf, sws2f, slwf, szbf]
            if layer == "a":
                for hh in range(HPC):
                    P.dma("sync", sb_co[:, AL5 + hh * 512:AL5 + (hh + 1) * 512],
                          alibi5[hh, :, :], "ld")
                    P.dma("sync", sb_co[:, ALD + hh * 128:ALD + (hh + 1) * 128],
                          alibid[hh, :, :], "ld")
                P.dma("sync", sb_co[:, ALC:ALC + HPC * 64], alibic, "ld")
            else:
                P.dma("sync", sb_co[:, DGM:DGM + 128], diagm, "ld")
                for qt in range((T + 2047) // 2048):
                    wdt = min(2048, T - qt * 2048)
                    P.dma("sync", krq[qt][0:64, 0:wdt],
                          kr_d[:, qt * 2048:qt * 2048 + wdt], "ld")
            qring = Ring(P, 2, dma_sems=["r16", "r17"])
            rring = Ring(P, 2, dma_sems=["r18", "r19"])
            yring = Ring(P, 2, dma_sems=["r20", "r21"])
            Tv = min(T, 4096)
            for h in range(HPC):
                P.wait("sync", "mm", P.val("mm"))
                l1 = P.dma("sync", skf[:, 0:Tv], kh_d[h, :, 0:Tv], "ld")
                if T > 4096:
                    l1 = P.dma("sync", sqf[:, 0:4096], kh_d[h, :, 4096:8192],
                               "ld")
                l1 = P.dma("sync",
                           svf[:, 0:Tv].rearrange("p (c n) -> p c n", n=128),
                           vt_d[h, 0:Tv, :].rearrange("(c p) n -> p c n",
                                                      p=128), "ld")
                if T > 4096:
                    l1 = P.dma("sync",
                               spf[:, :].rearrange("p (c n) -> p c n", n=128),
                               vt_d[h, 4096:8192, :]
                               .rearrange("(c p) n -> p c n", p=128), "ld")
                P.wait("tensor", "ld", l1)
                sbank = {}
                ytguard = [None]
                for qb in range(NQB):
                    tiles = k_tiles_for_qblock(qb)
                    nt = len(tiles)
                    mcol = sb_st[:, 70:71]
                    nmcol = sb_st[:, 71:72]
                    mt = sb_st[:, 69:70]
                    lsums = sb_st[:, 72:92]
                    lcol = sb_st[:, 94:95]
                    ilcol = sb_st[:, 95:96]
                    qs = qring.acquire("sync")
                    lq = qring.dma("sync", qs,
                                   sb_co[:, QRING + qs * 128:
                                         QRING + (qs + 1) * 128],
                                   qh_d[h, :, qb * 128:(qb + 1) * 128])
                    qsl = sb_co[:, QRING + qs * 128:QRING + (qs + 1) * 128]
                    if layer == "b":
                        rs = rring.acquire("sync")
                        lq = rring.dma("sync", rs,
                                       sb_co[0:64, RRING + rs * 128:
                                             RRING + (rs + 1) * 128],
                                       qr_d[h * DHR:(h + 1) * DHR,
                                            qb * 128:(qb + 1) * 128])
                        rsl = sb_co[0:64, RRING + rs * 128:RRING + (rs + 1) * 128]
                    P.wait("tensor", qring.dma_sems[qs], P.val(qring.dma_sems[qs]))
                    if layer == "b":
                        P.wait("tensor", rring.dma_sems[rs],
                               P.val(rring.dma_sems[rs]))
                    P.wait("tensor", "sf_scalar", P.val("sf_scalar"))
                    lastq = None
                    for j, (k0, w, diag) in enumerate(tiles):
                        bi = j % 2
                        bank = pb[bi]
                        P.wait("tensor", sbank.get(bi, ("ve", None))[0],
                               sbank.get(bi, ("ve", None))[1])
                        kbuf = skf if k0 < 4096 else sqf
                        kofs = k0 if k0 < 4096 else k0 - 4096
                        if layer == "a":
                            mm = P.emit("tensor", lambda E, b=bank, q=qsl,
                                        kb=kbuf, ko=kofs, w=w:
                                        E.matmul(b[:, 0:w], q, kb[:, ko:ko + w],
                                                 start=True, stop=True),
                                        inc="mm")
                        else:
                            P.emit("tensor", lambda E, b=bank, q=qsl, kb=kbuf,
                                   ko=kofs, w=w:
                                   E.matmul(b[:, 0:w], q, kb[:, ko:ko + w],
                                            start=True, stop=False))
                            rbuf = krq[k0 // 2048]
                            rofs = k0 % 2048
                            mm = P.emit("tensor", lambda E, b=bank, r=rsl,
                                        rb=rbuf, ro=rofs, w=w:
                                        E.matmul(b[:, 0:w], r,
                                                 rb[0:64, ro:ro + w],
                                                 start=False, stop=True),
                                        inc="mm")
                        doff = (qb * 128 - k0) // 128
                        if layer == "a":
                            cofs = (ALD + h * 128) if diag else (AL5 + h * 512)
                            P.wait("vector", "mm", mm)
                            tok = P.emit("vector", lambda E, b=bank, k0=k0, w=w,
                                         cofs=cofs, h=h, doff=doff, diag=diag:
                                         E.scalar_tensor_tensor(
                                             sb_s[:, k0:k0 + w], b[:, 0:w],
                                             (0.0 if diag else
                                              sb_co[:, ALC + h * 64 + doff:
                                                    ALC + h * 64 + doff + 1]),
                                             sb_co[:, cofs:cofs + w],
                                             OP.add, OP.add))
                            sbank[bi] = ("sf_vector", tok)
                            P.wait("vector", "sf_vector", tok)
                        elif diag:
                            P.wait("vector", "mm", mm)
                            tok = P.emit("vector", lambda E, b=bank, k0=k0, w=w:
                                         E.tensor_tensor(
                                             sb_s[:, k0:k0 + w], b[:, 0:w],
                                             sb_co[:, DGM:DGM + 128], OP.add))
                            sbank[bi] = ("sf_vector", tok)
                        else:
                            P.wait("scalar", "mm", mm)
                            tok = P.emit("scalar", lambda E, b=bank, k0=k0, w=w:
                                         E.copy(sb_s[:, k0:k0 + w], b[:, 0:w]))
                            sbank[bi] = ("sf_scalar", tok)
                            P.wait("vector", "sf_scalar", tok)
                        P.emit("vector", lambda E, k0=k0, w=w:
                               E.reduce_max(mt, sb_s[:, k0:k0 + w], axis=AXX))
                        if j == 0:
                            P.emit("vector", lambda E: E.tensor_copy(mcol, mt))
                        else:
                            P.emit("vector", lambda E:
                                   E.tensor_tensor(mcol, mcol, mt, OP.max))
                    qring.consumed(qs, "mm", P.val("mm"))
                    if layer == "b":
                        rring.consumed(rs, "mm", P.val("mm"))
                    ge = P.emit("vector", lambda E:
                                E.tensor_scalar_mul(nmcol, mcol, -1.0))
                    P.wait("scalar", "sf_vector", ge)
                    opsum = pb[4]
                    pguard = {}
                    for j, (k0, w, diag) in enumerate(tiles):
                        se = P.emit("scalar", lambda E, k0=k0, w=w, j=j:
                                    E.activation(sb_s[:, k0:k0 + w],
                                                 sb_s[:, k0:k0 + w],
                                                 AF.Exp, bias=nmcol,
                                                 accum_out=lsums[:, j:j + 1]))
                        P.wait("tensor", "sf_scalar", se)
                        lastpv = None
                        for c in range(w // 128):
                            P.wait("tensor", "mm", pguard.get(j % 2))
                            ve = t_evac(sb_s[:, k0 + c * 128:k0 + (c + 1) * 128],
                                        sptf[:, (j % 2) * 512 + c * 128:
                                             (j % 2) * 512 + (c + 1) * 128],
                                        f32=True)
                            P.wait("tensor", "ve", ve)
                            kc = k0 // 128 + c
                            vbuf = svf if kc < 32 else spf
                            vofs = (kc % 32) * 128
                            lastpv = P.emit(
                                "tensor", lambda E, j=j, c=c, vb=vbuf, vo=vofs,
                                first=(j == 0 and c == 0),
                                last=(j == nt - 1 and c == w // 128 - 1):
                                E.matmul(opsum[:, 0:128],
                                         sptf[:, (j % 2) * 512 + c * 128:
                                              (j % 2) * 512 + (c + 1) * 128],
                                         vb[:, vo:vo + 128],
                                         start=first, stop=last),
                                inc="mm" if (j == nt - 1 and c == w // 128 - 1)
                                else None)
                        pguard[j % 2] = P.val("mm")
                    mmend = P.val("mm")
                    P.wait("vector", "sf_scalar", P.val("sf_scalar"))
                    P.emit("vector", lambda E, nt=nt:
                           E.reduce_sum(lcol, lsums[:, 0:nt], axis=AXX))
                    P.emit("vector", lambda E: E.reciprocal(ilcol, lcol))
                    P.wait("vector", "mm", mmend)
                    P.wait("vector", "mm", ytguard[0])
                    ysl = yring.acquire("vector")
                    yv = P.emit("vector", lambda E, ysl=ysl:
                                E.tensor_scalar_mul(
                                    sfmf[:, ysl * 256:ysl * 256 + 128],
                                    opsum[:, 0:128], ilcol))
                    P.wait("tensor", "sf_vector", yv)
                    yve = t_evac(sfmf[:, ysl * 256:ysl * 256 + 128],
                                 sfmf[:, ysl * 256 + 128:ysl * 256 + 256],
                                 f32=True)
                    ytguard[0] = P.val("mm")
                    P.wait("sync", "ve", yve)
                    yring.dma("sync", ysl,
                              yfm_d[h * DH:(h + 1) * DH,
                                    qb * 128:(qb + 1) * 128],
                              sfmf[:, ysl * 256 + 128:ysl * 256 + 256])
                P.wait("sync", "st", P.val("st"))

        def p_partial_proj(w_ap):
            lw = None
            for c in range(4):
                wbuf = skf if c < 2 else sqf
                lw = P.dma("sync", wbuf[:, (c % 2) * 2048:(c % 2 + 1) * 2048],
                           w_ap[c * 128:(c + 1) * 128, :], "ld")
            P.wait("tensor", "ld", lw)
            yr = Ring(P, 2, dma_sems=["r16", "r17"])
            bank_guard = {}
            for t in range(T // 128):
                sl = yr.acquire("sync")
                ld = yr.dma("sync", sl,
                            sb_sc[:, sl * 512:sl * 512 + 512]
                            .rearrange("p (c n) -> p c n", n=128),
                            yfm_d[:, t * 128:(t + 1) * 128]
                            .rearrange("(c p) n -> p c n", p=128))
                P.wait("tensor", yr.dma_sems[sl], P.val(yr.dma_sems[sl]))
                lastmm = None
                for n in range(4):
                    bank = pb[n % 4]
                    P.wait("tensor", "sf_vector", bank_guard.get(n % 4))
                    mm = None
                    for kc in range(4):
                        wbuf = skf if kc < 2 else sqf
                        mm = P.emit("tensor", lambda E, b=bank, sl=sl, kc=kc,
                                    n=n, wb=wbuf,
                                    first=(kc == 0), last=(kc == 3):
                                    E.matmul(b[:, :],
                                             sb_sc[:, sl * 512 + kc * 128:
                                                   sl * 512 + (kc + 1) * 128],
                                             wb[:, (kc % 2) * 2048 + n * 512:
                                                (kc % 2) * 2048 + (n + 1) * 512],
                                             start=first, stop=last),
                                    inc="mm" if kc == 3 else None)
                    lastmm = mm
                    P.wait("vector", "mm", mm)
                    so = f32_ring.acquire("vector")
                    dst = sfmf[:, so * 512:(so + 1) * 512]
                    ve = P.emit("vector", lambda E, d=dst, b=bank:
                                E.tensor_copy(d, b[:, :]))
                    bank_guard[n % 4] = ve
                    P.wait("sync", "sf_vector", ve)
                    f32_ring.dma("sync", so,
                                 pp_ci[t * 128:(t + 1) * 128,
                                       n * 512:(n + 1) * 512], dst)
                yr.consumed(sl, "mm", lastmm)

        def p_residual(src_ap, dst_ap, with_bz):
            if with_bz:
                lb = P.dma("sync", sb_xn[:, :], bzrep[:, :], "ld")
                P.wait("vector", "ld", lb)
            for t in range(NTIL):
                t0 = t * 128
                l1 = P.dma("sync", sb_sc[:, 0:2048], pp_co[t0:t0 + 128, :], "ld")
                l2 = P.dma("sync", sb_xt[:, :], src_ap[t0:t0 + 128, :], "ld")
                P.wait("vector", "ld", l2)
                buf = sb_sc2[:, 0:2048]
                if with_bz:
                    P.emit("vector", lambda E, buf=buf: E.tensor_tensor(
                        buf, sb_sc[:, 0:2048], sb_xn[:, :], OP.add))
                    ve = P.emit("vector", lambda E, buf=buf: E.tensor_tensor(
                        buf, buf, sb_xt[:, :], OP.add))
                else:
                    ve = P.emit("vector", lambda E, buf=buf: E.tensor_tensor(
                        buf, sb_sc[:, 0:2048], sb_xt[:, :], OP.add))
                P.wait("sync", "sf_vector", ve)
                st = P.dma("sync", dst_ap[t0:t0 + 128, :], buf, "st")
                P.wait("vector", "st", st)

        def p_attn_b_h():
            lw = None
            for kc in range(16):
                hbuf = sb_q if kc < 8 else sb_k
                lbuf = sb_v if kc < 8 else sb_p
                lw = P.dma("sync", hbuf[:, (kc % 8) * 1024:(kc % 8 + 1) * 1024],
                           wa_h[kc * 128:(kc + 1) * 128, :], "ld")
                lw = P.dma("sync", lbuf[:, (kc % 8) * 1024:(kc % 8 + 1) * 1024],
                           wa_l[kc * 128:(kc + 1) * 128, :], "ld")
            P.wait("tensor", "ld", lw)
            xguard = [None]
            bank_guard = {}
            for tg in range(NTG):
                tloc = tg * 512
                P.wait("sync", "mm", xguard[0])
                lx = None
                for kc in range(16):
                    lx = P.dma("sync", sbf[:, kc * 512:(kc + 1) * 512],
                               xn_ci[kc * 128:(kc + 1) * 128, tloc:tloc + 512],
                               "ld")
                    lx = P.dma("sync",
                               sbf[:, 8192 + kc * 512:8192 + (kc + 1) * 512],
                               xn_ci[D + kc * 128:D + (kc + 1) * 128,
                                     tloc:tloc + 512], "ld")
                P.wait("tensor", "ld", lx)
                lastmm = None
                for m in range(8):
                    bi = m % 2
                    bank = pb[bi]
                    P.wait("tensor", "sf_vector", bank_guard.get(bi))
                    mm = None
                    for kc in range(16):
                        hbuf = sb_q if kc < 8 else sb_k
                        lbuf = sb_v if kc < 8 else sb_p
                        mm = mm3(bank[:, :],
                                 hbuf[:, (kc % 8) * 1024 + m * 128:
                                      (kc % 8) * 1024 + (m + 1) * 128],
                                 lbuf[:, (kc % 8) * 1024 + m * 128:
                                      (kc % 8) * 1024 + (m + 1) * 128],
                                 sbf[:, kc * 512:(kc + 1) * 512],
                                 sbf[:, 8192 + kc * 512:8192 + (kc + 1) * 512],
                                 first=(kc == 0), last=(kc == 15),
                                 inc="mm" if kc == 15 else None)
                    lastmm = mm
                    P.wait("vector", "mm", mm)
                    so = f32_ring.acquire("vector")
                    dst = sfmf[:, so * 512:(so + 1) * 512]
                    ve = P.emit("vector", lambda E, d=dst, b=bank, m=m:
                                E.tensor_scalar_add(d, b[:, :],
                                                    sb_bi[:, BA + m:BA + m + 1]))
                    bank_guard[bi] = ve
                    P.wait("sync", "sf_vector", ve)
                    f32_ring.dma("sync", so,
                                 h_ci[m * 128:(m + 1) * 128, tloc:tloc + 512],
                                 dst)
                xguard[0] = lastmm

        def p_attn_b_proj():
            lw = None
            for c in range(4):
                lw = P.dma("sync", sqf[:, c * 512:(c + 1) * 512],
                           wuq[c * 128:(c + 1) * 128, :], "ld")
                lw = P.dma("sync", skf[:, c * 512:(c + 1) * 512],
                           wuk[c * 128:(c + 1) * 128, :], "ld")
                lw = P.dma("sync", svf[:, c * 512:(c + 1) * 512],
                           wuv[c * 128:(c + 1) * 128, :], "ld")
            for kc in range(8):
                lw = P.dma("sync", spf[:, kc * 256:(kc + 1) * 256],
                           wc[kc * 128:(kc + 1) * 128, :], "ld")
                lw = P.dma("sync", sqf[:, 2048 + kc * 256:2048 + (kc + 1) * 256],
                           wcs[kc * 128:(kc + 1) * 128, :], "ld")
                lw = P.dma("sync", skf[:, 2048 + kc * 64:2048 + (kc + 1) * 64],
                           wb[kc * 128:(kc + 1) * 128, :], "ld")
                lw = P.dma("sync", skf[:, 2560 + kc * 64:2560 + (kc + 1) * 64],
                           wbs[kc * 128:(kc + 1) * 128, :], "ld")
            P.wait("tensor", "ld", lw)
            s192 = (DH + DHR) ** (-0.5)
            xguard = [None]
            bank_guard = {}
            for tg in range(FT):
                rank, tloc = tg // NTG, (tg % NTG) * 512
                half = (tg % 2) * 4096
                P.wait("sync", "mm", xguard[0] if tg >= 2 else None)
                lx = None
                for kc in range(8):
                    lx = P.dma("sync",
                               sb_s[:, half + kc * 512:half + (kc + 1) * 512],
                               h_co[rank * 2 * DL + kc * 128:
                                    rank * 2 * DL + (kc + 1) * 128,
                                    tloc:tloc + 512], "ld")
                P.wait("tensor", "ld", lx)
                lastmm = None
                for wi, (wbuf, dstd) in enumerate(
                        ((sqf, qh_d), (skf, kh_d), (svf, vt_d))):
                    kcs = range(4, 8) if wi == 0 else range(0, 4)
                    for m in range(HPC):
                        bi = m % 2
                        bank = pb[bi]
                        P.wait("tensor", "sf_vector", bank_guard.get(bi))
                        mm = None
                        for j, kc in enumerate(kcs):
                            mm = P.emit("tensor", lambda E, b=bank, wb=wbuf,
                                        m=m, j=j, kc=kc, half=half,
                                        first=(j == 0), last=(j == 3):
                                        E.matmul(b[:, :],
                                                 wb[:, j * 512 + m * 128:
                                                    j * 512 + (m + 1) * 128],
                                                 sb_s[:, half + kc * 512:
                                                      half + (kc + 1) * 512],
                                                 start=first, stop=last),
                                        inc="mm" if j == 3 else None)
                        lastmm = mm
                        P.wait("vector", "mm", mm)
                        so = f32_ring.acquire("vector")
                        dst = sfmf[:, so * 512:(so + 1) * 512]
                        sc = s192 if wi == 0 else 1.0
                        ve = P.emit("vector", lambda E, d=dst, b=bank, s=sc:
                                    E.tensor_scalar_mul(d, b[:, :], s))
                        bank_guard[bi] = ve
                        if wi < 2:
                            P.wait("sync", "sf_vector", ve)
                            f32_ring.dma("sync", so,
                                         dstd[m, :, tg * 512:(tg + 1) * 512],
                                         dst)
                        else:
                            P.wait("tensor", "sf_vector", ve)
                            tvs = None
                            for c in range(4):
                                tvs = t_evac(dst[:, c * 128:(c + 1) * 128],
                                             sptf[:, (tg % 2) * 512 + c * 128:
                                                  (tg % 2) * 512 + (c + 1) * 128],
                                             f32=True)
                            f32_ring.consumed(so, "mm", P.val("mm"))
                            P.wait("sync", "ve", tvs)
                            st = P.dma("sync",
                                       vt_d[m, tg * 512:(tg + 1) * 512, :]
                                       .rearrange("(c p) n -> p c n", p=128),
                                       sptf[:, (tg % 2) * 512:
                                            (tg % 2) * 512 + 512]
                                       .rearrange("p (c n) -> p c n", n=128),
                                       "st")
                            P.wait("vector", "st", st)
                for m2 in range(2):
                    for sw in range(2):
                        bank = pb[2 + sw]
                        P.wait("tensor", "sf_vector", bank_guard.get(2 + sw))
                        wsl = (spf, 0) if sw == 0 else (sqf, 2048)
                        mm = None
                        for kc in range(8):
                            mm = P.emit("tensor", lambda E, b=bank, wb=wsl[0],
                                        wo0=wsl[1], m2=m2, kc=kc, half=half,
                                        first=(kc == 0), last=(kc == 7):
                                        E.matmul(b[:, :],
                                                 wb[:, wo0 + kc * 256 + m2 * 128:
                                                    wo0 + kc * 256 + (m2 + 1) * 128],
                                                 sb_s[:, half + kc * 512:
                                                      half + (kc + 1) * 512],
                                                 start=first, stop=last),
                                        inc="mm" if kc == 7 else None)
                        lastmm = mm
                        P.wait("vector", "mm", mm)
                        dstf = sb_sc[:, 0:512] if sw == 0 else sb_sc[:, 512:1024]
                        bia = (BC_ + m2) if sw == 0 else (BCS + m2)
                        ve = P.emit("vector", lambda E, d=dstf, b=bank, bia=bia:
                                    E.tensor_scalar_add(
                                        d, b[:, :], sb_bi[:, bia:bia + 1]))
                        bank_guard[2 + sw] = ve
                    lr = P.dma("sync", sb_rp[:, 0:512],
                               cosqr[m2 * 128:(m2 + 1) * 128,
                                     tg * 512:(tg + 1) * 512], "ld")
                    lr = P.dma("sync", sb_rp[:, 512:1024],
                               sinqr[m2 * 128:(m2 + 1) * 128,
                                     tg * 512:(tg + 1) * 512], "ld")
                    P.wait("vector", "ld", lr)
                    P.emit("vector", lambda E: E.tensor_tensor(
                        sb_sc[:, 1024:1536], sb_sc[:, 0:512], sb_rp[:, 0:512],
                        OP.mult))
                    P.emit("vector", lambda E: E.tensor_tensor(
                        sb_sc[:, 1536:2048], sb_sc[:, 512:1024],
                        sb_rp[:, 512:1024], OP.mult))
                    P.emit("vector", lambda E: E.tensor_tensor(
                        sb_sc[:, 1024:1536], sb_sc[:, 1024:1536],
                        sb_sc[:, 1536:2048], OP.add))
                    so = f32_ring.acquire("vector")
                    dst = sfmf[:, so * 512:(so + 1) * 512]
                    ve = P.emit("vector", lambda E, d=dst:
                                E.tensor_scalar_mul(d, sb_sc[:, 1024:1536],
                                                    s192))
                    P.wait("sync", "sf_vector", ve)
                    f32_ring.dma("sync", so,
                                 qr_d[m2 * 128:(m2 + 1) * 128,
                                      tg * 512:(tg + 1) * 512], dst)
                for sw in range(2):
                    bank = pb[4 + sw]
                    P.wait("tensor", "sf_vector", bank_guard.get(4 + sw))
                    wofs = 2048 if sw == 0 else 2560
                    mm = None
                    for kc in range(8):
                        mm = P.emit("tensor", lambda E, b=bank, wofs=wofs,
                                    kc=kc, half=half,
                                    first=(kc == 0), last=(kc == 7):
                                    E.matmul(b[0:64, :],
                                             skf[:, wofs + kc * 64:
                                                 wofs + (kc + 1) * 64],
                                             sb_s[:, half + kc * 512:
                                                  half + (kc + 1) * 512],
                                             start=first, stop=last),
                                    inc="mm" if kc == 7 else None)
                    lastmm = mm
                    P.wait("vector", "mm", mm)
                    dstf = sb_sc[0:64, 0:512] if sw == 0 else sb_sc[0:64, 512:1024]
                    bia = BB_ if sw == 0 else BBS
                    ve = P.emit("vector", lambda E, d=dstf, b=bank, bia=bia:
                                E.tensor_scalar_add(d, b[0:64, :],
                                                    sb_bi[0:64, bia:bia + 1]))
                    bank_guard[4 + sw] = ve
                lr = P.dma("sync", sb_rp[0:64, 0:512],
                           coskr[:, tg * 512:(tg + 1) * 512], "ld")
                lr = P.dma("sync", sb_rp[0:64, 512:1024],
                           sinkr[:, tg * 512:(tg + 1) * 512], "ld")
                P.wait("vector", "ld", lr)
                P.emit("vector", lambda E: E.tensor_tensor(
                    sb_sc[0:64, 1024:1536], sb_sc[0:64, 0:512],
                    sb_rp[0:64, 0:512], OP.mult))
                P.emit("vector", lambda E: E.tensor_tensor(
                    sb_sc[0:64, 1536:2048], sb_sc[0:64, 512:1024],
                    sb_rp[0:64, 512:1024], OP.mult))
                so = f32_ring.acquire("vector")
                dst = sfmf[0:64, so * 512:(so + 1) * 512]
                ve = P.emit("vector", lambda E, d=dst: E.tensor_tensor(
                    d, sb_sc[0:64, 1024:1536], sb_sc[0:64, 1536:2048], OP.add))
                P.wait("sync", "sf_vector", ve)
                f32_ring.dma("sync", so, kr_d[:, tg * 512:(tg + 1) * 512], dst)
                xguard[0] = lastmm

        def p_routing():
            lw = None
            for kc in range(16):
                lw = P.dma("sync", sb_zb[:, kc * 8:(kc + 1) * 8],
                           confw_h[kc * 128:(kc + 1) * 128, :], "ld")
                lw = P.dma("sync", sb_zb[:, 128 + kc * 8:128 + (kc + 1) * 8],
                           confw_l[kc * 128:(kc + 1) * 128, :], "ld")
            P.wait("tensor", "ld", lw)
            xguard = [None]
            for tg in range(NTG):
                tloc = tg * 512
                P.wait("sync", "mm", xguard[0])
                lx = None
                for kc in range(16):
                    lx = P.dma("sync", sbf[:, kc * 512:(kc + 1) * 512],
                               xnfm_d[kc * 128:(kc + 1) * 128, tloc:tloc + 512],
                               "ld")
                    lx = P.dma("sync",
                               sbf[:, 8192 + kc * 512:8192 + (kc + 1) * 512],
                               xnfm_d[D + kc * 128:D + (kc + 1) * 128,
                                      tloc:tloc + 512], "ld")
                P.wait("tensor", "ld", lx)
                mm = None
                combos = [(0, 0), (0, 8192), (128, 0)]
                for ci, (wofs_, xofs_) in enumerate(combos):
                    for kc in range(16):
                        lastf = (ci == 2 and kc == 15)
                        mm = P.emit("tensor", lambda E, kc=kc, w0=wofs_,
                                    x0=xofs_, first=(ci == 0 and kc == 0),
                                    last=lastf:
                                    E.matmul(pb[0][0:NE, :],
                                             sb_zb[:, w0 + kc * 8:
                                                   w0 + (kc + 1) * 8],
                                             sbf[:, x0 + kc * 512:
                                                 x0 + (kc + 1) * 512],
                                             start=first, stop=last),
                                    inc="mm" if lastf else None)
                xguard[0] = mm
                P.wait("vector", "mm", mm)
                ve = P.emit("vector", lambda E, tg=tg: E.tensor_scalar_add(
                    sb_sc[0:NE, tg * 512:(tg + 1) * 512], pb[0][0:NE, :],
                    sb_bi[0:NE, CB_:CB_ + 1]))
                P.wait("tensor", "sf_vector", ve)
            ve = P.val("sf_vector")
            P.wait("scalar", "sf_vector", ve)
            se = P.emit("scalar", lambda E: E.activation(
                sb_sc[0:NE, 0:TOK], sb_sc[0:NE, 0:TOK], AF.Sigmoid))
            P.wait("vector", "sf_scalar", se)
            ve = P.emit("vector", lambda E: E.tensor_scalar_mul(
                sb_sc[0:NE, 0:TOK], sb_sc[0:NE, 0:TOK],
                sb_bi[0:NE, WE_:WE_ + 1]))
            P.wait("tensor", "sf_vector", ve)
            for t in range(NTIL):
                mm = P.emit("tensor", lambda E, t=t: E.transpose(
                    pb[1][:, 0:NE], sb_sc[0:NE, t * 128:(t + 1) * 128],
                    sb_cf[0:NE, 0:NE]), inc="mm")
                P.wait("vector", "mm", mm)
                ve = P.emit("vector", lambda E, t=t: E.tensor_copy(
                    sb_sc2[:, t * 8:(t + 1) * 8], pb[1][:, 0:NE]))
                P.wait("tensor", "sf_vector", ve)
            for t in range(NTIL):
                bid = sb_sc2[:, t * 8:(t + 1) * 8]
                top8 = sb_st[:, 96:104]
                m1 = sb_st[:, 104:105]
                m2 = sb_st[:, 105:106]
                d12 = sb_st[:, 106:107]
                w1 = sb_st[:, 107:108]
                w2 = sb_st[:, 108:109]
                eq1 = sb_st[:, 112:120]
                eq2 = sb_st[:, 120:128]
                ne1 = sb_st[:, 128:136]
                acc = sb_st[:, 136:144]
                P.emit("vector", lambda E, bid=bid: E.max(out=top8, in_=bid))
                P.emit("vector", lambda E: E.tensor_copy(m1, top8[:, 0:1]))
                P.emit("vector", lambda E: E.tensor_copy(m2, top8[:, 1:2]))
                ve = P.emit("vector", lambda E: E.tensor_scalar(
                    d12, m1, m2, None, OP.subtract))
                P.wait("scalar", "sf_vector", ve)
                se = P.emit("scalar", lambda E:
                            E.activation(w1, d12, AF.Sigmoid))
                P.wait("vector", "sf_scalar", se)
                P.emit("vector", lambda E: E.tensor_scalar(
                    w2, w1, -1.0, 1.0, OP.mult, OP.add))
                P.emit("vector", lambda E, bid=bid: E.tensor_scalar(
                    eq1, bid, m1, None, OP.is_equal))
                P.emit("vector", lambda E, bid=bid: E.tensor_scalar(
                    eq2, bid, m2, None, OP.is_equal))
                P.emit("vector", lambda E: E.tensor_scalar(
                    ne1, eq1, -1.0, 1.0, OP.mult, OP.add))
                P.emit("vector", lambda E: E.tensor_scalar(acc, eq1, w1, None,
                                                           OP.mult))
                P.emit("vector", lambda E: E.tensor_tensor(eq2, eq2, ne1,
                                                           OP.mult))
                P.emit("vector", lambda E: E.tensor_scalar(eq2, eq2, w2, None,
                                                           OP.mult))
                ve = P.emit("vector", lambda E: E.tensor_tensor(
                    acc, acc, eq2, OP.add))
                P.wait("tensor", "sf_vector", ve)
                mm = P.emit("tensor", lambda E: E.transpose(
                    pb[1][0:NE, 0:128], acc, sb_cf[:, 0:128]), inc="mm")
                P.wait("vector", "mm", mm)
                ve = P.emit("vector", lambda E, t=t: E.tensor_copy(
                    sb_sc[0:NE, 2048 + t * 128:2048 + (t + 1) * 128],
                    pb[1][0:NE, 0:128]))
                P.wait("tensor", "sf_vector", ve)
            P.wait("sync", "sf_vector", P.val("sf_vector"))
            P.dma("sync", wt_d[:, :], sb_sc[0:NE, 2048:2048 + TOK], "st")

        def emit_wbc(tg):
            for e_ in range(NE):
                lwv = P.dma("sync", sb_rp[0:1, 0:512],
                            wt_d[e_:e_ + 1, tg * 512:(tg + 1) * 512], "ld")
                P.wait("tensor", "ld", lwv)
                mm = P.emit("tensor", lambda E: E.matmul(
                    pb[2][:, :], sb_or[0:1, 0:128], sb_rp[0:1, 0:512],
                    start=True, stop=True), inc="mm")
                P.wait("sync", "mm", mm)
                P.wait("vector", "mm", mm)
                ve = P.emit("vector", lambda E, e_=e_: E.tensor_copy(
                    sb_sc[:, e_ * 512:(e_ + 1) * 512], pb[2][:, :]))
                P.wait("tensor", "sf_vector", ve)
            return P.val("sf_vector")

        def p_ffn_experts():
            whr = Ring(P, 4)
            hxr = Ring(P, 2, dma_sems=["r22", "r23"])
            wldr = Ring(P, 2)
            lorar = Ring(P, 2)
            for tg in range(NTG):
                tloc = tg * 512
                P.wait("sync", "mm", P.val("mm"))
                lx = None
                for kc in range(16):
                    lx = P.dma("sync", sbf[:, kc * 512:(kc + 1) * 512],
                               xnfm_d[kc * 128:(kc + 1) * 128, tloc:tloc + 512],
                               "ld")
                    lx = P.dma("sync",
                               sbf[:, 8192 + kc * 512:8192 + (kc + 1) * 512],
                               xnfm_d[D + kc * 128:D + (kc + 1) * 128,
                                      tloc:tloc + 512], "ld")
                emit_wbc(tg)
                for gi, (ah, al, xaofs) in enumerate(
                        ((gA_h, gA_l, 0), (uA_h, uA_l, 4096))):
                    P.wait("sync", "mm", P.val("mm"))
                    la = None
                    for kc in range(16):
                        hb = sb_wst if kc < 8 else sb_ws2
                        lb = sb_lw if kc < 8 else sb_zb
                        la = P.dma("sync",
                                   hb[:, (kc % 8) * 512:(kc % 8 + 1) * 512],
                                   ah[kc * 128:(kc + 1) * 128, :], "ld")
                        la = P.dma("sync",
                                   lb[:, (kc % 8) * 512:(kc % 8 + 1) * 512],
                                   al[kc * 128:(kc + 1) * 128, :], "ld")
                    P.wait("tensor", "ld", max(lx, la))
                    for til in range(4):
                        mm = None
                        for kc in range(16):
                            hb = sb_wst if kc < 8 else sb_ws2
                            lb = sb_lw if kc < 8 else sb_zb
                            mm = mm3(pb[2][:, :],
                                     sbf[:, kc * 512 + til * 128:
                                         kc * 512 + (til + 1) * 128],
                                     sbf[:, 8192 + kc * 512 + til * 128:
                                         8192 + kc * 512 + (til + 1) * 128],
                                     hb[:, (kc % 8) * 512:(kc % 8 + 1) * 512],
                                     lb[:, (kc % 8) * 512:(kc % 8 + 1) * 512],
                                     first=(kc == 0), last=(kc == 15),
                                     inc="mm" if kc == 15 else None)
                        P.wait("scalar", "mm", mm)
                        se = P.emit("scalar", lambda E:
                                    E.mul(sb_sm[:, :], pb[2][:, :], SCALING))
                        P.wait("vector", "sf_scalar", se)
                        vh = P.emit("vector", lambda E: E.tensor_copy(
                            sb_fm[:, 0:512], sb_sm[:, :]))
                        vl = P.emit("vector", lambda E: E.scalar_tensor_tensor(
                            sb_fm[:, 512:1024], sb_fm[:, 0:512], -1.0,
                            sb_sm[:, :], OP.mult, OP.add))
                        P.wait("tensor", "sf_vector", vl)
                        P.wait("tensor", "sf_scalar", se)
                        P.wait("scalar", "sf_vector", vl)
                        for c in range(4):
                            t_evac(sb_fm[:, c * 128:(c + 1) * 128],
                                   sb_p[:, xaofs + c * 512 + til * 128:
                                        xaofs + c * 512 + (til + 1) * 128])
                            t_evac(sb_fm[:, 512 + c * 128:512 + (c + 1) * 128],
                                   sb_p[:, xaofs + 2048 + c * 512 + til * 128:
                                        xaofs + 2048 + c * 512 + (til + 1) * 128])
                t_guard()
                for pr in range(4):
                    P.emit("tensor", lambda E, pr=pr: E.matmul(
                        pb[2 + pr][:, :], sb_or[0:1, 0:128],
                        sb_zf[0:1, 0:512], start=True, stop=False))
                P.wait("sync", "mm", P.val("mm"))
                for ic in range(NIC):
                    half = wldr.acquire("sync")
                    wofs = half * 2048
                    l1 = P.dma("sync", sb_wst[:, wofs:wofs + 2048],
                               bgate_h[ic * 128:(ic + 1) * 128, :], "ld")
                    l1 = P.dma("sync", sb_ws2[:, wofs:wofs + 2048],
                               bgate_l[ic * 128:(ic + 1) * 128, :], "ld")
                    l1 = P.dma("sync", sb_lw[:, wofs:wofs + 2048],
                               bup_h[ic * 128:(ic + 1) * 128, :], "ld")
                    l1 = P.dma("sync", sb_zb[:, wofs:wofs + 2048],
                               bup_l[ic * 128:(ic + 1) * 128, :], "ld")
                    lh = lorar.acquire("sync")
                    lofs = lh * 3072
                    l1 = P.dma("sync", sb_q[:, lofs:lofs + 512],
                               gB_h[ic * 128:(ic + 1) * 128, :], "ld")
                    l1 = P.dma("sync", sb_q[:, lofs + 512:lofs + 1024],
                               gB_l[ic * 128:(ic + 1) * 128, :], "ld")
                    l1 = P.dma("sync", sb_q[:, lofs + 1024:lofs + 1536],
                               uB_h[ic * 128:(ic + 1) * 128, :], "ld")
                    l1 = P.dma("sync", sb_q[:, lofs + 1536:lofs + 2048],
                               uB_l[ic * 128:(ic + 1) * 128, :], "ld")
                    l1 = P.dma("sync",
                               sqf[:, (lofs + 2048) // 2:(lofs + 2048) // 2 + 512],
                               dA[ic * 128:(ic + 1) * 128, :], "ld")
                    P.wait("tensor", "ld", l1)
                    for bsel in range(2):
                        hb = sb_wst if bsel == 0 else sb_lw
                        lb = sb_ws2 if bsel == 0 else sb_zb
                        bank = pb[0] if bsel == 0 else pb[1]
                        mm = None
                        for kc in range(16):
                            mm = mm3(bank[0:128, 0:512],
                                     hb[:, wofs + kc * 128:wofs + (kc + 1) * 128],
                                     lb[:, wofs + kc * 128:wofs + (kc + 1) * 128],
                                     sbf[:, kc * 512:(kc + 1) * 512],
                                     sbf[:, 8192 + kc * 512:
                                         8192 + (kc + 1) * 512],
                                     first=(kc == 0), last=(kc == 15),
                                     inc="mm" if kc == 15 else None)
                        P.wait("scalar", "mm", mm)
                        dstf = sb_rp[:, 0:512] if bsel == 0 else sb_rp[:, 512:1024]
                        se = P.emit("scalar", lambda E, d=dstf, b=bank:
                                    E.copy(d, b[0:128, 0:512]))
                        if bsel == 1:
                            P.wait("vector", "sf_scalar", se)
                    wldr.consumed(half, "mm", P.val("mm"))
                    sebase = P.val("sf_scalar")
                    for e_ in range(NE):
                        eb = (e_ % 2) * 64
                        ech = (e_ // 2)
                        mm = mm3(pb6[0:128, 0:512],
                                 sb_q[eb:eb + 64, lofs + ech * 128:
                                      lofs + (ech + 1) * 128],
                                 sb_q[eb:eb + 64, lofs + 512 + ech * 128:
                                      lofs + 512 + (ech + 1) * 128],
                                 sb_p[eb:eb + 64, ech * 512:(ech + 1) * 512],
                                 sb_p[eb:eb + 64, 2048 + ech * 512:
                                      2048 + (ech + 1) * 512],
                                 first=True, last=True, inc="mm")
                        P.wait("vector", "mm", mm)
                        P.wait("vector", "sf_scalar", sebase)
                        ve = P.emit("vector", lambda E: E.tensor_tensor(
                            sb_sc2[:, 0:512], sb_rp[:, 0:512],
                            pb6[0:128, 0:512], OP.add))
                        P.wait("scalar", "sf_vector", ve)
                        se = P.emit("scalar", lambda E: E.activation(
                            sb_sc2[:, 512:1024], sb_sc2[:, 0:512], AF.Silu))
                        P.wait("tensor", "sf_vector", ve)
                        mm = mm3(pb6[0:128, 0:512],
                                 sb_q[eb:eb + 64, lofs + 1024 + ech * 128:
                                      lofs + 1024 + (ech + 1) * 128],
                                 sb_q[eb:eb + 64, lofs + 1536 + ech * 128:
                                      lofs + 1536 + (ech + 1) * 128],
                                 sb_p[eb:eb + 64, 4096 + ech * 512:
                                      4096 + (ech + 1) * 512],
                                 sb_p[eb:eb + 64, 6144 + ech * 512:
                                      6144 + (ech + 1) * 512],
                                 first=True, last=True, inc="mm")
                        P.wait("vector", "mm", mm)
                        ve = P.emit("vector", lambda E: E.tensor_tensor(
                            sb_sc2[:, 1024:1536], sb_rp[:, 512:1024],
                            pb6[0:128, 0:512], OP.add))
                        P.wait("tensor", "sf_vector", ve)
                        P.wait("vector", "sf_scalar", se)
                        P.emit("vector", lambda E: E.tensor_tensor(
                            sb_sc2[:, 1536:2048], sb_sc2[:, 512:1024],
                            sb_sc2[:, 1024:1536], OP.mult))
                        so = whr.acquire("vector")
                        whs = svf[:, so * 512:(so + 1) * 512]
                        ve = P.emit("vector", lambda E, w=whs, e_=e_:
                                    E.tensor_tensor(
                                        w, sb_sc2[:, 1536:2048],
                                        sb_sc[:, e_ * 512:(e_ + 1) * 512],
                                        OP.mult))
                        P.wait("gpsimd", "sf_vector", ve)
                        if e_ == 0:
                            ge = P.emit("gpsimd", lambda E, w=whs:
                                        E.tensor_copy(sb_sm[:, :], w))
                        else:
                            ge = P.emit("gpsimd", lambda E, w=whs:
                                        E.tensor_tensor(sb_sm[:, :],
                                                        sb_sm[:, :], w, OP.add))
                        P.wait("tensor", "sf_gpsimd", ge)
                        mm = P.emit("tensor", lambda E, e_=e_, eb=eb, w=whs,
                                    lofs=lofs:
                                    E.matmul(pb[2 + e_ // 2][eb:eb + 64, :],
                                             sqf[:, (lofs + 2048) // 2 + e_ * 64:
                                                 (lofs + 2048) // 2 + (e_ + 1) * 64],
                                             w, start=False, stop=False),
                                    inc="mm")
                        whr.consumed(so, "mm", mm)
                        whr.consumed(so, "sf_gpsimd", ge)
                    lorar.consumed(lh, "mm", P.val("mm"))
                    P.wait("scalar", "sf_gpsimd", P.val("sf_gpsimd"))
                    so = hxr.acquire("scalar")
                    hof = 1024 + so * 1024
                    se = P.emit("scalar", lambda E, hof=hof:
                                E.copy(sb_fm[:, hof:hof + 512], sb_sm[:, :]))
                    P.wait("vector", "sf_scalar", se)
                    ve = P.emit("vector", lambda E, hof=hof:
                                E.scalar_tensor_tensor(
                                    sb_fm[:, hof + 512:hof + 1024],
                                    sb_fm[:, hof:hof + 512], -1.0,
                                    sb_sm[:, :], OP.mult, OP.add))
                        
                    P.wait("sync", "sf_vector", ve)
                    hxr.dma("sync", so, hmix_d[tg, ic, :, :],
                            sb_fm[:, hof:hof + 1024])
                    P.wait("gpsimd", "sf_vector", ve)
                t_release()
                mmz = None
                for pr in range(4):
                    mmz = P.emit("tensor", lambda E, pr=pr: E.matmul(
                        pb[2 + pr][:, :], sb_or[0:1, 0:128],
                        sb_zf[0:1, 0:512], start=False, stop=True), inc="mm")
                P.wait("vector", "mm", mmz)
                P.wait("vector", "sf_gpsimd", P.val("sf_gpsimd"))
                for pr in range(4):
                    ve = P.emit("vector", lambda E, pr=pr:
                                E.tensor_scalar_mul(
                                    svf[:, pr * 512:(pr + 1) * 512],
                                    pb[2 + pr][:, :], SCALING))
                    P.wait("sync", "sf_vector", ve)
                    st = P.dma("sync", z_d[tg, :, pr * 512:(pr + 1) * 512],
                               svf[:, pr * 512:(pr + 1) * 512], "st")
                    P.wait("vector", "st", st)
                    P.wait("tensor", "sf_vector", ve)

        def p_ffn_down(src_ap, dst_ap):
            banks = list(pb) + [pb6, pb7]
            hr = Ring(P, 2)
            br = Ring(P, 2)
            for tg in range(NTG):
                P.wait("sync", "mm", P.val("mm"))
                lz = P.dma("sync", slwf[:, 0:2048], z_d[tg, :, :], "ld")
                for dh in range(2):
                    P.wait("sync", "mm", P.val("mm"))
                    ldb = None
                    for c in range(4):
                        ldb = P.dma("sync", sb_s[:, c * 1024:(c + 1) * 1024],
                                    dB[c * 128:(c + 1) * 128,
                                       dh * 1024:dh * 1024 + 1024], "ld")
                    lastmm = None
                    for ic in range(NIC):
                        hs = hr.acquire("sync")
                        l1 = P.dma("sync", sb_wst[:, hs * 2048:hs * 2048 + 1024],
                                   hmix_d[tg, ic, :, :], "ld")
                        bs = br.acquire("sync")
                        l2 = P.dma("sync", sb_ws2[:, bs * 2048:bs * 2048 + 1024],
                                   bdown_h[ic * 128:(ic + 1) * 128,
                                           dh * 1024:dh * 1024 + 1024], "ld")
                        l2 = P.dma("sync",
                                   sb_ws2[:, bs * 2048 + 1024:bs * 2048 + 2048],
                                   bdown_l[ic * 128:(ic + 1) * 128,
                                           dh * 1024:dh * 1024 + 1024], "ld")
                        P.wait("tensor", "ld", max(l1, l2))
                        if ic == 0:
                            P.wait("tensor", "ld", max(lz, ldb))
                        mm = None
                        for til in range(4):
                            for dc in range(2):
                                mm = mm3(banks[til * 2 + dc][:, :],
                                         sb_wst[:, hs * 2048 + til * 128:
                                                hs * 2048 + (til + 1) * 128],
                                         sb_wst[:, hs * 2048 + 512 + til * 128:
                                                hs * 2048 + 512 + (til + 1) * 128],
                                         sb_ws2[:, bs * 2048 + dc * 512:
                                                bs * 2048 + (dc + 1) * 512],
                                         sb_ws2[:, bs * 2048 + 1024 + dc * 512:
                                                bs * 2048 + 1024 + (dc + 1) * 512],
                                         first=(ic == 0), last=False,
                                         inc="mm" if (til == 3 and dc == 1)
                                         else None)
                        hr.consumed(hs, "mm", mm)
                        br.consumed(bs, "mm", mm)
                        lastmm = mm
                    for e_ in range(NE):
                        eb = (e_ % 2) * 64
                        ech = e_ // 2
                        for til in range(4):
                            for dc in range(2):
                                lastf = (e_ == NE - 1)
                                lastmm = P.emit(
                                    "tensor", lambda E, eb=eb, ech=ech,
                                    til=til, dc=dc, last=lastf:
                                    E.matmul(
                                        banks[til * 2 + dc][:, :],
                                        slwf[eb:eb + 64,
                                             ech * 512 + til * 128:
                                             ech * 512 + (til + 1) * 128],
                                        sb_s[eb:eb + 64,
                                             ech * 1024 + dc * 512:
                                             ech * 1024 + dc * 512 + 512],
                                        start=False, stop=last),
                                    inc="mm" if (e_ == NE - 1 and til == 3 and
                                                 dc == 1) else None)
                    P.wait("vector", "mm", P.val("mm"))
                    for til in range(4):
                        t0 = tg * 512 + til * 128
                        lo = P.dma("sync", sb_xt[:, 0:1024],
                                   src_ap[t0:t0 + 128,
                                          dh * 1024:dh * 1024 + 1024], "ld")
                        P.wait("vector", "ld", lo)
                        ve = None
                        for dc in range(2):
                            ve = P.emit("vector", lambda E, til=til, dc=dc:
                                        E.tensor_tensor(
                                            sb_sc[:, til * 1024 + dc * 512:
                                                  til * 1024 + (dc + 1) * 512],
                                            banks[til * 2 + dc][:, :],
                                            sb_xt[:, dc * 512:(dc + 1) * 512],
                                            OP.add))
                        P.wait("sync", "sf_vector", ve)
                        st = P.dma("sync",
                                   dst_ap[t0:t0 + 128,
                                          dh * 1024:dh * 1024 + 1024],
                                   sb_sc[:, til * 1024:til * 1024 + 1024], "st")
                        P.wait("vector", "st", st)
                    P.wait("tensor", "sf_vector", P.val("sf_vector"))

        pcount = [0]

        def phase(fn, *a, **kw):
            pcount[0] += 1
            if pcount[0] > PHASE_LIMIT:
                return
            fn(*a, **kw)
            P.barrier()

        phase(p_consts)
        phase(p_cast_weights)

        def sub_attn_a(src, dst):
            phase(p_ln_fm, src, xn_ci)
            phase(p_coll, "AllGather", xn_ci, xn_co)
            phase(p_attn_a_qkv)
            phase(p_attn_heads, "a")
            phase(p_partial_proj, wo)
            phase(p_coll, "ReduceScatter", pp_ci, pp_co)
            phase(p_residual, src, dst, False)

        def sub_ffn(src, dst):
            phase(p_ln_fm, src, xnfm_d)
            phase(p_routing)
            phase(p_ffn_experts)
            phase(p_ffn_down, src, dst)

        def sub_attn_b(src, dst):
            phase(p_ln_fm, src, xn_ci)
            phase(p_attn_b_h)
            phase(p_coll, "AllGather", h_ci, h_co)
            phase(p_attn_b_proj)
            phase(p_attn_heads, "b")
            phase(p_partial_proj, wz)
            phase(p_coll, "ReduceScatter", pp_ci, pp_co)
            phase(p_residual, src, dst, True)

        subs = [
            lambda d: sub_attn_a(x0, d),
            lambda d: sub_ffn(xres, d),
            lambda d: sub_attn_b(xres, d),
            lambda d: sub_ffn(xres, d),
        ]
        for i in range(N_SUB):
            dst = yout if i == N_SUB - 1 else xres
            subs[i](dst)

        P.wait("gpsimd", "st", P.val("st"))
        P.wait("gpsimd", "ld", P.val("ld"))
        for i in range(24):
            P.wait("gpsimd", f"r{i}", P.val(f"r{i}"))

        with nc.Block() as block:
            for ename in ENGINES:
                def make(ename):
                    def body(E):
                        for fn in P.ops[ename]:
                            fn(E)
                    return body
                getattr(block, ename)(make(ename))

    return nc



def _rope_tables(t, d):
    freq = np.arange(1, t + 1, dtype=np.float64)[:, None]
    pos = np.repeat(np.arange(d // 2, dtype=np.float64), 2)[None]
    theta = np.exp(-pos / d * math.log(10000.0))
    ang = freq * theta
    return np.cos(ang).astype(np.float32), np.sin(ang).astype(np.float32)


def _swap_pairs(a, axis):
    idx = np.arange(a.shape[axis])
    idx = idx.reshape(-1, 2)[:, ::-1].reshape(-1)
    return np.take(a, idx, axis=axis)


def kernel(**inputs):
    dm = _dims()
    HPC, TOK = dm["HPC"], dm["TOK"]
    f32 = np.float32

    Wqkv = np.asarray(inputs["L0_attn_a_W_qkv"], f32)
    Wo = np.asarray(inputs["L0_attn_a_W_o"], f32)
    x_in = np.asarray(inputs["L0_attn_a_x"], f32)
    WA = np.asarray(inputs["L0_attn_b_WA"], f32)
    bA = np.asarray(inputs["L0_attn_b_bA"], f32)
    WB = np.asarray(inputs["L0_attn_b_WB"], f32)
    bB = np.asarray(inputs["L0_attn_b_bB"], f32)
    WC = np.asarray(inputs["L0_attn_b_WC"], f32)
    bC = np.asarray(inputs["L0_attn_b_bC"], f32)
    WUQ = np.asarray(inputs["L0_attn_b_WUQ"], f32)
    WUK = np.asarray(inputs["L0_attn_b_WUK"], f32)
    WUV = np.asarray(inputs["L0_attn_b_WUV"], f32)
    WZ = np.asarray(inputs["L0_attn_b_WZ"], f32)
    bZ = np.asarray(inputs["L0_attn_b_bZ"], f32)
    wealth = np.asarray(inputs["L0_ffn_expert_wealth"], f32)
    confW = np.asarray(inputs["L0_ffn_conf_W"], f32)
    confb = np.asarray(inputs["L0_ffn_conf_b"], f32)
    bgate = np.asarray(inputs["L0_ffn_base_gate"], f32)
    bup = np.asarray(inputs["L0_ffn_base_up"], f32)
    bdown = np.asarray(inputs["L0_ffn_base_down"], f32)
    gateA = np.asarray(inputs["L0_ffn_gate_A"], f32)
    gateB = np.asarray(inputs["L0_ffn_gate_B"], f32)
    upA = np.asarray(inputs["L0_ffn_up_A"], f32)
    upB = np.asarray(inputs["L0_ffn_up_B"], f32)
    downA = np.asarray(inputs["L0_ffn_down_A"], f32)
    downB = np.asarray(inputs["L0_ffn_down_B"], f32)

    C = lambda a: np.ascontiguousarray(a, dtype=f32)
    identb = np.eye(128, dtype=BFD)
    identf = np.eye(128, dtype=f32)
    onesf = np.ones((1, 128), f32)
    diagm = np.where(np.arange(128)[None, :] > np.arange(128)[:, None],
                     NEG, 0.0).astype(f32)
    cos_k, sin_k = _rope_tables(T, DHR)
    cos_q, sin_q = _rope_tables(T, 2 * DL)
    sgn_k = np.where(np.arange(DHR) % 2 == 0, -1.0, 1.0).astype(f32)
    sgn_q = np.where(np.arange(2 * DL) % 2 == 0, -1.0, 1.0).astype(f32)
    sin_k = sin_k * sgn_k[None, :]
    sin_q = sin_q * sgn_q[None, :]

    slopes = 2.0 ** (-8.0 / np.arange(1, NH + 1))
    ii = np.arange(128)[:, None]
    jj = np.arange(512)[None, :]

    in_maps = []
    for c in range(8):
        b, g = c // 4, c % 4
        heads = slopes[g * HPC:(g + 1) * HPC]
        al5 = (heads[:, None, None] * (jj - ii)[None]).astype(f32)
        jd = jj[:, :128]
        ald = (heads[:, None, None] * (jd - ii)[None]).astype(f32)
        ald = ald + np.where(jd > ii, NEG, 0.0)[None]
        ald = ald.astype(f32)
        alc = np.zeros((128, HPC * 64), f32)
        for hh in range(HPC):
            alc[:, hh * 64:(hh + 1) * 64] = (
                -heads[hh] * 128.0 * np.arange(64, dtype=f32))[None, :]
        wcs_full = _swap_pairs(WC, axis=1)
        m = {
            "x0": C(x_in[b, g * TOK:(g + 1) * TOK, :]),
            "wq": C(Wqkv[:, g * 512:(g + 1) * 512]),
            "wk": C(Wqkv[:, D + g * 512:D + (g + 1) * 512]),
            "wv": C(Wqkv[:, 2 * D + g * 512:2 * D + (g + 1) * 512]),
            "wo": C(Wo[g * 512:(g + 1) * 512, :]),
            "wa": C(WA), "ba": C(bA.reshape(8, 128).T),
            "wb": C(WB), "wbs": C(_swap_pairs(WB, axis=1)),
            "bb": C(bB.reshape(DHR, 1)),
            "bbs": C(_swap_pairs(bB, axis=0).reshape(DHR, 1)),
            "wc": C(WC[:, g * 256:(g + 1) * 256]),
            "wcs": C(wcs_full[:, g * 256:(g + 1) * 256]),
            "bc": C(bC[g * 256:(g + 1) * 256].reshape(2, 128).T),
            "bcs": C(_swap_pairs(bC, axis=0)[g * 256:(g + 1) * 256].reshape(2, 128).T),
            "wuq": C(WUQ[:, g * 512:(g + 1) * 512]),
            "wuk": C(WUK[:, g * 512:(g + 1) * 512]),
            "wuv": C(WUV[:, g * 512:(g + 1) * 512]),
            "wz": C(WZ[g * 512:(g + 1) * 512, :]),
            "bzrep": C(np.broadcast_to(bZ[None, :], (128, D))),
            "wealth": C(wealth.reshape(NE, 1)),
            "confw": C(confW.T), "confb": C(confb.reshape(NE, 1)),
            "bgate": C(bgate.reshape(D // 128, 128, DI // 128, 128)
                      .transpose(2, 1, 0, 3).reshape(DI // 128 * 128, D)),
            "bup": C(bup.reshape(D // 128, 128, DI // 128, 128)
                     .transpose(2, 1, 0, 3).reshape(DI // 128 * 128, D)),
            "bdown": C(bdown),
            "gA": C(gateA.transpose(1, 0, 2).reshape(D, NE * RK)),
            "uA": C(upA.transpose(1, 0, 2).reshape(D, NE * RK)),
            "gB": C(gateB.reshape(NE // 2, 2, RK, DI // 128, 128)
                    .transpose(3, 1, 2, 0, 4).reshape(DI // 128 * 128,
                                                      NE // 2 * 128)),
            "uB": C(upB.reshape(NE // 2, 2, RK, DI // 128, 128)
                    .transpose(3, 1, 2, 0, 4).reshape(DI // 128 * 128,
                                                      NE // 2 * 128)),
            "dA": C(downA.transpose(1, 0, 2).reshape(DI, NE * RK)),
            "dB": C(downB.reshape(NE * RK, D)),
            "identb": identb, "identf": identf, "onesf": onesf,
            "alibi5": al5, "alibid": ald, "alibic": alc, "diagm": diagm,
            "coskr": C(cos_k.T), "sinkr": C(sin_k.T),
            "cosqr": C(cos_q.T[g * 256:(g + 1) * 256, :]),
            "sinqr": C(sin_q.T[g * 256:(g + 1) * 256, :]),
        }
        in_maps.append(m)

    nc = build_nc()
    res = run_bass_kernel_spmd(nc, in_maps, core_ids=list(range(8)))
    out = np.empty((B, T, D), dtype=f32)
    for c in range(8):
        b, g = c // 4, c % 4
        out[b, g * TOK:(g + 1) * TOK, :] = res.results[c]["y"]
    return out


# version 06
# speedup vs baseline: 1.0112x; 1.0112x over previous
import math
from contextlib import ExitStack

import numpy as np
import ml_dtypes
import concourse.bass as bass
import concourse.mybir as mybir
from concourse.bass_utils import run_bass_kernel_spmd

F32 = mybir.dt.float32
BF16 = mybir.dt.bfloat16
AF = mybir.ActivationFunctionType
OP = mybir.AluOpType
AXX = mybir.AxisListType.X

B, T, D = 2, 8192, 2048
NH, DH = 16, 128
GRP = 4
DI = 7168
NE, RK = 8, 64
DL = 512
DHR = 64
SCALING = 16.0 / 64.0
NEG = -30000.0
N_SUB = 4
PHASE_LIMIT = 999

BFD = np.dtype(ml_dtypes.bfloat16)
ENGINES = ("sync", "vector", "scalar", "tensor", "gpsimd")
SELF_SYNC = ("vector", "scalar", "gpsimd")


def _dims():
    HPC = NH // GRP
    TOK = T // GRP
    return dict(HPC=HPC, TOK=TOK, NTIL=TOK // 128, NTG=TOK // 512,
                FT=T // 512, NIC=DI // 128, NQB=T // 128)


class Prog:
    def __init__(self):
        self.ops = {e: [] for e in ENGINES}
        self.sem = {}
        self.handles = None
        self.last = {e: None for e in ENGINES}

    def new_sem(self, name):
        self.sem[name] = 0

    def emit(self, eng, fn, inc=None, by=1):
        if eng in SELF_SYNC:
            if self.last[eng] is not None:
                self.wait(eng, *self.last[eng])
            if inc is None:
                inc, by = "sf_" + eng, 1
        if inc is not None:
            self.sem[inc] += by

            def wrapped(E, fn=fn, s=inc, b=by):
                return fn(E).then_inc(self.handles[s], b)
            self.ops[eng].append(wrapped)
            if eng in SELF_SYNC:
                self.last[eng] = (inc, self.sem[inc])
            return self.sem[inc]
        self.ops[eng].append(fn)
        return None

    def wait(self, eng, sem, val):
        if val is None:
            return
        self.ops[eng].append(
            lambda E, s=sem, v=val: E.wait_ge(self.handles[s], v))

    def val(self, sem):
        return self.sem[sem]

    def dma(self, eng, out, in_, inc="st"):
        self.sem[inc] += 16

        def fn(E, o=out, i=in_, s=inc):
            return E.dma_start(out=o, in_=i).then_inc(self.handles[s], 16)
        self.ops[eng].append(fn)
        return self.sem[inc]

    def barrier(self):
        for s, v in self.sem.items():
            if s != "bar" and v > 0:
                self.wait("sync", s, v)
        self.sem["bar"] += 5
        tgt = self.sem["bar"]
        for e in ENGINES:
            def fin(E):
                n = E.engine_nop() if hasattr(E, "engine_nop") else E.nop()
                return n.then_inc(self.handles["bar"], 1)
            self.ops[e].append(fin)
        for e in ENGINES:
            self.wait(e, "bar", tgt)


class Ring:

    def __init__(self, P, nslots, dma_sems=None):
        self.P = P
        self.n = nslots
        self.tok = [[] for _ in range(nslots)]
        self.i = 0
        self.dma_sems = dma_sems
        if dma_sems is not None:
            assert len(dma_sems) == nslots

    def acquire(self, eng):
        s = self.i % self.n
        self.i += 1
        for sem, val in self.tok[s]:
            self.P.wait(eng, sem, val)
        self.tok[s] = []
        if self.dma_sems is not None:
            self.P.wait(eng, self.dma_sems[s], self.P.val(self.dma_sems[s]))
        return s

    def dma(self, eng, s, out, in_):
        return self.P.dma(eng, out, in_, self.dma_sems[s])

    def consumed(self, s, sem, val):
        self.tok[s].append((sem, val))


def hop(P, src, dst, vals):
    for sem, val in vals:
        P.wait(src, sem, val)
    P.sem["hs"] += 1
    tgt = P.sem["hs"]

    def fin(E):
        n = E.engine_nop() if hasattr(E, "engine_nop") else E.nop()
        return n.then_inc(P.handles["hs"], 1)
    P.ops[src].append(fin)
    P.wait(dst, "hs", tgt)


def k_tiles_for_qblock(qb):
    q0 = qb * 128
    tiles = [(i * 512, 512, False) for i in range(q0 // 512)]
    base = (q0 // 512) * 512
    for j in range((q0 - base) // 128):
        tiles.append((base + j * 128, 128, False))
    tiles.append((q0, 128, True))
    return tiles


def build_nc():
    dm = _dims()
    HPC, TOK, NTIL, NTG, FT = dm["HPC"], dm["TOK"], dm["NTIL"], dm["NTG"], dm["FT"]
    NIC, NQB = dm["NIC"], dm["NQB"]
    nc = bass.Bass("TRN2", target_bir_lowering=False)

    def inp(name, shape, dtype=F32):
        return nc.dram_tensor(name, shape, dtype, kind="ExternalInput").ap()

    x0 = inp("x0", [TOK, D])
    wq = inp("wq", [D, HPC * DH]); wk = inp("wk", [D, HPC * DH])
    wv = inp("wv", [D, HPC * DH])
    wo = inp("wo", [HPC * DH, D])
    wa = inp("wa", [D, 2 * DL]); ba = inp("ba", [128, 8])
    wb = inp("wb", [2 * DL, DHR]); wbs = inp("wbs", [2 * DL, DHR])
    bb = inp("bb", [DHR, 1]); bbs = inp("bbs", [DHR, 1])
    wc = inp("wc", [2 * DL, HPC * DHR]); wcs = inp("wcs", [2 * DL, HPC * DHR])
    bc = inp("bc", [128, 2]); bcs = inp("bcs", [128, 2])
    wuq = inp("wuq", [DL, HPC * DH]); wuk = inp("wuk", [DL, HPC * DH])
    wuv = inp("wuv", [DL, HPC * DH])
    wz = inp("wz", [HPC * DH, D])
    bzrep = inp("bzrep", [128, D])
    wealth = inp("wealth", [NE, 1]); confw = inp("confw", [D, NE])
    confb = inp("confb", [NE, 1])
    bgate = inp("bgate", [NIC * 128, D]); bup = inp("bup", [NIC * 128, D])
    bdown = inp("bdown", [DI, D])
    gA = inp("gA", [D, NE * RK]); uA = inp("uA", [D, NE * RK])
    gB = inp("gB", [NIC * 128, (NE // 2) * 128])
    uB = inp("uB", [NIC * 128, (NE // 2) * 128])
    dA = inp("dA", [DI, NE * RK]); dB = inp("dB", [NE * RK, D])
    identb = inp("identb", [128, 128], BF16)
    identf = inp("identf", [128, 128])
    onesf = inp("onesf", [1, 128])
    alibi5 = inp("alibi5", [HPC, 128, 512])
    alibid = inp("alibid", [HPC, 128, 128])
    alibic = inp("alibic", [128, HPC * 64])
    diagm = inp("diagm", [128, 128])
    coskr = inp("coskr", [DHR, T]); sinkr = inp("sinkr", [DHR, T])
    cosqr = inp("cosqr", [HPC * DHR, T]); sinqr = inp("sinqr", [HPC * DHR, T])

    yout = nc.dram_tensor("y", [TOK, D], F32, kind="ExternalOutput").ap()

    def scrt(name, shape, dtype=BF16):
        return nc.dram_tensor(name, shape, dtype).ap()

    xres = scrt("xres", [TOK, D], F32)
    wpairs = {}
    for nm, w in [("wq", wq), ("wk", wk), ("wv", wv), ("wa", wa),
                  ("bgate", bgate), ("bup", bup), ("bdown", bdown),
                  ("gA", gA), ("uA", uA), ("gB", gB), ("uB", uB),
                  ("confw", confw)]:
        wpairs[nm] = (w, scrt(nm + "_h", list(w.shape)),
                      scrt(nm + "_l", list(w.shape)))
    wq_h, wq_l = wpairs["wq"][1:]
    wk_h, wk_l = wpairs["wk"][1:]
    wv_h, wv_l = wpairs["wv"][1:]
    wa_h, wa_l = wpairs["wa"][1:]
    bgate_h, bgate_l = wpairs["bgate"][1:]
    bup_h, bup_l = wpairs["bup"][1:]
    bdown_h, bdown_l = wpairs["bdown"][1:]
    gA_h, gA_l = wpairs["gA"][1:]
    uA_h, uA_l = wpairs["uA"][1:]
    gB_h, gB_l = wpairs["gB"][1:]
    uB_h, uB_l = wpairs["uB"][1:]
    confw_h, confw_l = wpairs["confw"][1:]
    xn_ci = scrt("xn_ci", [2 * D, TOK]); xn_co = scrt("xn_co", [GRP * 2 * D, TOK])
    h_ci = scrt("h_ci", [2 * DL, TOK], F32)
    h_co = scrt("h_co", [GRP * 2 * DL, TOK], F32)
    pp_ci = scrt("pp_ci", [T, D], F32); pp_co = scrt("pp_co", [TOK, D], F32)
    qh_d = scrt("qh_d", [HPC, DH, T], F32); kh_d = scrt("kh_d", [HPC, DH, T], F32)
    vt_d = scrt("vt_d", [HPC, T, DH], F32)
    qr_d = scrt("qr_d", [HPC * DHR, T], F32); kr_d = scrt("kr_d", [DHR, T], F32)
    yfm_d = scrt("yfm_d", [HPC * DH, T], F32)
    xnfm_d = scrt("xnfm_d", [2 * D, TOK])
    wt_d = scrt("wt_d", [NE, TOK], F32)
    hmix_d = scrt("hmix_d", [NTG, NIC, 128, 1024])
    z_d = scrt("z_d", [NTG, 128, (NE // 2) * 512], F32)

    P = Prog()
    for s in ["ld", "lds", "st", "bar", "hs", "mm", "ve", "se", "ge", "cc",
              "sf_vector", "sf_scalar", "sf_gpsimd"]:
        P.new_sem(s)
    for i in range(24):
        P.new_sem(f"r{i}")

    with ExitStack() as stack:
        en = stack.enter_context
        sb_s = en(nc.sbuf_tensor("sb_s", [128, 8192], F32))
        sb_q = en(nc.sbuf_tensor("sb_q", [128, 8192], BF16))
        sb_k = en(nc.sbuf_tensor("sb_k", [128, 8192], BF16))
        sb_v = en(nc.sbuf_tensor("sb_v", [128, 8192], BF16))
        sb_p = en(nc.sbuf_tensor("sb_p", [128, 8192], BF16))
        sb_pt = en(nc.sbuf_tensor("sb_pt", [128, 2048], BF16))
        sb_zb = en(nc.sbuf_tensor("sb_zb", [128, 4096], BF16))
        sb_xt = en(nc.sbuf_tensor("sb_xt", [128, 2048], F32))
        sb_xn = en(nc.sbuf_tensor("sb_xn", [128, 2048], F32))
        sb_fm = en(nc.sbuf_tensor("sb_fm", [128, 4096], BF16))
        sb_wst = en(nc.sbuf_tensor("sb_wst", [128, 4096], BF16))
        sb_ws2 = en(nc.sbuf_tensor("sb_ws2", [128, 4096], BF16))
        sb_lw = en(nc.sbuf_tensor("sb_lw", [128, 4096], BF16))
        sb_sc = en(nc.sbuf_tensor("sb_sc", [128, 4096], F32))
        sb_sc2 = en(nc.sbuf_tensor("sb_sc2", [128, 2048], F32))
        sb_sm = en(nc.sbuf_tensor("sb_sm", [128, 512], F32))
        sb_st = en(nc.sbuf_tensor("sb_st", [128, 160], F32))
        sb_cb = en(nc.sbuf_tensor("sb_cb", [128, 128], BF16))
        sb_zr = en(nc.sbuf_tensor("sb_zr", [1, 640], BF16))
        sb_or = en(nc.sbuf_tensor("sb_or", [1, 128], F32))
        sb_zf = en(nc.sbuf_tensor("sb_zf", [1, 512], F32))
        sb_cf = en(nc.sbuf_tensor("sb_cf", [128, 128], F32))
        sb_rp = en(nc.sbuf_tensor("sb_rp", [128, 1024], F32))
        sb_bi = en(nc.sbuf_tensor("sb_bi", [128, 20], F32))
        for cval, ccol in ((1e-5, 16), (-0.5, 17), (SCALING, 18)):
            nc.const_aps.register(F32, cval, sb_bi[:, ccol:ccol + 1])
        pb = [en(nc.psum_tensor(f"pb{i}", [128, 512], F32)) for i in range(6)]
        ptbs = [en(nc.psum_tensor(f"ptb{i}", [128, 1024], BF16))
                for i in range(2)]
        pb6 = ptbs[0][:, :].bitcast(F32)
        pb7 = ptbs[1][:, :].bitcast(F32)
        ptf = [ptbs[0][:, :].bitcast(F32), ptbs[1][:, :].bitcast(F32)]

        sems = {s: en(nc.semaphore(s)) for s in P.sem}
        P.handles = sems

        sbf = sb_s[:, :].bitcast(BF16)
        sqf = sb_q[:, :].bitcast(F32)
        skf = sb_k[:, :].bitcast(F32)
        svf = sb_v[:, :].bitcast(F32)
        spf = sb_p[:, :].bitcast(F32)
        sptf = sb_pt[:, :].bitcast(F32)
        swstf = sb_wst[:, :].bitcast(F32)
        sws2f = sb_ws2[:, :].bitcast(F32)
        slwf = sb_lw[:, :].bitcast(F32)
        szbf = sb_zb[:, :].bitcast(F32)
        sfmf = sb_fm[:, :].bitcast(F32)
        svbf = sb_v[:, :]

        sb_co = sb_sc
        AL5, ALD = 0, HPC * 512
        DGM = ALD + HPC * 128
        ALC = DGM + 128
        QRING = 2944
        RRING = 3200
        assert RRING + 512 <= 4096
        BA, BB_, BBS, BC_, BCS, CB_, WE_ = 0, 8, 9, 10, 12, 14, 15

        fm_ring = Ring(P, 8, dma_sems=[f"r{i}" for i in range(8)])
        f32_ring = Ring(P, 4, dma_sems=[f"r{i}" for i in range(8, 12)])
        tr_state = {"i": 0, "tok": [None, None]}

        def t_evac(src, dst, n=128, f32=False):
            bk = tr_state["i"] % 2
            off = ((tr_state["i"] // 2) % (4 if f32 else 4)) * 128
            tr_state["i"] += 1
            if tr_state["tok"][bk] is not None:
                P.wait("tensor", "ve", tr_state["tok"][bk])
            if f32:
                dst_ps = ptf[bk][:, off:off + n]
                ident = sb_cf[0:src.shape[0], 0:src.shape[0]]
            else:
                dst_ps = ptbs[bk][:, off:off + n]
                ident = sb_cb[0:src.shape[0], 0:src.shape[0]]
            mm = P.emit("tensor", lambda E, d=dst_ps, a=src, i=ident:
                        E.transpose(d, a, i), inc="mm")
            P.wait("vector", "mm", mm)
            if f32:
                src_ps = ptf[bk][0:dst.shape[0], off:off + n]
            else:
                src_ps = ptbs[bk][0:dst.shape[0], off:off + n]
            ve = P.emit("vector", lambda E, d=dst, s=src_ps:
                        E.tensor_copy(d, s), inc="ve")
            tr_state["tok"][bk] = ve
            return ve

        def t_guard():
            P.wait("tensor", "ve", P.val("ve"))
            tr_state["tok"][0] = None
            tr_state["tok"][1] = None

        def t_release():
            tr_state["tok"][0] = P.val("ve")
            tr_state["tok"][1] = P.val("ve")

        def mm3(bank_ap, lh, ll, rh, rl, first, last, inc=None):
            P.emit("tensor", lambda E: E.matmul(bank_ap, lh, rh,
                                                start=first, stop=False))
            P.emit("tensor", lambda E: E.matmul(bank_ap, lh, rl,
                                                start=False, stop=False))
            return P.emit("tensor", lambda E: E.matmul(
                bank_ap, ll, rh, start=False, stop=last), inc=inc)

        def p_consts():
            P.emit("vector", lambda E: E.memset(sb_zr[:, :], 0.0))
            P.emit("vector", lambda E: E.memset(sb_zf[:, :], 0.0))
            for cval, ccol in ((1e-5, 16), (-0.5, 17), (SCALING, 18)):
                P.emit("vector", lambda E, v=cval, c=ccol:
                       E.memset(sb_bi[:, c:c + 1], v))
            P.dma("sync", sb_cb[:, 0:128], identb, "ld")
            P.dma("sync", sb_cf[:, 0:128], identf, "ld")
            P.dma("sync", sb_bi[:, BA:BA + 8], ba, "ld")
            P.dma("sync", sb_bi[0:DHR, BB_:BB_ + 1], bb, "ld")
            P.dma("sync", sb_bi[0:DHR, BBS:BBS + 1], bbs, "ld")
            P.dma("sync", sb_bi[:, BC_:BC_ + 2], bc, "ld")
            P.dma("sync", sb_bi[:, BCS:BCS + 2], bcs, "ld")
            P.dma("sync", sb_bi[0:NE, CB_:CB_ + 1], confb, "ld")
            P.dma("sync", sb_bi[0:NE, WE_:WE_ + 1], wealth, "ld")
            P.dma("sync", sb_or[0:1, 0:128], onesf, "ld")

        def p_cast_weights():
            ldr = Ring(P, 2)
            stw = Ring(P, 2, dma_sems=["r12", "r13"])
            low = Ring(P, 2, dma_sems=["r14", "r15"])
            for nm, (w32, whi, wlo) in wpairs.items():
                rows, cols = w32.shape
                assert cols <= 4096, (nm, cols)
                for r0 in range(0, rows, 128):
                    rr = min(128, rows - r0)
                    sl = ldr.acquire("sync")
                    src = sb_s[0:rr, sl * 4096:sl * 4096 + cols]
                    ld = P.dma("sync", src, w32[r0:r0 + rr, :], "ld")
                    P.wait("scalar", "ld", ld)
                    so = stw.acquire("scalar")
                    hdst = sb_zb[0:rr, so * 2048:so * 2048 + cols]
                    se = P.emit("scalar", lambda E, d=hdst, s=src: E.copy(d, s))
                    P.wait("sync", "se" if False else "sf_scalar", se)
                    stw.dma("sync", so, whi[r0:r0 + rr, :], hdst)
                    lso = low.acquire("vector")
                    ldst = sb_p[0:rr, lso * 2048:lso * 2048 + cols]
                    P.wait("vector", "sf_scalar", se)
                    ve = P.emit("vector", lambda E, d=ldst, h=hdst, s=src:
                                E.scalar_tensor_tensor(
                                    d, h, -1.0, s, OP.mult, OP.add))
                    ldr.consumed(sl, "sf_vector", ve)
                    stw.consumed(so, "sf_vector", ve)
                    P.wait("sync", "sf_vector", ve)
                    low.dma("sync", lso, wlo[r0:r0 + rr, :], ldst)

        def p_ln_fm(src_ap, fm_dram):
            for tile in range(NTIL):
                t0 = tile * 128
                xt = sb_xt[:, :]
                ld = P.dma("sync", xt, src_ap[t0:t0 + 128, :], "ld")
                P.wait("vector", "ld", ld)
                P.wait("scalar", "ld", ld)
                mcol = sb_st[:, 64:65]
                vcol = sb_st[:, 65:66]
                rcol = sb_st[:, 66:67]
                se = P.emit("scalar", lambda E: E.activation(
                    sb_xn[:, :], xt, AF.Square, accum_out=vcol))
                P.emit("vector", lambda E: E.reduce_sum(mcol, xt, axis=AXX))
                P.emit("vector",
                       lambda E: E.tensor_scalar_mul(mcol, mcol, 1.0 / D))
                P.wait("vector", "sf_scalar", se)
                P.emit("vector",
                       lambda E: E.tensor_scalar_mul(vcol, vcol, 1.0 / D))
                P.emit("vector",
                       lambda E: E.tensor_tensor(rcol, mcol, mcol, OP.mult))
                ve = P.emit("vector", lambda E: E.tensor_tensor(
                    vcol, vcol, rcol, OP.subtract))
                P.wait("scalar", "sf_vector", ve)
                P.emit("scalar",
                       lambda E: E.activation(rcol, vcol, AF.Ln, bias=1e-5))
                se = P.emit("scalar", lambda E: E.activation(
                    rcol, rcol, AF.Exp, scale=-0.5))
                P.wait("vector", "sf_scalar", se)
                P.emit("vector", lambda E: E.tensor_scalar(
                    sb_xn[:, :], xt, mcol, rcol, OP.subtract, OP.mult))
                for half in range(2):
                    if half == 0:
                        ve = P.emit("vector", lambda E: E.tensor_copy(
                            sb_fm[:, 0:2048], sb_xn[:, :]))
                    else:
                        ve = P.emit("vector", lambda E: E.scalar_tensor_tensor(
                            sb_fm[:, 0:2048], sb_fm[:, 0:2048], -1.0,
                            sb_xn[:, :], OP.mult, OP.add))
                    P.wait("tensor", "sf_vector", ve)
                    last = None
                    for c in range(16):
                        last = t_evac(sb_fm[:, c * 128:(c + 1) * 128],
                                      sb_fm[:, 2048 + c * 128:
                                            2048 + (c + 1) * 128])
                    P.wait("sync", "ve", last)
                    st = P.dma("sync",
                               fm_dram[half * D:(half + 1) * D, t0:t0 + 128]
                               .rearrange("(c p) n -> p c n", p=128),
                               sb_fm[:, 2048:4096]
                               .rearrange("p (c n) -> p c n", n=128), "st")
                    P.wait("sync", "st", st)
                    P.wait("vector", "st", st)

        def p_coll(kind, cin, cout):
            op = OP.add if kind == "ReduceScatter" else OP.bypass
            cc = P.emit("gpsimd", lambda E, a=cin, b=cout, k=kind, o=op:
                        E.collective_compute(
                            k, o, replica_groups=[[0, 1, 2, 3], [4, 5, 6, 7]],
                            ins=[a[:, :]], outs=[b[:, :]]), inc="cc")
            P.wait("gpsimd", "cc", cc)

        def p_attn_a_qkv():
            qsc = 1.0 / math.sqrt(DH)
            for wi, (whi, wlo, dstd, sc) in enumerate(
                    ((wq_h, wq_l, qh_d, qsc), (wk_h, wk_l, kh_d, 1.0),
                     (wv_h, wv_l, vt_d, 1.0))):
                P.wait("sync", "mm", P.val("mm"))
                lw = None
                for kc in range(16):
                    lw = P.dma("sync", sb_q[:, kc * 512:(kc + 1) * 512],
                               whi[kc * 128:(kc + 1) * 128, :], "ld")
                    lw = P.dma("sync", sb_k[:, kc * 512:(kc + 1) * 512],
                               wlo[kc * 128:(kc + 1) * 128, :], "ld")
                P.wait("tensor", "ld", lw)
                xguard = [None]
                bank_guard = {}
                for tg in range(FT):
                    rank, tloc = tg // NTG, (tg % NTG) * 512
                    P.wait("sync", "mm", xguard[0])
                    lx = None
                    for kc in range(16):
                        lx = P.dma("sync",
                                   sbf[:, kc * 512:(kc + 1) * 512],
                                   xn_co[rank * 2 * D + kc * 128:
                                         rank * 2 * D + (kc + 1) * 128,
                                         tloc:tloc + 512], "ld")
                        lx = P.dma("sync",
                                   sbf[:, 8192 + kc * 512:8192 + (kc + 1) * 512],
                                   xn_co[rank * 2 * D + D + kc * 128:
                                         rank * 2 * D + D + (kc + 1) * 128,
                                         tloc:tloc + 512], "ld")
                    P.wait("tensor", "ld", lx)
                    lastmm = None
                    for m in range(HPC):
                        bi = m % 2
                        bank = pb[bi]
                        P.wait("tensor", "sf_vector", bank_guard.get(bi))
                        mm = None
                        for kc in range(16):
                            mm = mm3(bank[:, :],
                                     sb_q[:, kc * 512 + m * 128:
                                          kc * 512 + (m + 1) * 128],
                                     sb_k[:, kc * 512 + m * 128:
                                          kc * 512 + (m + 1) * 128],
                                     sbf[:, kc * 512:(kc + 1) * 512],
                                     sbf[:, 8192 + kc * 512:
                                         8192 + (kc + 1) * 512],
                                     first=(kc == 0), last=(kc == 15),
                                     inc="mm" if kc == 15 else None)
                        lastmm = mm
                        P.wait("vector", "mm", mm)
                        so = f32_ring.acquire("vector")
                        dst = sfmf[:, so * 512:(so + 1) * 512]
                        ve = P.emit("vector", lambda E, d=dst, b=bank, s=sc:
                                    E.tensor_scalar_mul(d, b[:, :], s))
                        bank_guard[bi] = ve
                        if wi < 2:
                            P.wait("sync", "sf_vector", ve)
                            f32_ring.dma("sync", so,
                                         dstd[m, :, tg * 512:(tg + 1) * 512],
                                         dst)
                        else:
                            P.wait("tensor", "sf_vector", ve)
                            tvs = None
                            for c in range(4):
                                tvs = t_evac(dst[:, c * 128:(c + 1) * 128],
                                             sptf[:, (tg % 2) * 512 + c * 128:
                                                  (tg % 2) * 512 + (c + 1) * 128],
                                             f32=True)
                            f32_ring.consumed(so, "mm", P.val("mm"))
                            P.wait("sync", "ve", tvs)
                            st = P.dma("sync",
                                       vt_d[m, tg * 512:(tg + 1) * 512, :]
                                       .rearrange("(c p) n -> p c n", p=128),
                                       sptf[:, (tg % 2) * 512:
                                            (tg % 2) * 512 + 512]
                                       .rearrange("p (c n) -> p c n", n=128),
                                       "st")
                            P.wait("vector", "st", st)
                    xguard[0] = lastmm

        def p_attn_heads(layer):
            krq = [swstf, sws2f, slwf, szbf]
            if layer == "a":
                for hh in range(HPC):
                    P.dma("sync", sb_co[:, AL5 + hh * 512:AL5 + (hh + 1) * 512],
                          alibi5[hh, :, :], "ld")
                    P.dma("sync", sb_co[:, ALD + hh * 128:ALD + (hh + 1) * 128],
                          alibid[hh, :, :], "ld")
                P.dma("sync", sb_co[:, ALC:ALC + HPC * 64], alibic, "ld")
            else:
                P.dma("sync", sb_co[:, DGM:DGM + 128], diagm, "ld")
                for qt in range((T + 2047) // 2048):
                    wdt = min(2048, T - qt * 2048)
                    P.dma("sync", krq[qt][0:64, 0:wdt],
                          kr_d[:, qt * 2048:qt * 2048 + wdt], "ld")
            qring = Ring(P, 2, dma_sems=["r16", "r17"])
            rring = Ring(P, 2, dma_sems=["r18", "r19"])
            yring = Ring(P, 2, dma_sems=["r20", "r21"])
            Tv = min(T, 4096)
            for h in range(HPC):
                P.wait("sync", "mm", P.val("mm"))
                l1 = P.dma("sync", skf[:, 0:Tv], kh_d[h, :, 0:Tv], "ld")
                if T > 4096:
                    l1 = P.dma("sync", sqf[:, 0:4096], kh_d[h, :, 4096:8192],
                               "ld")
                l1 = P.dma("sync",
                           svf[:, 0:Tv].rearrange("p (c n) -> p c n", n=128),
                           vt_d[h, 0:Tv, :].rearrange("(c p) n -> p c n",
                                                      p=128), "ld")
                if T > 4096:
                    l1 = P.dma("sync",
                               spf[:, :].rearrange("p (c n) -> p c n", n=128),
                               vt_d[h, 4096:8192, :]
                               .rearrange("(c p) n -> p c n", p=128), "ld")
                P.wait("tensor", "ld", l1)
                sbank = {}
                ytguard = [None]
                for qb in range(NQB):
                    tiles = k_tiles_for_qblock(qb)
                    nt = len(tiles)
                    mcol = sb_st[:, 70:71]
                    nmcol = sb_st[:, 71:72]
                    mt = sb_st[:, 69:70]
                    lsums = sb_st[:, 72:92]
                    lcol = sb_st[:, 94:95]
                    ilcol = sb_st[:, 95:96]
                    qs = qring.acquire("sync")
                    lq = qring.dma("sync", qs,
                                   sb_co[:, QRING + qs * 128:
                                         QRING + (qs + 1) * 128],
                                   qh_d[h, :, qb * 128:(qb + 1) * 128])
                    qsl = sb_co[:, QRING + qs * 128:QRING + (qs + 1) * 128]
                    if layer == "b":
                        rs = rring.acquire("sync")
                        lq = rring.dma("sync", rs,
                                       sb_co[0:64, RRING + rs * 128:
                                             RRING + (rs + 1) * 128],
                                       qr_d[h * DHR:(h + 1) * DHR,
                                            qb * 128:(qb + 1) * 128])
                        rsl = sb_co[0:64, RRING + rs * 128:RRING + (rs + 1) * 128]
                    P.wait("tensor", qring.dma_sems[qs], P.val(qring.dma_sems[qs]))
                    if layer == "b":
                        P.wait("tensor", rring.dma_sems[rs],
                               P.val(rring.dma_sems[rs]))
                    P.wait("tensor", "sf_scalar", P.val("sf_scalar"))
                    lastq = None
                    for j, (k0, w, diag) in enumerate(tiles):
                        bi = j % 2
                        bank = pb[bi]
                        P.wait("tensor", sbank.get(bi, ("ve", None))[0],
                               sbank.get(bi, ("ve", None))[1])
                        kbuf = skf if k0 < 4096 else sqf
                        kofs = k0 if k0 < 4096 else k0 - 4096
                        if layer == "a":
                            mm = P.emit("tensor", lambda E, b=bank, q=qsl,
                                        kb=kbuf, ko=kofs, w=w:
                                        E.matmul(b[:, 0:w], q, kb[:, ko:ko + w],
                                                 start=True, stop=True),
                                        inc="mm")
                        else:
                            P.emit("tensor", lambda E, b=bank, q=qsl, kb=kbuf,
                                   ko=kofs, w=w:
                                   E.matmul(b[:, 0:w], q, kb[:, ko:ko + w],
                                            start=True, stop=False))
                            rbuf = krq[k0 // 2048]
                            rofs = k0 % 2048
                            mm = P.emit("tensor", lambda E, b=bank, r=rsl,
                                        rb=rbuf, ro=rofs, w=w:
                                        E.matmul(b[:, 0:w], r,
                                                 rb[0:64, ro:ro + w],
                                                 start=False, stop=True),
                                        inc="mm")
                        doff = (qb * 128 - k0) // 128
                        if layer == "a":
                            cofs = (ALD + h * 128) if diag else (AL5 + h * 512)
                            P.wait("vector", "mm", mm)
                            tok = P.emit("vector", lambda E, b=bank, k0=k0, w=w,
                                         cofs=cofs, h=h, doff=doff, diag=diag:
                                         E.scalar_tensor_tensor(
                                             sb_s[:, k0:k0 + w], b[:, 0:w],
                                             (0.0 if diag else
                                              sb_co[:, ALC + h * 64 + doff:
                                                    ALC + h * 64 + doff + 1]),
                                             sb_co[:, cofs:cofs + w],
                                             OP.add, OP.add))
                            sbank[bi] = ("sf_vector", tok)
                            P.wait("vector", "sf_vector", tok)
                        elif diag:
                            P.wait("vector", "mm", mm)
                            tok = P.emit("vector", lambda E, b=bank, k0=k0, w=w:
                                         E.tensor_tensor(
                                             sb_s[:, k0:k0 + w], b[:, 0:w],
                                             sb_co[:, DGM:DGM + 128], OP.add))
                            sbank[bi] = ("sf_vector", tok)
                        else:
                            P.wait("scalar", "mm", mm)
                            tok = P.emit("scalar", lambda E, b=bank, k0=k0, w=w:
                                         E.copy(sb_s[:, k0:k0 + w], b[:, 0:w]))
                            sbank[bi] = ("sf_scalar", tok)
                            P.wait("vector", "sf_scalar", tok)
                        P.emit("vector", lambda E, k0=k0, w=w:
                               E.reduce_max(mt, sb_s[:, k0:k0 + w], axis=AXX))
                        if j == 0:
                            P.emit("vector", lambda E: E.tensor_copy(mcol, mt))
                        else:
                            P.emit("vector", lambda E:
                                   E.tensor_tensor(mcol, mcol, mt, OP.max))
                    qring.consumed(qs, "mm", P.val("mm"))
                    if layer == "b":
                        rring.consumed(rs, "mm", P.val("mm"))
                    ge = P.emit("vector", lambda E:
                                E.tensor_scalar_mul(nmcol, mcol, -1.0))
                    P.wait("scalar", "sf_vector", ge)
                    opsum = pb[4]
                    pguard = {}
                    for j, (k0, w, diag) in enumerate(tiles):
                        se = P.emit("scalar", lambda E, k0=k0, w=w, j=j:
                                    E.activation(sb_s[:, k0:k0 + w],
                                                 sb_s[:, k0:k0 + w],
                                                 AF.Exp, bias=nmcol,
                                                 accum_out=lsums[:, j:j + 1]))
                        P.wait("tensor", "sf_scalar", se)
                        lastpv = None
                        for c in range(w // 128):
                            P.wait("tensor", "mm", pguard.get(j % 2))
                            ve = t_evac(sb_s[:, k0 + c * 128:k0 + (c + 1) * 128],
                                        sptf[:, (j % 2) * 512 + c * 128:
                                             (j % 2) * 512 + (c + 1) * 128],
                                        f32=True)
                            P.wait("tensor", "ve", ve)
                            kc = k0 // 128 + c
                            vbuf = svf if kc < 32 else spf
                            vofs = (kc % 32) * 128
                            lastpv = P.emit(
                                "tensor", lambda E, j=j, c=c, vb=vbuf, vo=vofs,
                                first=(j == 0 and c == 0),
                                last=(j == nt - 1 and c == w // 128 - 1):
                                E.matmul(opsum[:, 0:128],
                                         sptf[:, (j % 2) * 512 + c * 128:
                                              (j % 2) * 512 + (c + 1) * 128],
                                         vb[:, vo:vo + 128],
                                         start=first, stop=last),
                                inc="mm" if (j == nt - 1 and c == w // 128 - 1)
                                else None)
                        pguard[j % 2] = P.val("mm")
                    mmend = P.val("mm")
                    P.wait("vector", "sf_scalar", P.val("sf_scalar"))
                    P.emit("vector", lambda E, nt=nt:
                           E.reduce_sum(lcol, lsums[:, 0:nt], axis=AXX))
                    P.emit("vector", lambda E: E.reciprocal(ilcol, lcol))
                    P.wait("vector", "mm", mmend)
                    P.wait("vector", "mm", ytguard[0])
                    ysl = yring.acquire("vector")
                    yv = P.emit("vector", lambda E, ysl=ysl:
                                E.tensor_scalar_mul(
                                    sfmf[:, ysl * 256:ysl * 256 + 128],
                                    opsum[:, 0:128], ilcol))
                    P.wait("tensor", "sf_vector", yv)
                    yve = t_evac(sfmf[:, ysl * 256:ysl * 256 + 128],
                                 sfmf[:, ysl * 256 + 128:ysl * 256 + 256],
                                 f32=True)
                    ytguard[0] = P.val("mm")
                    P.wait("sync", "ve", yve)
                    yring.dma("sync", ysl,
                              yfm_d[h * DH:(h + 1) * DH,
                                    qb * 128:(qb + 1) * 128],
                              sfmf[:, ysl * 256 + 128:ysl * 256 + 256])
                P.wait("sync", "st", P.val("st"))

        def p_partial_proj(w_ap):
            lw = None
            for c in range(4):
                wbuf = skf if c < 2 else sqf
                lw = P.dma("sync", wbuf[:, (c % 2) * 2048:(c % 2 + 1) * 2048],
                           w_ap[c * 128:(c + 1) * 128, :], "ld")
            P.wait("tensor", "ld", lw)
            yr = Ring(P, 2, dma_sems=["r16", "r17"])
            bank_guard = {}
            for t in range(T // 128):
                sl = yr.acquire("sync")
                ld = yr.dma("sync", sl,
                            sb_sc[:, sl * 512:sl * 512 + 512]
                            .rearrange("p (c n) -> p c n", n=128),
                            yfm_d[:, t * 128:(t + 1) * 128]
                            .rearrange("(c p) n -> p c n", p=128))
                P.wait("tensor", yr.dma_sems[sl], P.val(yr.dma_sems[sl]))
                lastmm = None
                for n in range(4):
                    bank = pb[n % 4]
                    P.wait("tensor", "sf_vector", bank_guard.get(n % 4))
                    mm = None
                    for kc in range(4):
                        wbuf = skf if kc < 2 else sqf
                        mm = P.emit("tensor", lambda E, b=bank, sl=sl, kc=kc,
                                    n=n, wb=wbuf,
                                    first=(kc == 0), last=(kc == 3):
                                    E.matmul(b[:, :],
                                             sb_sc[:, sl * 512 + kc * 128:
                                                   sl * 512 + (kc + 1) * 128],
                                             wb[:, (kc % 2) * 2048 + n * 512:
                                                (kc % 2) * 2048 + (n + 1) * 512],
                                             start=first, stop=last),
                                    inc="mm" if kc == 3 else None)
                    lastmm = mm
                    P.wait("vector", "mm", mm)
                    so = f32_ring.acquire("vector")
                    dst = sfmf[:, so * 512:(so + 1) * 512]
                    ve = P.emit("vector", lambda E, d=dst, b=bank:
                                E.tensor_copy(d, b[:, :]))
                    bank_guard[n % 4] = ve
                    P.wait("sync", "sf_vector", ve)
                    f32_ring.dma("sync", so,
                                 pp_ci[t * 128:(t + 1) * 128,
                                       n * 512:(n + 1) * 512], dst)
                yr.consumed(sl, "mm", lastmm)

        def p_residual(src_ap, dst_ap, with_bz):
            if with_bz:
                lb = P.dma("sync", sb_xn[:, :], bzrep[:, :], "ld")
                P.wait("vector", "ld", lb)
            for t in range(NTIL):
                t0 = t * 128
                l1 = P.dma("sync", sb_sc[:, 0:2048], pp_co[t0:t0 + 128, :], "ld")
                l2 = P.dma("sync", sb_xt[:, :], src_ap[t0:t0 + 128, :], "ld")
                P.wait("vector", "ld", l2)
                buf = sb_sc2[:, 0:2048]
                if with_bz:
                    P.emit("vector", lambda E, buf=buf: E.tensor_tensor(
                        buf, sb_sc[:, 0:2048], sb_xn[:, :], OP.add))
                    ve = P.emit("vector", lambda E, buf=buf: E.tensor_tensor(
                        buf, buf, sb_xt[:, :], OP.add))
                else:
                    ve = P.emit("vector", lambda E, buf=buf: E.tensor_tensor(
                        buf, sb_sc[:, 0:2048], sb_xt[:, :], OP.add))
                P.wait("sync", "sf_vector", ve)
                st = P.dma("sync", dst_ap[t0:t0 + 128, :], buf, "st")
                P.wait("vector", "st", st)

        def p_attn_b_h():
            lw = None
            for kc in range(16):
                hbuf = sb_q if kc < 8 else sb_k
                lbuf = sb_v if kc < 8 else sb_p
                lw = P.dma("sync", hbuf[:, (kc % 8) * 1024:(kc % 8 + 1) * 1024],
                           wa_h[kc * 128:(kc + 1) * 128, :], "ld")
                lw = P.dma("sync", lbuf[:, (kc % 8) * 1024:(kc % 8 + 1) * 1024],
                           wa_l[kc * 128:(kc + 1) * 128, :], "ld")
            P.wait("tensor", "ld", lw)
            xguard = [None]
            bank_guard = {}
            for tg in range(NTG):
                tloc = tg * 512
                P.wait("sync", "mm", xguard[0])
                lx = None
                for kc in range(16):
                    lx = P.dma("sync", sbf[:, kc * 512:(kc + 1) * 512],
                               xn_ci[kc * 128:(kc + 1) * 128, tloc:tloc + 512],
                               "ld")
                    lx = P.dma("sync",
                               sbf[:, 8192 + kc * 512:8192 + (kc + 1) * 512],
                               xn_ci[D + kc * 128:D + (kc + 1) * 128,
                                     tloc:tloc + 512], "ld")
                P.wait("tensor", "ld", lx)
                lastmm = None
                for m in range(8):
                    bi = m % 2
                    bank = pb[bi]
                    P.wait("tensor", "sf_vector", bank_guard.get(bi))
                    mm = None
                    for kc in range(16):
                        hbuf = sb_q if kc < 8 else sb_k
                        lbuf = sb_v if kc < 8 else sb_p
                        mm = mm3(bank[:, :],
                                 hbuf[:, (kc % 8) * 1024 + m * 128:
                                      (kc % 8) * 1024 + (m + 1) * 128],
                                 lbuf[:, (kc % 8) * 1024 + m * 128:
                                      (kc % 8) * 1024 + (m + 1) * 128],
                                 sbf[:, kc * 512:(kc + 1) * 512],
                                 sbf[:, 8192 + kc * 512:8192 + (kc + 1) * 512],
                                 first=(kc == 0), last=(kc == 15),
                                 inc="mm" if kc == 15 else None)
                    lastmm = mm
                    P.wait("vector", "mm", mm)
                    so = f32_ring.acquire("vector")
                    dst = sfmf[:, so * 512:(so + 1) * 512]
                    ve = P.emit("vector", lambda E, d=dst, b=bank, m=m:
                                E.tensor_scalar_add(d, b[:, :],
                                                    sb_bi[:, BA + m:BA + m + 1]))
                    bank_guard[bi] = ve
                    P.wait("sync", "sf_vector", ve)
                    f32_ring.dma("sync", so,
                                 h_ci[m * 128:(m + 1) * 128, tloc:tloc + 512],
                                 dst)
                xguard[0] = lastmm

        def p_attn_b_proj():
            lw = None
            for c in range(4):
                lw = P.dma("sync", sqf[:, c * 512:(c + 1) * 512],
                           wuq[c * 128:(c + 1) * 128, :], "ld")
                lw = P.dma("sync", skf[:, c * 512:(c + 1) * 512],
                           wuk[c * 128:(c + 1) * 128, :], "ld")
                lw = P.dma("sync", svf[:, c * 512:(c + 1) * 512],
                           wuv[c * 128:(c + 1) * 128, :], "ld")
            for kc in range(8):
                lw = P.dma("sync", spf[:, kc * 256:(kc + 1) * 256],
                           wc[kc * 128:(kc + 1) * 128, :], "ld")
                lw = P.dma("sync", sqf[:, 2048 + kc * 256:2048 + (kc + 1) * 256],
                           wcs[kc * 128:(kc + 1) * 128, :], "ld")
                lw = P.dma("sync", skf[:, 2048 + kc * 64:2048 + (kc + 1) * 64],
                           wb[kc * 128:(kc + 1) * 128, :], "ld")
                lw = P.dma("sync", skf[:, 2560 + kc * 64:2560 + (kc + 1) * 64],
                           wbs[kc * 128:(kc + 1) * 128, :], "ld")
            P.wait("tensor", "ld", lw)
            s192 = (DH + DHR) ** (-0.5)
            xguard = [None]
            bank_guard = {}
            for tg in range(FT):
                rank, tloc = tg // NTG, (tg % NTG) * 512
                half = (tg % 2) * 4096
                P.wait("sync", "mm", xguard[0] if tg >= 2 else None)
                lx = None
                for kc in range(8):
                    lx = P.dma("sync",
                               sb_s[:, half + kc * 512:half + (kc + 1) * 512],
                               h_co[rank * 2 * DL + kc * 128:
                                    rank * 2 * DL + (kc + 1) * 128,
                                    tloc:tloc + 512], "ld")
                P.wait("tensor", "ld", lx)
                lastmm = None
                for wi, (wbuf, dstd) in enumerate(
                        ((sqf, qh_d), (skf, kh_d), (svf, vt_d))):
                    kcs = range(4, 8) if wi == 0 else range(0, 4)
                    for m in range(HPC):
                        bi = m % 2
                        bank = pb[bi]
                        P.wait("tensor", "sf_vector", bank_guard.get(bi))
                        mm = None
                        for j, kc in enumerate(kcs):
                            mm = P.emit("tensor", lambda E, b=bank, wb=wbuf,
                                        m=m, j=j, kc=kc, half=half,
                                        first=(j == 0), last=(j == 3):
                                        E.matmul(b[:, :],
                                                 wb[:, j * 512 + m * 128:
                                                    j * 512 + (m + 1) * 128],
                                                 sb_s[:, half + kc * 512:
                                                      half + (kc + 1) * 512],
                                                 start=first, stop=last),
                                        inc="mm" if j == 3 else None)
                        lastmm = mm
                        P.wait("vector", "mm", mm)
                        so = f32_ring.acquire("vector")
                        dst = sfmf[:, so * 512:(so + 1) * 512]
                        sc = s192 if wi == 0 else 1.0
                        ve = P.emit("vector", lambda E, d=dst, b=bank, s=sc:
                                    E.tensor_scalar_mul(d, b[:, :], s))
                        bank_guard[bi] = ve
                        if wi < 2:
                            P.wait("sync", "sf_vector", ve)
                            f32_ring.dma("sync", so,
                                         dstd[m, :, tg * 512:(tg + 1) * 512],
                                         dst)
                        else:
                            P.wait("tensor", "sf_vector", ve)
                            tvs = None
                            for c in range(4):
                                tvs = t_evac(dst[:, c * 128:(c + 1) * 128],
                                             sptf[:, (tg % 2) * 512 + c * 128:
                                                  (tg % 2) * 512 + (c + 1) * 128],
                                             f32=True)
                            f32_ring.consumed(so, "mm", P.val("mm"))
                            P.wait("sync", "ve", tvs)
                            st = P.dma("sync",
                                       vt_d[m, tg * 512:(tg + 1) * 512, :]
                                       .rearrange("(c p) n -> p c n", p=128),
                                       sptf[:, (tg % 2) * 512:
                                            (tg % 2) * 512 + 512]
                                       .rearrange("p (c n) -> p c n", n=128),
                                       "st")
                            P.wait("vector", "st", st)
                for m2 in range(2):
                    for sw in range(2):
                        bank = pb[2 + sw]
                        P.wait("tensor", "sf_vector", bank_guard.get(2 + sw))
                        wsl = (spf, 0) if sw == 0 else (sqf, 2048)
                        mm = None
                        for kc in range(8):
                            mm = P.emit("tensor", lambda E, b=bank, wb=wsl[0],
                                        wo0=wsl[1], m2=m2, kc=kc, half=half,
                                        first=(kc == 0), last=(kc == 7):
                                        E.matmul(b[:, :],
                                                 wb[:, wo0 + kc * 256 + m2 * 128:
                                                    wo0 + kc * 256 + (m2 + 1) * 128],
                                                 sb_s[:, half + kc * 512:
                                                      half + (kc + 1) * 512],
                                                 start=first, stop=last),
                                        inc="mm" if kc == 7 else None)
                        lastmm = mm
                        P.wait("vector", "mm", mm)
                        dstf = sb_sc[:, 0:512] if sw == 0 else sb_sc[:, 512:1024]
                        bia = (BC_ + m2) if sw == 0 else (BCS + m2)
                        ve = P.emit("vector", lambda E, d=dstf, b=bank, bia=bia:
                                    E.tensor_scalar_add(
                                        d, b[:, :], sb_bi[:, bia:bia + 1]))
                        bank_guard[2 + sw] = ve
                    lr = P.dma("sync", sb_rp[:, 0:512],
                               cosqr[m2 * 128:(m2 + 1) * 128,
                                     tg * 512:(tg + 1) * 512], "ld")
                    lr = P.dma("sync", sb_rp[:, 512:1024],
                               sinqr[m2 * 128:(m2 + 1) * 128,
                                     tg * 512:(tg + 1) * 512], "ld")
                    P.wait("vector", "ld", lr)
                    P.emit("vector", lambda E: E.tensor_tensor(
                        sb_sc[:, 1024:1536], sb_sc[:, 0:512], sb_rp[:, 0:512],
                        OP.mult))
                    P.emit("vector", lambda E: E.tensor_tensor(
                        sb_sc[:, 1536:2048], sb_sc[:, 512:1024],
                        sb_rp[:, 512:1024], OP.mult))
                    P.emit("vector", lambda E: E.tensor_tensor(
                        sb_sc[:, 1024:1536], sb_sc[:, 1024:1536],
                        sb_sc[:, 1536:2048], OP.add))
                    so = f32_ring.acquire("vector")
                    dst = sfmf[:, so * 512:(so + 1) * 512]
                    ve = P.emit("vector", lambda E, d=dst:
                                E.tensor_scalar_mul(d, sb_sc[:, 1024:1536],
                                                    s192))
                    P.wait("sync", "sf_vector", ve)
                    f32_ring.dma("sync", so,
                                 qr_d[m2 * 128:(m2 + 1) * 128,
                                      tg * 512:(tg + 1) * 512], dst)
                for sw in range(2):
                    bank = pb[4 + sw]
                    P.wait("tensor", "sf_vector", bank_guard.get(4 + sw))
                    wofs = 2048 if sw == 0 else 2560
                    mm = None
                    for kc in range(8):
                        mm = P.emit("tensor", lambda E, b=bank, wofs=wofs,
                                    kc=kc, half=half,
                                    first=(kc == 0), last=(kc == 7):
                                    E.matmul(b[0:64, :],
                                             skf[:, wofs + kc * 64:
                                                 wofs + (kc + 1) * 64],
                                             sb_s[:, half + kc * 512:
                                                  half + (kc + 1) * 512],
                                             start=first, stop=last),
                                    inc="mm" if kc == 7 else None)
                    lastmm = mm
                    P.wait("vector", "mm", mm)
                    dstf = sb_sc[0:64, 0:512] if sw == 0 else sb_sc[0:64, 512:1024]
                    bia = BB_ if sw == 0 else BBS
                    ve = P.emit("vector", lambda E, d=dstf, b=bank, bia=bia:
                                E.tensor_scalar_add(d, b[0:64, :],
                                                    sb_bi[0:64, bia:bia + 1]))
                    bank_guard[4 + sw] = ve
                lr = P.dma("sync", sb_rp[0:64, 0:512],
                           coskr[:, tg * 512:(tg + 1) * 512], "ld")
                lr = P.dma("sync", sb_rp[0:64, 512:1024],
                           sinkr[:, tg * 512:(tg + 1) * 512], "ld")
                P.wait("vector", "ld", lr)
                P.emit("vector", lambda E: E.tensor_tensor(
                    sb_sc[0:64, 1024:1536], sb_sc[0:64, 0:512],
                    sb_rp[0:64, 0:512], OP.mult))
                P.emit("vector", lambda E: E.tensor_tensor(
                    sb_sc[0:64, 1536:2048], sb_sc[0:64, 512:1024],
                    sb_rp[0:64, 512:1024], OP.mult))
                so = f32_ring.acquire("vector")
                dst = sfmf[0:64, so * 512:(so + 1) * 512]
                ve = P.emit("vector", lambda E, d=dst: E.tensor_tensor(
                    d, sb_sc[0:64, 1024:1536], sb_sc[0:64, 1536:2048], OP.add))
                P.wait("sync", "sf_vector", ve)
                f32_ring.dma("sync", so, kr_d[:, tg * 512:(tg + 1) * 512], dst)
                xguard[0] = lastmm

        def p_routing():
            lw = None
            for kc in range(16):
                lw = P.dma("sync", sb_zb[:, kc * 8:(kc + 1) * 8],
                           confw_h[kc * 128:(kc + 1) * 128, :], "ld")
                lw = P.dma("sync", sb_zb[:, 128 + kc * 8:128 + (kc + 1) * 8],
                           confw_l[kc * 128:(kc + 1) * 128, :], "ld")
            P.wait("tensor", "ld", lw)
            xguard = [None]
            for tg in range(NTG):
                tloc = tg * 512
                P.wait("sync", "mm", xguard[0])
                lx = None
                for kc in range(16):
                    lx = P.dma("sync", sbf[:, kc * 512:(kc + 1) * 512],
                               xnfm_d[kc * 128:(kc + 1) * 128, tloc:tloc + 512],
                               "ld")
                    lx = P.dma("sync",
                               sbf[:, 8192 + kc * 512:8192 + (kc + 1) * 512],
                               xnfm_d[D + kc * 128:D + (kc + 1) * 128,
                                      tloc:tloc + 512], "ld")
                P.wait("tensor", "ld", lx)
                mm = None
                combos = [(0, 0), (0, 8192), (128, 0)]
                for ci, (wofs_, xofs_) in enumerate(combos):
                    for kc in range(16):
                        lastf = (ci == 2 and kc == 15)
                        mm = P.emit("tensor", lambda E, kc=kc, w0=wofs_,
                                    x0=xofs_, first=(ci == 0 and kc == 0),
                                    last=lastf:
                                    E.matmul(pb[0][0:NE, :],
                                             sb_zb[:, w0 + kc * 8:
                                                   w0 + (kc + 1) * 8],
                                             sbf[:, x0 + kc * 512:
                                                 x0 + (kc + 1) * 512],
                                             start=first, stop=last),
                                    inc="mm" if lastf else None)
                xguard[0] = mm
                P.wait("vector", "mm", mm)
                ve = P.emit("vector", lambda E, tg=tg: E.tensor_scalar_add(
                    sb_sc[0:NE, tg * 512:(tg + 1) * 512], pb[0][0:NE, :],
                    sb_bi[0:NE, CB_:CB_ + 1]))
                P.wait("tensor", "sf_vector", ve)
            ve = P.val("sf_vector")
            P.wait("scalar", "sf_vector", ve)
            se = P.emit("scalar", lambda E: E.activation(
                sb_sc[0:NE, 0:TOK], sb_sc[0:NE, 0:TOK], AF.Sigmoid))
            P.wait("vector", "sf_scalar", se)
            ve = P.emit("vector", lambda E: E.tensor_scalar_mul(
                sb_sc[0:NE, 0:TOK], sb_sc[0:NE, 0:TOK],
                sb_bi[0:NE, WE_:WE_ + 1]))
            P.wait("tensor", "sf_vector", ve)
            for t in range(NTIL):
                mm = P.emit("tensor", lambda E, t=t: E.transpose(
                    pb[1][:, 0:NE], sb_sc[0:NE, t * 128:(t + 1) * 128],
                    sb_cf[0:NE, 0:NE]), inc="mm")
                P.wait("vector", "mm", mm)
                ve = P.emit("vector", lambda E, t=t: E.tensor_copy(
                    sb_sc2[:, t * 8:(t + 1) * 8], pb[1][:, 0:NE]))
                P.wait("tensor", "sf_vector", ve)
            for t in range(NTIL):
                bid = sb_sc2[:, t * 8:(t + 1) * 8]
                top8 = sb_st[:, 96:104]
                m1 = sb_st[:, 104:105]
                m2 = sb_st[:, 105:106]
                d12 = sb_st[:, 106:107]
                w1 = sb_st[:, 107:108]
                w2 = sb_st[:, 108:109]
                eq1 = sb_st[:, 112:120]
                eq2 = sb_st[:, 120:128]
                ne1 = sb_st[:, 128:136]
                acc = sb_st[:, 136:144]
                P.emit("vector", lambda E, bid=bid: E.max(out=top8, in_=bid))
                P.emit("vector", lambda E: E.tensor_copy(m1, top8[:, 0:1]))
                P.emit("vector", lambda E: E.tensor_copy(m2, top8[:, 1:2]))
                ve = P.emit("vector", lambda E: E.tensor_scalar(
                    d12, m1, m2, None, OP.subtract))
                P.wait("scalar", "sf_vector", ve)
                se = P.emit("scalar", lambda E:
                            E.activation(w1, d12, AF.Sigmoid))
                P.wait("vector", "sf_scalar", se)
                P.emit("vector", lambda E: E.tensor_scalar(
                    w2, w1, -1.0, 1.0, OP.mult, OP.add))
                P.emit("vector", lambda E, bid=bid: E.tensor_scalar(
                    eq1, bid, m1, None, OP.is_equal))
                P.emit("vector", lambda E, bid=bid: E.tensor_scalar(
                    eq2, bid, m2, None, OP.is_equal))
                P.emit("vector", lambda E: E.tensor_scalar(
                    ne1, eq1, -1.0, 1.0, OP.mult, OP.add))
                P.emit("vector", lambda E: E.tensor_scalar(acc, eq1, w1, None,
                                                           OP.mult))
                P.emit("vector", lambda E: E.tensor_tensor(eq2, eq2, ne1,
                                                           OP.mult))
                P.emit("vector", lambda E: E.tensor_scalar(eq2, eq2, w2, None,
                                                           OP.mult))
                ve = P.emit("vector", lambda E: E.tensor_tensor(
                    acc, acc, eq2, OP.add))
                P.wait("tensor", "sf_vector", ve)
                mm = P.emit("tensor", lambda E: E.transpose(
                    pb[1][0:NE, 0:128], acc, sb_cf[:, 0:128]), inc="mm")
                P.wait("vector", "mm", mm)
                ve = P.emit("vector", lambda E, t=t: E.tensor_copy(
                    sb_sc[0:NE, 2048 + t * 128:2048 + (t + 1) * 128],
                    pb[1][0:NE, 0:128]))
                P.wait("tensor", "sf_vector", ve)
            P.wait("sync", "sf_vector", P.val("sf_vector"))
            P.dma("sync", wt_d[:, :], sb_sc[0:NE, 2048:2048 + TOK], "st")

        def emit_wbc(tg):
            for e_ in range(NE):
                lwv = P.dma("sync", sb_rp[0:1, 0:512],
                            wt_d[e_:e_ + 1, tg * 512:(tg + 1) * 512], "ld")
                P.wait("tensor", "ld", lwv)
                mm = P.emit("tensor", lambda E: E.matmul(
                    pb[2][:, :], sb_or[0:1, 0:128], sb_rp[0:1, 0:512],
                    start=True, stop=True), inc="mm")
                P.wait("sync", "mm", mm)
                P.wait("vector", "mm", mm)
                ve = P.emit("vector", lambda E, e_=e_: E.tensor_copy(
                    sb_sc[:, e_ * 512:(e_ + 1) * 512], pb[2][:, :]))
                P.wait("tensor", "sf_vector", ve)
            return P.val("sf_vector")

        def p_ffn_experts():
            whr = Ring(P, 4)
            hxr = Ring(P, 2, dma_sems=["r22", "r23"])
            wldr = Ring(P, 2)
            lorar = Ring(P, 2)
            for tg in range(NTG):
                tloc = tg * 512
                P.wait("sync", "mm", P.val("mm"))
                lx = None
                for kc in range(16):
                    lx = P.dma("sync", sbf[:, kc * 512:(kc + 1) * 512],
                               xnfm_d[kc * 128:(kc + 1) * 128, tloc:tloc + 512],
                               "ld")
                    lx = P.dma("sync",
                               sbf[:, 8192 + kc * 512:8192 + (kc + 1) * 512],
                               xnfm_d[D + kc * 128:D + (kc + 1) * 128,
                                      tloc:tloc + 512], "ld")
                emit_wbc(tg)
                for gi, (ah, al, xaofs) in enumerate(
                        ((gA_h, gA_l, 0), (uA_h, uA_l, 4096))):
                    P.wait("sync", "mm", P.val("mm"))
                    la = None
                    for kc in range(16):
                        hb = sb_wst if kc < 8 else sb_ws2
                        lb = sb_lw if kc < 8 else sb_zb
                        la = P.dma("sync",
                                   hb[:, (kc % 8) * 512:(kc % 8 + 1) * 512],
                                   ah[kc * 128:(kc + 1) * 128, :], "ld")
                        la = P.dma("sync",
                                   lb[:, (kc % 8) * 512:(kc % 8 + 1) * 512],
                                   al[kc * 128:(kc + 1) * 128, :], "ld")
                    P.wait("tensor", "ld", max(lx, la))
                    for til in range(4):
                        mm = None
                        for kc in range(16):
                            hb = sb_wst if kc < 8 else sb_ws2
                            lb = sb_lw if kc < 8 else sb_zb
                            mm = mm3(pb[2][:, :],
                                     sbf[:, kc * 512 + til * 128:
                                         kc * 512 + (til + 1) * 128],
                                     sbf[:, 8192 + kc * 512 + til * 128:
                                         8192 + kc * 512 + (til + 1) * 128],
                                     hb[:, (kc % 8) * 512:(kc % 8 + 1) * 512],
                                     lb[:, (kc % 8) * 512:(kc % 8 + 1) * 512],
                                     first=(kc == 0), last=(kc == 15),
                                     inc="mm" if kc == 15 else None)
                        P.wait("scalar", "mm", mm)
                        se = P.emit("scalar", lambda E:
                                    E.mul(sb_sm[:, :], pb[2][:, :], SCALING))
                        P.wait("vector", "sf_scalar", se)
                        vh = P.emit("vector", lambda E: E.tensor_copy(
                            sb_fm[:, 0:512], sb_sm[:, :]))
                        vl = P.emit("vector", lambda E: E.scalar_tensor_tensor(
                            sb_fm[:, 512:1024], sb_fm[:, 0:512], -1.0,
                            sb_sm[:, :], OP.mult, OP.add))
                        P.wait("tensor", "sf_vector", vl)
                        P.wait("tensor", "sf_scalar", se)
                        P.wait("scalar", "sf_vector", vl)
                        for c in range(4):
                            t_evac(sb_fm[:, c * 128:(c + 1) * 128],
                                   sb_p[:, xaofs + c * 512 + til * 128:
                                        xaofs + c * 512 + (til + 1) * 128])
                            t_evac(sb_fm[:, 512 + c * 128:512 + (c + 1) * 128],
                                   sb_p[:, xaofs + 2048 + c * 512 + til * 128:
                                        xaofs + 2048 + c * 512 + (til + 1) * 128])
                t_guard()
                for pr in range(4):
                    P.emit("tensor", lambda E, pr=pr: E.matmul(
                        pb[2 + pr][:, :], sb_or[0:1, 0:128],
                        sb_zf[0:1, 0:512], start=True, stop=False))
                P.wait("sync", "mm", P.val("mm"))
                for ic in range(NIC):
                    half = wldr.acquire("sync")
                    wofs = half * 2048
                    l1 = P.dma("sync", sb_wst[:, wofs:wofs + 2048],
                               bgate_h[ic * 128:(ic + 1) * 128, :], "ld")
                    l1 = P.dma("sync", sb_ws2[:, wofs:wofs + 2048],
                               bgate_l[ic * 128:(ic + 1) * 128, :], "ld")
                    l1 = P.dma("sync", sb_lw[:, wofs:wofs + 2048],
                               bup_h[ic * 128:(ic + 1) * 128, :], "ld")
                    l1 = P.dma("sync", sb_zb[:, wofs:wofs + 2048],
                               bup_l[ic * 128:(ic + 1) * 128, :], "ld")
                    lh = lorar.acquire("sync")
                    lofs = lh * 3072
                    l1 = P.dma("sync", sb_q[:, lofs:lofs + 512],
                               gB_h[ic * 128:(ic + 1) * 128, :], "ld")
                    l1 = P.dma("sync", sb_q[:, lofs + 512:lofs + 1024],
                               gB_l[ic * 128:(ic + 1) * 128, :], "ld")
                    l1 = P.dma("sync", sb_q[:, lofs + 1024:lofs + 1536],
                               uB_h[ic * 128:(ic + 1) * 128, :], "ld")
                    l1 = P.dma("sync", sb_q[:, lofs + 1536:lofs + 2048],
                               uB_l[ic * 128:(ic + 1) * 128, :], "ld")
                    l1 = P.dma("sync",
                               sqf[:, (lofs + 2048) // 2:(lofs + 2048) // 2 + 512],
                               dA[ic * 128:(ic + 1) * 128, :], "ld")
                    P.wait("tensor", "ld", l1)
                    for bsel in range(2):
                        hb = sb_wst if bsel == 0 else sb_lw
                        lb = sb_ws2 if bsel == 0 else sb_zb
                        bank = pb[0] if bsel == 0 else pb[1]
                        mm = None
                        for kc in range(16):
                            mm = mm3(bank[0:128, 0:512],
                                     hb[:, wofs + kc * 128:wofs + (kc + 1) * 128],
                                     lb[:, wofs + kc * 128:wofs + (kc + 1) * 128],
                                     sbf[:, kc * 512:(kc + 1) * 512],
                                     sbf[:, 8192 + kc * 512:
                                         8192 + (kc + 1) * 512],
                                     first=(kc == 0), last=(kc == 15),
                                     inc="mm" if kc == 15 else None)
                        P.wait("scalar", "mm", mm)
                        dstf = sb_rp[:, 0:512] if bsel == 0 else sb_rp[:, 512:1024]
                        se = P.emit("scalar", lambda E, d=dstf, b=bank:
                                    E.copy(d, b[0:128, 0:512]))
                        if bsel == 1:
                            P.wait("vector", "sf_scalar", se)
                    wldr.consumed(half, "mm", P.val("mm"))
                    sebase = P.val("sf_scalar")
                    for e_ in range(NE):
                        eb = (e_ % 2) * 64
                        ech = (e_ // 2)
                        mm = mm3(pb6[0:128, 0:512],
                                 sb_q[eb:eb + 64, lofs + ech * 128:
                                      lofs + (ech + 1) * 128],
                                 sb_q[eb:eb + 64, lofs + 512 + ech * 128:
                                      lofs + 512 + (ech + 1) * 128],
                                 sb_p[eb:eb + 64, ech * 512:(ech + 1) * 512],
                                 sb_p[eb:eb + 64, 2048 + ech * 512:
                                      2048 + (ech + 1) * 512],
                                 first=True, last=True, inc="mm")
                        P.wait("vector", "mm", mm)
                        P.wait("vector", "sf_scalar", sebase)
                        ve = P.emit("vector", lambda E: E.tensor_tensor(
                            sb_sc2[:, 0:512], sb_rp[:, 0:512],
                            pb6[0:128, 0:512], OP.add))
                        P.wait("scalar", "sf_vector", ve)
                        se = P.emit("scalar", lambda E: E.activation(
                            sb_sc2[:, 512:1024], sb_sc2[:, 0:512], AF.Silu))
                        P.wait("tensor", "sf_vector", ve)
                        mm = mm3(pb6[0:128, 0:512],
                                 sb_q[eb:eb + 64, lofs + 1024 + ech * 128:
                                      lofs + 1024 + (ech + 1) * 128],
                                 sb_q[eb:eb + 64, lofs + 1536 + ech * 128:
                                      lofs + 1536 + (ech + 1) * 128],
                                 sb_p[eb:eb + 64, 4096 + ech * 512:
                                      4096 + (ech + 1) * 512],
                                 sb_p[eb:eb + 64, 6144 + ech * 512:
                                      6144 + (ech + 1) * 512],
                                 first=True, last=True, inc="mm")
                        P.wait("vector", "mm", mm)
                        ve = P.emit("vector", lambda E: E.tensor_tensor(
                            sb_sc2[:, 1024:1536], sb_rp[:, 512:1024],
                            pb6[0:128, 0:512], OP.add))
                        P.wait("tensor", "sf_vector", ve)
                        P.wait("vector", "sf_scalar", se)
                        P.emit("vector", lambda E: E.tensor_tensor(
                            sb_sc2[:, 1536:2048], sb_sc2[:, 512:1024],
                            sb_sc2[:, 1024:1536], OP.mult))
                        so = whr.acquire("vector")
                        whs = svf[:, so * 512:(so + 1) * 512]
                        ve = P.emit("vector", lambda E, w=whs, e_=e_:
                                    E.tensor_tensor(
                                        w, sb_sc2[:, 1536:2048],
                                        sb_sc[:, e_ * 512:(e_ + 1) * 512],
                                        OP.mult))
                        P.wait("gpsimd", "sf_vector", ve)
                        if e_ == 0:
                            ge = P.emit("gpsimd", lambda E, w=whs:
                                        E.tensor_copy(sb_sm[:, :], w))
                        else:
                            ge = P.emit("gpsimd", lambda E, w=whs:
                                        E.tensor_tensor(sb_sm[:, :],
                                                        sb_sm[:, :], w, OP.add))
                        P.wait("tensor", "sf_gpsimd", ge)
                        mm = P.emit("tensor", lambda E, e_=e_, eb=eb, w=whs,
                                    lofs=lofs:
                                    E.matmul(pb[2 + e_ // 2][eb:eb + 64, :],
                                             sqf[:, (lofs + 2048) // 2 + e_ * 64:
                                                 (lofs + 2048) // 2 + (e_ + 1) * 64],
                                             w, start=False, stop=False),
                                    inc="mm")
                        whr.consumed(so, "mm", mm)
                        whr.consumed(so, "sf_gpsimd", ge)
                    lorar.consumed(lh, "mm", P.val("mm"))
                    P.wait("scalar", "sf_gpsimd", P.val("sf_gpsimd"))
                    so = hxr.acquire("scalar")
                    hof = 1024 + so * 1024
                    se = P.emit("scalar", lambda E, hof=hof:
                                E.copy(sb_fm[:, hof:hof + 512], sb_sm[:, :]))
                    P.wait("vector", "sf_scalar", se)
                    ve = P.emit("vector", lambda E, hof=hof:
                                E.scalar_tensor_tensor(
                                    sb_fm[:, hof + 512:hof + 1024],
                                    sb_fm[:, hof:hof + 512], -1.0,
                                    sb_sm[:, :], OP.mult, OP.add))
                        
                    P.wait("sync", "sf_vector", ve)
                    hxr.dma("sync", so, hmix_d[tg, ic, :, :],
                            sb_fm[:, hof:hof + 1024])
                    P.wait("gpsimd", "sf_vector", ve)
                t_release()
                mmz = None
                for pr in range(4):
                    mmz = P.emit("tensor", lambda E, pr=pr: E.matmul(
                        pb[2 + pr][:, :], sb_or[0:1, 0:128],
                        sb_zf[0:1, 0:512], start=False, stop=True), inc="mm")
                P.wait("vector", "mm", mmz)
                P.wait("vector", "sf_gpsimd", P.val("sf_gpsimd"))
                for pr in range(4):
                    ve = P.emit("vector", lambda E, pr=pr:
                                E.tensor_scalar_mul(
                                    svf[:, pr * 512:(pr + 1) * 512],
                                    pb[2 + pr][:, :], SCALING))
                    P.wait("sync", "sf_vector", ve)
                    st = P.dma("sync", z_d[tg, :, pr * 512:(pr + 1) * 512],
                               svf[:, pr * 512:(pr + 1) * 512], "st")
                    P.wait("vector", "st", st)
                    P.wait("tensor", "sf_vector", ve)

        def p_ffn_down(src_ap, dst_ap):
            banks = list(pb) + [pb6, pb7]
            hr = Ring(P, 3)
            br = Ring(P, 2)
            for tg in range(NTG):
                P.wait("sync", "mm", P.val("mm"))
                lz = P.dma("sync", slwf[:, 0:2048], z_d[tg, :, :], "ld")
                for dh in range(2):
                    P.wait("sync", "mm", P.val("mm"))
                    ldb = None
                    for c in range(4):
                        ldb = P.dma("sync", sb_s[:, c * 1024:(c + 1) * 1024],
                                    dB[c * 128:(c + 1) * 128,
                                       dh * 1024:dh * 1024 + 1024], "ld")
                    lastmm = None
                    for ic in range(NIC):
                        hs = hr.acquire("sync")
                        l1 = P.dma("sync", sb_wst[:, hs * 1024:hs * 1024 + 1024],
                                   hmix_d[tg, ic, :, :], "ld")
                        bs = br.acquire("sync")
                        l2 = P.dma("sync", sb_ws2[:, bs * 2048:bs * 2048 + 1024],
                                   bdown_h[ic * 128:(ic + 1) * 128,
                                           dh * 1024:dh * 1024 + 1024], "ld")
                        l2 = P.dma("sync",
                                   sb_ws2[:, bs * 2048 + 1024:bs * 2048 + 2048],
                                   bdown_l[ic * 128:(ic + 1) * 128,
                                           dh * 1024:dh * 1024 + 1024], "ld")
                        hop(P, "sync", "tensor",
                            [("ld", max(l1, l2, lz, ldb))])
                        mm = None
                        for til in range(4):
                            for dc in range(2):
                                mm = mm3(banks[til * 2 + dc][:, :],
                                         sb_wst[:, hs * 1024 + til * 128:
                                                hs * 1024 + (til + 1) * 128],
                                         sb_wst[:, hs * 1024 + 512 + til * 128:
                                                hs * 1024 + 512 + (til + 1) * 128],
                                         sb_ws2[:, bs * 2048 + dc * 512:
                                                bs * 2048 + (dc + 1) * 512],
                                         sb_ws2[:, bs * 2048 + 1024 + dc * 512:
                                                bs * 2048 + 1024 + (dc + 1) * 512],
                                         first=(ic == 0), last=False,
                                         inc="mm" if (til == 3 and dc == 1)
                                         else None)
                        hr.consumed(hs, "mm", mm)
                        br.consumed(bs, "mm", mm)
                        lastmm = mm
                    for e_ in range(NE):
                        eb = (e_ % 2) * 64
                        ech = e_ // 2
                        for til in range(4):
                            for dc in range(2):
                                lastf = (e_ == NE - 1)
                                lastmm = P.emit(
                                    "tensor", lambda E, eb=eb, ech=ech,
                                    til=til, dc=dc, last=lastf:
                                    E.matmul(
                                        banks[til * 2 + dc][:, :],
                                        slwf[eb:eb + 64,
                                             ech * 512 + til * 128:
                                             ech * 512 + (til + 1) * 128],
                                        sb_s[eb:eb + 64,
                                             ech * 1024 + dc * 512:
                                             ech * 1024 + dc * 512 + 512],
                                        start=False, stop=last),
                                    inc="mm" if (e_ == NE - 1 and til == 3 and
                                                 dc == 1) else None)
                    P.wait("vector", "mm", P.val("mm"))
                    for til in range(4):
                        t0 = tg * 512 + til * 128
                        lo = P.dma("sync", sb_xt[:, 0:1024],
                                   src_ap[t0:t0 + 128,
                                          dh * 1024:dh * 1024 + 1024], "ld")
                        P.wait("vector", "ld", lo)
                        ve = None
                        for dc in range(2):
                            ve = P.emit("vector", lambda E, til=til, dc=dc:
                                        E.tensor_tensor(
                                            sb_sc[:, til * 1024 + dc * 512:
                                                  til * 1024 + (dc + 1) * 512],
                                            banks[til * 2 + dc][:, :],
                                            sb_xt[:, dc * 512:(dc + 1) * 512],
                                            OP.add))
                        P.wait("sync", "sf_vector", ve)
                        st = P.dma("sync",
                                   dst_ap[t0:t0 + 128,
                                          dh * 1024:dh * 1024 + 1024],
                                   sb_sc[:, til * 1024:til * 1024 + 1024], "st")
                        P.wait("vector", "st", st)
                    P.wait("tensor", "sf_vector", P.val("sf_vector"))

        pcount = [0]

        def phase(fn, *a, **kw):
            pcount[0] += 1
            if pcount[0] > PHASE_LIMIT:
                return
            fn(*a, **kw)
            P.barrier()

        phase(p_consts)
        phase(p_cast_weights)

        def sub_attn_a(src, dst):
            phase(p_ln_fm, src, xn_ci)
            phase(p_coll, "AllGather", xn_ci, xn_co)
            phase(p_attn_a_qkv)
            phase(p_attn_heads, "a")
            phase(p_partial_proj, wo)
            phase(p_coll, "ReduceScatter", pp_ci, pp_co)
            phase(p_residual, src, dst, False)

        def sub_ffn(src, dst):
            phase(p_ln_fm, src, xnfm_d)
            phase(p_routing)
            phase(p_ffn_experts)
            phase(p_ffn_down, src, dst)

        def sub_attn_b(src, dst):
            phase(p_ln_fm, src, xn_ci)
            phase(p_attn_b_h)
            phase(p_coll, "AllGather", h_ci, h_co)
            phase(p_attn_b_proj)
            phase(p_attn_heads, "b")
            phase(p_partial_proj, wz)
            phase(p_coll, "ReduceScatter", pp_ci, pp_co)
            phase(p_residual, src, dst, True)

        subs = [
            lambda d: sub_attn_a(x0, d),
            lambda d: sub_ffn(xres, d),
            lambda d: sub_attn_b(xres, d),
            lambda d: sub_ffn(xres, d),
        ]
        for i in range(N_SUB):
            dst = yout if i == N_SUB - 1 else xres
            subs[i](dst)

        P.wait("gpsimd", "st", P.val("st"))
        P.wait("gpsimd", "ld", P.val("ld"))
        for i in range(24):
            P.wait("gpsimd", f"r{i}", P.val(f"r{i}"))

        with nc.Block() as block:
            for ename in ENGINES:
                def make(ename):
                    def body(E):
                        for fn in P.ops[ename]:
                            fn(E)
                    return body
                getattr(block, ename)(make(ename))

    return nc



def _rope_tables(t, d):
    freq = np.arange(1, t + 1, dtype=np.float64)[:, None]
    pos = np.repeat(np.arange(d // 2, dtype=np.float64), 2)[None]
    theta = np.exp(-pos / d * math.log(10000.0))
    ang = freq * theta
    return np.cos(ang).astype(np.float32), np.sin(ang).astype(np.float32)


def _swap_pairs(a, axis):
    idx = np.arange(a.shape[axis])
    idx = idx.reshape(-1, 2)[:, ::-1].reshape(-1)
    return np.take(a, idx, axis=axis)


def kernel(**inputs):
    dm = _dims()
    HPC, TOK = dm["HPC"], dm["TOK"]
    f32 = np.float32

    Wqkv = np.asarray(inputs["L0_attn_a_W_qkv"], f32)
    Wo = np.asarray(inputs["L0_attn_a_W_o"], f32)
    x_in = np.asarray(inputs["L0_attn_a_x"], f32)
    WA = np.asarray(inputs["L0_attn_b_WA"], f32)
    bA = np.asarray(inputs["L0_attn_b_bA"], f32)
    WB = np.asarray(inputs["L0_attn_b_WB"], f32)
    bB = np.asarray(inputs["L0_attn_b_bB"], f32)
    WC = np.asarray(inputs["L0_attn_b_WC"], f32)
    bC = np.asarray(inputs["L0_attn_b_bC"], f32)
    WUQ = np.asarray(inputs["L0_attn_b_WUQ"], f32)
    WUK = np.asarray(inputs["L0_attn_b_WUK"], f32)
    WUV = np.asarray(inputs["L0_attn_b_WUV"], f32)
    WZ = np.asarray(inputs["L0_attn_b_WZ"], f32)
    bZ = np.asarray(inputs["L0_attn_b_bZ"], f32)
    wealth = np.asarray(inputs["L0_ffn_expert_wealth"], f32)
    confW = np.asarray(inputs["L0_ffn_conf_W"], f32)
    confb = np.asarray(inputs["L0_ffn_conf_b"], f32)
    bgate = np.asarray(inputs["L0_ffn_base_gate"], f32)
    bup = np.asarray(inputs["L0_ffn_base_up"], f32)
    bdown = np.asarray(inputs["L0_ffn_base_down"], f32)
    gateA = np.asarray(inputs["L0_ffn_gate_A"], f32)
    gateB = np.asarray(inputs["L0_ffn_gate_B"], f32)
    upA = np.asarray(inputs["L0_ffn_up_A"], f32)
    upB = np.asarray(inputs["L0_ffn_up_B"], f32)
    downA = np.asarray(inputs["L0_ffn_down_A"], f32)
    downB = np.asarray(inputs["L0_ffn_down_B"], f32)

    C = lambda a: np.ascontiguousarray(a, dtype=f32)
    identb = np.eye(128, dtype=BFD)
    identf = np.eye(128, dtype=f32)
    onesf = np.ones((1, 128), f32)
    diagm = np.where(np.arange(128)[None, :] > np.arange(128)[:, None],
                     NEG, 0.0).astype(f32)
    cos_k, sin_k = _rope_tables(T, DHR)
    cos_q, sin_q = _rope_tables(T, 2 * DL)
    sgn_k = np.where(np.arange(DHR) % 2 == 0, -1.0, 1.0).astype(f32)
    sgn_q = np.where(np.arange(2 * DL) % 2 == 0, -1.0, 1.0).astype(f32)
    sin_k = sin_k * sgn_k[None, :]
    sin_q = sin_q * sgn_q[None, :]

    slopes = 2.0 ** (-8.0 / np.arange(1, NH + 1))
    ii = np.arange(128)[:, None]
    jj = np.arange(512)[None, :]

    in_maps = []
    for c in range(8):
        b, g = c // 4, c % 4
        heads = slopes[g * HPC:(g + 1) * HPC]
        al5 = (heads[:, None, None] * (jj - ii)[None]).astype(f32)
        jd = jj[:, :128]
        ald = (heads[:, None, None] * (jd - ii)[None]).astype(f32)
        ald = ald + np.where(jd > ii, NEG, 0.0)[None]
        ald = ald.astype(f32)
        alc = np.zeros((128, HPC * 64), f32)
        for hh in range(HPC):
            alc[:, hh * 64:(hh + 1) * 64] = (
                -heads[hh] * 128.0 * np.arange(64, dtype=f32))[None, :]
        wcs_full = _swap_pairs(WC, axis=1)
        m = {
            "x0": C(x_in[b, g * TOK:(g + 1) * TOK, :]),
            "wq": C(Wqkv[:, g * 512:(g + 1) * 512]),
            "wk": C(Wqkv[:, D + g * 512:D + (g + 1) * 512]),
            "wv": C(Wqkv[:, 2 * D + g * 512:2 * D + (g + 1) * 512]),
            "wo": C(Wo[g * 512:(g + 1) * 512, :]),
            "wa": C(WA), "ba": C(bA.reshape(8, 128).T),
            "wb": C(WB), "wbs": C(_swap_pairs(WB, axis=1)),
            "bb": C(bB.reshape(DHR, 1)),
            "bbs": C(_swap_pairs(bB, axis=0).reshape(DHR, 1)),
            "wc": C(WC[:, g * 256:(g + 1) * 256]),
            "wcs": C(wcs_full[:, g * 256:(g + 1) * 256]),
            "bc": C(bC[g * 256:(g + 1) * 256].reshape(2, 128).T),
            "bcs": C(_swap_pairs(bC, axis=0)[g * 256:(g + 1) * 256].reshape(2, 128).T),
            "wuq": C(WUQ[:, g * 512:(g + 1) * 512]),
            "wuk": C(WUK[:, g * 512:(g + 1) * 512]),
            "wuv": C(WUV[:, g * 512:(g + 1) * 512]),
            "wz": C(WZ[g * 512:(g + 1) * 512, :]),
            "bzrep": C(np.broadcast_to(bZ[None, :], (128, D))),
            "wealth": C(wealth.reshape(NE, 1)),
            "confw": C(confW.T), "confb": C(confb.reshape(NE, 1)),
            "bgate": C(bgate.reshape(D // 128, 128, DI // 128, 128)
                      .transpose(2, 1, 0, 3).reshape(DI // 128 * 128, D)),
            "bup": C(bup.reshape(D // 128, 128, DI // 128, 128)
                     .transpose(2, 1, 0, 3).reshape(DI // 128 * 128, D)),
            "bdown": C(bdown),
            "gA": C(gateA.transpose(1, 0, 2).reshape(D, NE * RK)),
            "uA": C(upA.transpose(1, 0, 2).reshape(D, NE * RK)),
            "gB": C(gateB.reshape(NE // 2, 2, RK, DI // 128, 128)
                    .transpose(3, 1, 2, 0, 4).reshape(DI // 128 * 128,
                                                      NE // 2 * 128)),
            "uB": C(upB.reshape(NE // 2, 2, RK, DI // 128, 128)
                    .transpose(3, 1, 2, 0, 4).reshape(DI // 128 * 128,
                                                      NE // 2 * 128)),
            "dA": C(downA.transpose(1, 0, 2).reshape(DI, NE * RK)),
            "dB": C(downB.reshape(NE * RK, D)),
            "identb": identb, "identf": identf, "onesf": onesf,
            "alibi5": al5, "alibid": ald, "alibic": alc, "diagm": diagm,
            "coskr": C(cos_k.T), "sinkr": C(sin_k.T),
            "cosqr": C(cos_q.T[g * 256:(g + 1) * 256, :]),
            "sinqr": C(sin_q.T[g * 256:(g + 1) * 256, :]),
        }
        in_maps.append(m)

    nc = build_nc()
    res = run_bass_kernel_spmd(nc, in_maps, core_ids=list(range(8)))
    out = np.empty((B, T, D), dtype=f32)
    for c in range(8):
        b, g = c // 4, c % 4
        out[b, g * TOK:(g + 1) * TOK, :] = res.results[c]["y"]
    return out


# version 07 (final)
# speedup vs baseline: 1.0127x; 1.0015x over previous
import math
from contextlib import ExitStack

import numpy as np
import ml_dtypes
import concourse.bass as bass
import concourse.mybir as mybir
from concourse.bass_utils import run_bass_kernel_spmd

F32 = mybir.dt.float32
BF16 = mybir.dt.bfloat16
AF = mybir.ActivationFunctionType
OP = mybir.AluOpType
AXX = mybir.AxisListType.X

B, T, D = 2, 8192, 2048
NH, DH = 16, 128
GRP = 4
DI = 7168
NE, RK = 8, 64
DL = 512
DHR = 64
SCALING = 16.0 / 64.0
NEG = -30000.0
N_SUB = 4
PHASE_LIMIT = 999

BFD = np.dtype(ml_dtypes.bfloat16)
ENGINES = ("sync", "vector", "scalar", "tensor", "gpsimd")
SELF_SYNC = ("vector", "scalar", "gpsimd")


def _dims():
    HPC = NH // GRP
    TOK = T // GRP
    return dict(HPC=HPC, TOK=TOK, NTIL=TOK // 128, NTG=TOK // 512,
                FT=T // 512, NIC=DI // 128, NQB=T // 128)


class Prog:
    def __init__(self):
        self.ops = {e: [] for e in ENGINES}
        self.sem = {}
        self.handles = None
        self.last = {e: None for e in ENGINES}

    def new_sem(self, name):
        self.sem[name] = 0

    def emit(self, eng, fn, inc=None, by=1):
        if eng in SELF_SYNC:
            if self.last[eng] is not None:
                self.wait(eng, *self.last[eng])
            if inc is None:
                inc, by = "sf_" + eng, 1
        if inc is not None:
            self.sem[inc] += by

            def wrapped(E, fn=fn, s=inc, b=by):
                return fn(E).then_inc(self.handles[s], b)
            self.ops[eng].append(wrapped)
            if eng in SELF_SYNC:
                self.last[eng] = (inc, self.sem[inc])
            return self.sem[inc]
        self.ops[eng].append(fn)
        return None

    def wait(self, eng, sem, val):
        if val is None:
            return
        self.ops[eng].append(
            lambda E, s=sem, v=val: E.wait_ge(self.handles[s], v))

    def val(self, sem):
        return self.sem[sem]

    def dma(self, eng, out, in_, inc="st"):
        self.sem[inc] += 16

        def fn(E, o=out, i=in_, s=inc):
            return E.dma_start(out=o, in_=i).then_inc(self.handles[s], 16)
        self.ops[eng].append(fn)
        return self.sem[inc]

    def barrier(self):
        for s, v in self.sem.items():
            if s != "bar" and v > 0:
                self.wait("sync", s, v)
        self.sem["bar"] += 5
        tgt = self.sem["bar"]
        for e in ENGINES:
            def fin(E):
                n = E.engine_nop() if hasattr(E, "engine_nop") else E.nop()
                return n.then_inc(self.handles["bar"], 1)
            self.ops[e].append(fin)
        for e in ENGINES:
            self.wait(e, "bar", tgt)


class Ring:

    def __init__(self, P, nslots, dma_sems=None):
        self.P = P
        self.n = nslots
        self.tok = [[] for _ in range(nslots)]
        self.i = 0
        self.dma_sems = dma_sems
        if dma_sems is not None:
            assert len(dma_sems) == nslots

    def acquire(self, eng):
        s = self.i % self.n
        self.i += 1
        for sem, val in self.tok[s]:
            self.P.wait(eng, sem, val)
        self.tok[s] = []
        if self.dma_sems is not None:
            self.P.wait(eng, self.dma_sems[s], self.P.val(self.dma_sems[s]))
        return s

    def dma(self, eng, s, out, in_):
        return self.P.dma(eng, out, in_, self.dma_sems[s])

    def consumed(self, s, sem, val):
        self.tok[s].append((sem, val))


def hop(P, src, dst, vals):
    for sem, val in vals:
        P.wait(src, sem, val)
    P.sem["hs"] += 1
    tgt = P.sem["hs"]

    def fin(E):
        n = E.engine_nop() if hasattr(E, "engine_nop") else E.nop()
        return n.then_inc(P.handles["hs"], 1)
    P.ops[src].append(fin)
    P.wait(dst, "hs", tgt)


def k_tiles_for_qblock(qb):
    q0 = qb * 128
    tiles = [(i * 512, 512, False) for i in range(q0 // 512)]
    base = (q0 // 512) * 512
    for j in range((q0 - base) // 128):
        tiles.append((base + j * 128, 128, False))
    tiles.append((q0, 128, True))
    return tiles


def build_nc():
    dm = _dims()
    HPC, TOK, NTIL, NTG, FT = dm["HPC"], dm["TOK"], dm["NTIL"], dm["NTG"], dm["FT"]
    NIC, NQB = dm["NIC"], dm["NQB"]
    nc = bass.Bass("TRN2", target_bir_lowering=False)

    def inp(name, shape, dtype=F32):
        return nc.dram_tensor(name, shape, dtype, kind="ExternalInput").ap()

    x0 = inp("x0", [TOK, D])
    wq = inp("wq", [D, HPC * DH]); wk = inp("wk", [D, HPC * DH])
    wv = inp("wv", [D, HPC * DH])
    wo = inp("wo", [HPC * DH, D])
    wa = inp("wa", [D, 2 * DL]); ba = inp("ba", [128, 8])
    wb = inp("wb", [2 * DL, DHR]); wbs = inp("wbs", [2 * DL, DHR])
    bb = inp("bb", [DHR, 1]); bbs = inp("bbs", [DHR, 1])
    wc = inp("wc", [2 * DL, HPC * DHR]); wcs = inp("wcs", [2 * DL, HPC * DHR])
    bc = inp("bc", [128, 2]); bcs = inp("bcs", [128, 2])
    wuq = inp("wuq", [DL, HPC * DH]); wuk = inp("wuk", [DL, HPC * DH])
    wuv = inp("wuv", [DL, HPC * DH])
    wz = inp("wz", [HPC * DH, D])
    bzrep = inp("bzrep", [128, D])
    wealth = inp("wealth", [NE, 1]); confw = inp("confw", [D, NE])
    confb = inp("confb", [NE, 1])
    bgate = inp("bgate", [NIC * 128, D]); bup = inp("bup", [NIC * 128, D])
    bdown = inp("bdown", [DI, D])
    gA = inp("gA", [D, NE * RK]); uA = inp("uA", [D, NE * RK])
    gB = inp("gB", [NIC * 128, (NE // 2) * 128])
    uB = inp("uB", [NIC * 128, (NE // 2) * 128])
    dA = inp("dA", [DI, NE * RK]); dB = inp("dB", [NE * RK, D])
    identb = inp("identb", [128, 128], BF16)
    identf = inp("identf", [128, 128])
    onesf = inp("onesf", [1, 128])
    alibi5 = inp("alibi5", [HPC, 128, 512])
    alibid = inp("alibid", [HPC, 128, 128])
    alibic = inp("alibic", [128, HPC * 64])
    diagm = inp("diagm", [128, 128])
    coskr = inp("coskr", [DHR, T]); sinkr = inp("sinkr", [DHR, T])
    cosqr = inp("cosqr", [HPC * DHR, T]); sinqr = inp("sinqr", [HPC * DHR, T])

    yout = nc.dram_tensor("y", [TOK, D], F32, kind="ExternalOutput").ap()

    def scrt(name, shape, dtype=BF16):
        return nc.dram_tensor(name, shape, dtype).ap()

    xres = scrt("xres", [TOK, D], F32)
    wpairs = {}
    for nm, w in [("wq", wq), ("wk", wk), ("wv", wv), ("wa", wa),
                  ("bgate", bgate), ("bup", bup), ("bdown", bdown),
                  ("gA", gA), ("uA", uA), ("gB", gB), ("uB", uB),
                  ("confw", confw)]:
        wpairs[nm] = (w, scrt(nm + "_h", list(w.shape)),
                      scrt(nm + "_l", list(w.shape)))
    wq_h, wq_l = wpairs["wq"][1:]
    wk_h, wk_l = wpairs["wk"][1:]
    wv_h, wv_l = wpairs["wv"][1:]
    wa_h, wa_l = wpairs["wa"][1:]
    bgate_h, bgate_l = wpairs["bgate"][1:]
    bup_h, bup_l = wpairs["bup"][1:]
    bdown_h, bdown_l = wpairs["bdown"][1:]
    gA_h, gA_l = wpairs["gA"][1:]
    uA_h, uA_l = wpairs["uA"][1:]
    gB_h, gB_l = wpairs["gB"][1:]
    uB_h, uB_l = wpairs["uB"][1:]
    confw_h, confw_l = wpairs["confw"][1:]
    xn_ci = scrt("xn_ci", [2 * D, TOK]); xn_co = scrt("xn_co", [GRP * 2 * D, TOK])
    h_ci = scrt("h_ci", [2 * DL, TOK], F32)
    h_co = scrt("h_co", [GRP * 2 * DL, TOK], F32)
    pp_ci = scrt("pp_ci", [T, D], F32); pp_co = scrt("pp_co", [TOK, D], F32)
    qh_d = scrt("qh_d", [HPC, DH, T], F32); kh_d = scrt("kh_d", [HPC, DH, T], F32)
    vt_d = scrt("vt_d", [HPC, T, DH], F32)
    qr_d = scrt("qr_d", [HPC * DHR, T], F32); kr_d = scrt("kr_d", [DHR, T], F32)
    yfm_d = scrt("yfm_d", [HPC * DH, T], F32)
    xnfm_d = scrt("xnfm_d", [2 * D, TOK])
    wt_d = scrt("wt_d", [NE, TOK], F32)
    hmix_d = scrt("hmix_d", [NTG, NIC, 128, 1024])
    z_d = scrt("z_d", [NTG, 128, (NE // 2) * 512], F32)

    P = Prog()
    for s in ["ld", "lds", "st", "bar", "hs", "mm", "ve", "se", "ge", "cc",
              "sf_vector", "sf_scalar", "sf_gpsimd"]:
        P.new_sem(s)
    for i in range(24):
        P.new_sem(f"r{i}")

    with ExitStack() as stack:
        en = stack.enter_context
        sb_s = en(nc.sbuf_tensor("sb_s", [128, 8192], F32))
        sb_q = en(nc.sbuf_tensor("sb_q", [128, 8192], BF16))
        sb_k = en(nc.sbuf_tensor("sb_k", [128, 8192], BF16))
        sb_v = en(nc.sbuf_tensor("sb_v", [128, 8192], BF16))
        sb_p = en(nc.sbuf_tensor("sb_p", [128, 8192], BF16))
        sb_pt = en(nc.sbuf_tensor("sb_pt", [128, 2048], BF16))
        sb_zb = en(nc.sbuf_tensor("sb_zb", [128, 4096], BF16))
        sb_xt = en(nc.sbuf_tensor("sb_xt", [128, 2048], F32))
        sb_xn = en(nc.sbuf_tensor("sb_xn", [128, 2048], F32))
        sb_fm = en(nc.sbuf_tensor("sb_fm", [128, 4096], BF16))
        sb_wst = en(nc.sbuf_tensor("sb_wst", [128, 4096], BF16))
        sb_ws2 = en(nc.sbuf_tensor("sb_ws2", [128, 4096], BF16))
        sb_lw = en(nc.sbuf_tensor("sb_lw", [128, 4096], BF16))
        sb_sc = en(nc.sbuf_tensor("sb_sc", [128, 4096], F32))
        sb_sc2 = en(nc.sbuf_tensor("sb_sc2", [128, 2048], F32))
        sb_sm = en(nc.sbuf_tensor("sb_sm", [128, 512], F32))
        sb_st = en(nc.sbuf_tensor("sb_st", [128, 160], F32))
        sb_cb = en(nc.sbuf_tensor("sb_cb", [128, 128], BF16))
        sb_zr = en(nc.sbuf_tensor("sb_zr", [1, 640], BF16))
        sb_or = en(nc.sbuf_tensor("sb_or", [1, 128], F32))
        sb_zf = en(nc.sbuf_tensor("sb_zf", [1, 512], F32))
        sb_cf = en(nc.sbuf_tensor("sb_cf", [128, 128], F32))
        sb_rp = en(nc.sbuf_tensor("sb_rp", [128, 1024], F32))
        sb_bi = en(nc.sbuf_tensor("sb_bi", [128, 20], F32))
        for cval, ccol in ((1e-5, 16), (-0.5, 17), (SCALING, 18)):
            nc.const_aps.register(F32, cval, sb_bi[:, ccol:ccol + 1])
        pb = [en(nc.psum_tensor(f"pb{i}", [128, 512], F32)) for i in range(6)]
        ptbs = [en(nc.psum_tensor(f"ptb{i}", [128, 1024], BF16))
                for i in range(2)]
        pb6 = ptbs[0][:, :].bitcast(F32)
        pb7 = ptbs[1][:, :].bitcast(F32)
        ptf = [ptbs[0][:, :].bitcast(F32), ptbs[1][:, :].bitcast(F32)]

        sems = {s: en(nc.semaphore(s)) for s in P.sem}
        P.handles = sems

        sbf = sb_s[:, :].bitcast(BF16)
        sqf = sb_q[:, :].bitcast(F32)
        skf = sb_k[:, :].bitcast(F32)
        svf = sb_v[:, :].bitcast(F32)
        spf = sb_p[:, :].bitcast(F32)
        sptf = sb_pt[:, :].bitcast(F32)
        swstf = sb_wst[:, :].bitcast(F32)
        sws2f = sb_ws2[:, :].bitcast(F32)
        slwf = sb_lw[:, :].bitcast(F32)
        szbf = sb_zb[:, :].bitcast(F32)
        sfmf = sb_fm[:, :].bitcast(F32)
        svbf = sb_v[:, :]

        sb_co = sb_sc
        AL5, ALD = 0, HPC * 512
        DGM = ALD + HPC * 128
        ALC = DGM + 128
        QRING = 2944
        RRING = 3200
        assert RRING + 512 <= 4096
        BA, BB_, BBS, BC_, BCS, CB_, WE_ = 0, 8, 9, 10, 12, 14, 15

        fm_ring = Ring(P, 8, dma_sems=[f"r{i}" for i in range(8)])
        f32_ring = Ring(P, 4, dma_sems=[f"r{i}" for i in range(8, 12)])
        tr_state = {"i": 0, "tok": [None, None]}

        def t_evac(src, dst, n=128, f32=False):
            bk = tr_state["i"] % 2
            off = ((tr_state["i"] // 2) % (4 if f32 else 4)) * 128
            tr_state["i"] += 1
            if tr_state["tok"][bk] is not None:
                P.wait("tensor", "ve", tr_state["tok"][bk])
            if f32:
                dst_ps = ptf[bk][:, off:off + n]
                ident = sb_cf[0:src.shape[0], 0:src.shape[0]]
            else:
                dst_ps = ptbs[bk][:, off:off + n]
                ident = sb_cb[0:src.shape[0], 0:src.shape[0]]
            mm = P.emit("tensor", lambda E, d=dst_ps, a=src, i=ident:
                        E.transpose(d, a, i), inc="mm")
            P.wait("vector", "mm", mm)
            if f32:
                src_ps = ptf[bk][0:dst.shape[0], off:off + n]
            else:
                src_ps = ptbs[bk][0:dst.shape[0], off:off + n]
            ve = P.emit("vector", lambda E, d=dst, s=src_ps:
                        E.tensor_copy(d, s), inc="ve")
            tr_state["tok"][bk] = ve
            return ve

        def t_guard():
            P.wait("tensor", "ve", P.val("ve"))
            tr_state["tok"][0] = None
            tr_state["tok"][1] = None

        def t_release():
            tr_state["tok"][0] = P.val("ve")
            tr_state["tok"][1] = P.val("ve")

        def mm3(bank_ap, lh, ll, rh, rl, first, last, inc=None):
            P.emit("tensor", lambda E: E.matmul(bank_ap, lh, rh,
                                                start=first, stop=False))
            P.emit("tensor", lambda E: E.matmul(bank_ap, lh, rl,
                                                start=False, stop=False))
            return P.emit("tensor", lambda E: E.matmul(
                bank_ap, ll, rh, start=False, stop=last), inc=inc)

        def p_consts():
            P.emit("vector", lambda E: E.memset(sb_zr[:, :], 0.0))
            P.emit("vector", lambda E: E.memset(sb_zf[:, :], 0.0))
            for cval, ccol in ((1e-5, 16), (-0.5, 17), (SCALING, 18)):
                P.emit("vector", lambda E, v=cval, c=ccol:
                       E.memset(sb_bi[:, c:c + 1], v))
            P.dma("sync", sb_cb[:, 0:128], identb, "ld")
            P.dma("sync", sb_cf[:, 0:128], identf, "ld")
            P.dma("sync", sb_bi[:, BA:BA + 8], ba, "ld")
            P.dma("sync", sb_bi[0:DHR, BB_:BB_ + 1], bb, "ld")
            P.dma("sync", sb_bi[0:DHR, BBS:BBS + 1], bbs, "ld")
            P.dma("sync", sb_bi[:, BC_:BC_ + 2], bc, "ld")
            P.dma("sync", sb_bi[:, BCS:BCS + 2], bcs, "ld")
            P.dma("sync", sb_bi[0:NE, CB_:CB_ + 1], confb, "ld")
            P.dma("sync", sb_bi[0:NE, WE_:WE_ + 1], wealth, "ld")
            P.dma("sync", sb_or[0:1, 0:128], onesf, "ld")

        def p_cast_weights():
            ldr = Ring(P, 2)
            stw = Ring(P, 2, dma_sems=["r12", "r13"])
            low = Ring(P, 2, dma_sems=["r14", "r15"])
            for nm, (w32, whi, wlo) in wpairs.items():
                rows, cols = w32.shape
                assert cols <= 4096, (nm, cols)
                for r0 in range(0, rows, 128):
                    rr = min(128, rows - r0)
                    sl = ldr.acquire("sync")
                    src = sb_s[0:rr, sl * 4096:sl * 4096 + cols]
                    ld = P.dma("sync", src, w32[r0:r0 + rr, :], "ld")
                    P.wait("scalar", "ld", ld)
                    so = stw.acquire("scalar")
                    hdst = sb_zb[0:rr, so * 2048:so * 2048 + cols]
                    se = P.emit("scalar", lambda E, d=hdst, s=src: E.copy(d, s))
                    P.wait("sync", "se" if False else "sf_scalar", se)
                    stw.dma("sync", so, whi[r0:r0 + rr, :], hdst)
                    lso = low.acquire("vector")
                    ldst = sb_p[0:rr, lso * 2048:lso * 2048 + cols]
                    P.wait("vector", "sf_scalar", se)
                    ve = P.emit("vector", lambda E, d=ldst, h=hdst, s=src:
                                E.scalar_tensor_tensor(
                                    d, h, -1.0, s, OP.mult, OP.add))
                    ldr.consumed(sl, "sf_vector", ve)
                    stw.consumed(so, "sf_vector", ve)
                    P.wait("sync", "sf_vector", ve)
                    low.dma("sync", lso, wlo[r0:r0 + rr, :], ldst)

        def p_ln_fm(src_ap, fm_dram):
            for tile in range(NTIL):
                t0 = tile * 128
                xt = sb_xt[:, :]
                ld = P.dma("sync", xt, src_ap[t0:t0 + 128, :], "ld")
                P.wait("vector", "ld", ld)
                P.wait("scalar", "ld", ld)
                mcol = sb_st[:, 64:65]
                vcol = sb_st[:, 65:66]
                rcol = sb_st[:, 66:67]
                se = P.emit("scalar", lambda E: E.activation(
                    sb_xn[:, :], xt, AF.Square, accum_out=vcol))
                P.emit("vector", lambda E: E.reduce_sum(mcol, xt, axis=AXX))
                P.emit("vector",
                       lambda E: E.tensor_scalar_mul(mcol, mcol, 1.0 / D))
                P.wait("vector", "sf_scalar", se)
                P.emit("vector",
                       lambda E: E.tensor_scalar_mul(vcol, vcol, 1.0 / D))
                P.emit("vector",
                       lambda E: E.tensor_tensor(rcol, mcol, mcol, OP.mult))
                ve = P.emit("vector", lambda E: E.tensor_tensor(
                    vcol, vcol, rcol, OP.subtract))
                P.wait("scalar", "sf_vector", ve)
                P.emit("scalar",
                       lambda E: E.activation(rcol, vcol, AF.Ln, bias=1e-5))
                se = P.emit("scalar", lambda E: E.activation(
                    rcol, rcol, AF.Exp, scale=-0.5))
                P.wait("vector", "sf_scalar", se)
                P.emit("vector", lambda E: E.tensor_scalar(
                    sb_xn[:, :], xt, mcol, rcol, OP.subtract, OP.mult))
                for half in range(2):
                    if half == 0:
                        ve = P.emit("vector", lambda E: E.tensor_copy(
                            sb_fm[:, 0:2048], sb_xn[:, :]))
                    else:
                        ve = P.emit("vector", lambda E: E.scalar_tensor_tensor(
                            sb_fm[:, 0:2048], sb_fm[:, 0:2048], -1.0,
                            sb_xn[:, :], OP.mult, OP.add))
                    P.wait("tensor", "sf_vector", ve)
                    last = None
                    for c in range(16):
                        last = t_evac(sb_fm[:, c * 128:(c + 1) * 128],
                                      sb_fm[:, 2048 + c * 128:
                                            2048 + (c + 1) * 128])
                    P.wait("sync", "ve", last)
                    st = P.dma("sync",
                               fm_dram[half * D:(half + 1) * D, t0:t0 + 128]
                               .rearrange("(c p) n -> p c n", p=128),
                               sb_fm[:, 2048:4096]
                               .rearrange("p (c n) -> p c n", n=128), "st")
                    P.wait("sync", "st", st)
                    P.wait("vector", "st", st)

        def p_coll(kind, cin, cout):
            op = OP.add if kind == "ReduceScatter" else OP.bypass
            cc = P.emit("gpsimd", lambda E, a=cin, b=cout, k=kind, o=op:
                        E.collective_compute(
                            k, o, replica_groups=[[0, 1, 2, 3], [4, 5, 6, 7]],
                            ins=[a[:, :]], outs=[b[:, :]]), inc="cc")
            P.wait("gpsimd", "cc", cc)

        def p_attn_a_qkv():
            qsc = 1.0 / math.sqrt(DH)
            for wi, (whi, wlo, dstd, sc) in enumerate(
                    ((wq_h, wq_l, qh_d, qsc), (wk_h, wk_l, kh_d, 1.0),
                     (wv_h, wv_l, vt_d, 1.0))):
                P.wait("sync", "mm", P.val("mm"))
                lw = None
                for kc in range(16):
                    lw = P.dma("sync", sb_q[:, kc * 512:(kc + 1) * 512],
                               whi[kc * 128:(kc + 1) * 128, :], "ld")
                    lw = P.dma("sync", sb_k[:, kc * 512:(kc + 1) * 512],
                               wlo[kc * 128:(kc + 1) * 128, :], "ld")
                P.wait("tensor", "ld", lw)
                xguard = [None]
                bank_guard = {}
                for tg in range(FT):
                    rank, tloc = tg // NTG, (tg % NTG) * 512
                    P.wait("sync", "mm", xguard[0])
                    lx = None
                    for kc in range(16):
                        lx = P.dma("sync",
                                   sbf[:, kc * 512:(kc + 1) * 512],
                                   xn_co[rank * 2 * D + kc * 128:
                                         rank * 2 * D + (kc + 1) * 128,
                                         tloc:tloc + 512], "ld")
                        lx = P.dma("sync",
                                   sbf[:, 8192 + kc * 512:8192 + (kc + 1) * 512],
                                   xn_co[rank * 2 * D + D + kc * 128:
                                         rank * 2 * D + D + (kc + 1) * 128,
                                         tloc:tloc + 512], "ld")
                    P.wait("tensor", "ld", lx)
                    lastmm = None
                    for m in range(HPC):
                        bi = m % 2
                        bank = pb[bi]
                        P.wait("tensor", "sf_vector", bank_guard.get(bi))
                        mm = None
                        for kc in range(16):
                            mm = mm3(bank[:, :],
                                     sb_q[:, kc * 512 + m * 128:
                                          kc * 512 + (m + 1) * 128],
                                     sb_k[:, kc * 512 + m * 128:
                                          kc * 512 + (m + 1) * 128],
                                     sbf[:, kc * 512:(kc + 1) * 512],
                                     sbf[:, 8192 + kc * 512:
                                         8192 + (kc + 1) * 512],
                                     first=(kc == 0), last=(kc == 15),
                                     inc="mm" if kc == 15 else None)
                        lastmm = mm
                        P.wait("vector", "mm", mm)
                        so = f32_ring.acquire("vector")
                        dst = sfmf[:, so * 512:(so + 1) * 512]
                        ve = P.emit("vector", lambda E, d=dst, b=bank, s=sc:
                                    E.tensor_scalar_mul(d, b[:, :], s))
                        bank_guard[bi] = ve
                        if wi < 2:
                            P.wait("sync", "sf_vector", ve)
                            f32_ring.dma("sync", so,
                                         dstd[m, :, tg * 512:(tg + 1) * 512],
                                         dst)
                        else:
                            P.wait("tensor", "sf_vector", ve)
                            tvs = None
                            for c in range(4):
                                tvs = t_evac(dst[:, c * 128:(c + 1) * 128],
                                             sptf[:, (tg % 2) * 512 + c * 128:
                                                  (tg % 2) * 512 + (c + 1) * 128],
                                             f32=True)
                            f32_ring.consumed(so, "mm", P.val("mm"))
                            P.wait("sync", "ve", tvs)
                            st = P.dma("sync",
                                       vt_d[m, tg * 512:(tg + 1) * 512, :]
                                       .rearrange("(c p) n -> p c n", p=128),
                                       sptf[:, (tg % 2) * 512:
                                            (tg % 2) * 512 + 512]
                                       .rearrange("p (c n) -> p c n", n=128),
                                       "st")
                            P.wait("vector", "st", st)
                    xguard[0] = lastmm

        def p_attn_heads(layer):
            krq = [swstf, sws2f, slwf, szbf]
            if layer == "a":
                for hh in range(HPC):
                    P.dma("sync", sb_co[:, AL5 + hh * 512:AL5 + (hh + 1) * 512],
                          alibi5[hh, :, :], "ld")
                    P.dma("sync", sb_co[:, ALD + hh * 128:ALD + (hh + 1) * 128],
                          alibid[hh, :, :], "ld")
                P.dma("sync", sb_co[:, ALC:ALC + HPC * 64], alibic, "ld")
            else:
                P.dma("sync", sb_co[:, DGM:DGM + 128], diagm, "ld")
                for qt in range((T + 2047) // 2048):
                    wdt = min(2048, T - qt * 2048)
                    P.dma("sync", krq[qt][0:64, 0:wdt],
                          kr_d[:, qt * 2048:qt * 2048 + wdt], "ld")
            qring = Ring(P, 2, dma_sems=["r16", "r17"])
            rring = Ring(P, 2, dma_sems=["r18", "r19"])
            yring = Ring(P, 2, dma_sems=["r20", "r21"])
            Tv = min(T, 4096)
            for h in range(HPC):
                P.wait("sync", "mm", P.val("mm"))
                l1 = P.dma("sync", skf[:, 0:Tv], kh_d[h, :, 0:Tv], "ld")
                if T > 4096:
                    l1 = P.dma("sync", sqf[:, 0:4096], kh_d[h, :, 4096:8192],
                               "ld")
                l1 = P.dma("sync",
                           svf[:, 0:Tv].rearrange("p (c n) -> p c n", n=128),
                           vt_d[h, 0:Tv, :].rearrange("(c p) n -> p c n",
                                                      p=128), "ld")
                if T > 4096:
                    l1 = P.dma("sync",
                               spf[:, :].rearrange("p (c n) -> p c n", n=128),
                               vt_d[h, 4096:8192, :]
                               .rearrange("(c p) n -> p c n", p=128), "ld")
                P.wait("tensor", "ld", l1)
                sbank = {}
                ytguard = [None]
                for qb in range(NQB):
                    tiles = k_tiles_for_qblock(qb)
                    nt = len(tiles)
                    mcol = sb_st[:, 70:71]
                    nmcol = sb_st[:, 71:72]
                    mt = sb_st[:, 69:70]
                    lsums = sb_st[:, 72:92]
                    lcol = sb_st[:, 94:95]
                    ilcol = sb_st[:, 95:96]
                    qs = qring.acquire("sync")
                    lq = qring.dma("sync", qs,
                                   sb_co[:, QRING + qs * 128:
                                         QRING + (qs + 1) * 128],
                                   qh_d[h, :, qb * 128:(qb + 1) * 128])
                    qsl = sb_co[:, QRING + qs * 128:QRING + (qs + 1) * 128]
                    if layer == "b":
                        rs = rring.acquire("sync")
                        lq = rring.dma("sync", rs,
                                       sb_co[0:64, RRING + rs * 128:
                                             RRING + (rs + 1) * 128],
                                       qr_d[h * DHR:(h + 1) * DHR,
                                            qb * 128:(qb + 1) * 128])
                        rsl = sb_co[0:64, RRING + rs * 128:RRING + (rs + 1) * 128]
                    P.wait("tensor", qring.dma_sems[qs], P.val(qring.dma_sems[qs]))
                    if layer == "b":
                        P.wait("tensor", rring.dma_sems[rs],
                               P.val(rring.dma_sems[rs]))
                    P.wait("tensor", "sf_scalar", P.val("sf_scalar"))
                    lastq = None
                    for j, (k0, w, diag) in enumerate(tiles):
                        bi = j % 2
                        bank = pb[bi]
                        P.wait("tensor", sbank.get(bi, ("ve", None))[0],
                               sbank.get(bi, ("ve", None))[1])
                        kbuf = skf if k0 < 4096 else sqf
                        kofs = k0 if k0 < 4096 else k0 - 4096
                        if layer == "a":
                            mm = P.emit("tensor", lambda E, b=bank, q=qsl,
                                        kb=kbuf, ko=kofs, w=w:
                                        E.matmul(b[:, 0:w], q, kb[:, ko:ko + w],
                                                 start=True, stop=True),
                                        inc="mm")
                        else:
                            P.emit("tensor", lambda E, b=bank, q=qsl, kb=kbuf,
                                   ko=kofs, w=w:
                                   E.matmul(b[:, 0:w], q, kb[:, ko:ko + w],
                                            start=True, stop=False))
                            rbuf = krq[k0 // 2048]
                            rofs = k0 % 2048
                            mm = P.emit("tensor", lambda E, b=bank, r=rsl,
                                        rb=rbuf, ro=rofs, w=w:
                                        E.matmul(b[:, 0:w], r,
                                                 rb[0:64, ro:ro + w],
                                                 start=False, stop=True),
                                        inc="mm")
                        doff = (qb * 128 - k0) // 128
                        if layer == "a":
                            cofs = (ALD + h * 128) if diag else (AL5 + h * 512)
                            P.wait("vector", "mm", mm)
                            tok = P.emit("vector", lambda E, b=bank, k0=k0, w=w,
                                         cofs=cofs, h=h, doff=doff, diag=diag:
                                         E.scalar_tensor_tensor(
                                             sb_s[:, k0:k0 + w], b[:, 0:w],
                                             (0.0 if diag else
                                              sb_co[:, ALC + h * 64 + doff:
                                                    ALC + h * 64 + doff + 1]),
                                             sb_co[:, cofs:cofs + w],
                                             OP.add, OP.add))
                            sbank[bi] = ("sf_vector", tok)
                            P.wait("vector", "sf_vector", tok)
                        elif diag:
                            P.wait("vector", "mm", mm)
                            tok = P.emit("vector", lambda E, b=bank, k0=k0, w=w:
                                         E.tensor_tensor(
                                             sb_s[:, k0:k0 + w], b[:, 0:w],
                                             sb_co[:, DGM:DGM + 128], OP.add))
                            sbank[bi] = ("sf_vector", tok)
                        else:
                            P.wait("scalar", "mm", mm)
                            tok = P.emit("scalar", lambda E, b=bank, k0=k0, w=w:
                                         E.copy(sb_s[:, k0:k0 + w], b[:, 0:w]))
                            sbank[bi] = ("sf_scalar", tok)
                            P.wait("vector", "sf_scalar", tok)
                        P.emit("vector", lambda E, k0=k0, w=w:
                               E.reduce_max(mt, sb_s[:, k0:k0 + w], axis=AXX))
                        if j == 0:
                            P.emit("vector", lambda E: E.tensor_copy(mcol, mt))
                        else:
                            P.emit("vector", lambda E:
                                   E.tensor_tensor(mcol, mcol, mt, OP.max))
                    qring.consumed(qs, "mm", P.val("mm"))
                    if layer == "b":
                        rring.consumed(rs, "mm", P.val("mm"))
                    ge = P.emit("vector", lambda E:
                                E.tensor_scalar_mul(nmcol, mcol, -1.0))
                    P.wait("scalar", "sf_vector", ge)
                    opsum = pb[4]
                    pguard = {}
                    for j, (k0, w, diag) in enumerate(tiles):
                        se = P.emit("scalar", lambda E, k0=k0, w=w, j=j:
                                    E.activation(sb_s[:, k0:k0 + w],
                                                 sb_s[:, k0:k0 + w],
                                                 AF.Exp, bias=nmcol,
                                                 accum_out=lsums[:, j:j + 1]))
                        P.wait("tensor", "sf_scalar", se)
                        lastpv = None
                        for c in range(w // 128):
                            P.wait("tensor", "mm", pguard.get(j % 2))
                            ve = t_evac(sb_s[:, k0 + c * 128:k0 + (c + 1) * 128],
                                        sptf[:, (j % 2) * 512 + c * 128:
                                             (j % 2) * 512 + (c + 1) * 128],
                                        f32=True)
                            P.wait("tensor", "ve", ve)
                            kc = k0 // 128 + c
                            vbuf = svf if kc < 32 else spf
                            vofs = (kc % 32) * 128
                            lastpv = P.emit(
                                "tensor", lambda E, j=j, c=c, vb=vbuf, vo=vofs,
                                first=(j == 0 and c == 0),
                                last=(j == nt - 1 and c == w // 128 - 1):
                                E.matmul(opsum[:, 0:128],
                                         sptf[:, (j % 2) * 512 + c * 128:
                                              (j % 2) * 512 + (c + 1) * 128],
                                         vb[:, vo:vo + 128],
                                         start=first, stop=last),
                                inc="mm" if (j == nt - 1 and c == w // 128 - 1)
                                else None)
                        pguard[j % 2] = P.val("mm")
                    mmend = P.val("mm")
                    P.wait("vector", "sf_scalar", P.val("sf_scalar"))
                    P.emit("vector", lambda E, nt=nt:
                           E.reduce_sum(lcol, lsums[:, 0:nt], axis=AXX))
                    P.emit("vector", lambda E: E.reciprocal(ilcol, lcol))
                    P.wait("vector", "mm", mmend)
                    P.wait("vector", "mm", ytguard[0])
                    ysl = yring.acquire("vector")
                    yv = P.emit("vector", lambda E, ysl=ysl:
                                E.tensor_scalar_mul(
                                    sfmf[:, ysl * 256:ysl * 256 + 128],
                                    opsum[:, 0:128], ilcol))
                    P.wait("tensor", "sf_vector", yv)
                    yve = t_evac(sfmf[:, ysl * 256:ysl * 256 + 128],
                                 sfmf[:, ysl * 256 + 128:ysl * 256 + 256],
                                 f32=True)
                    ytguard[0] = P.val("mm")
                    P.wait("sync", "ve", yve)
                    yring.dma("sync", ysl,
                              yfm_d[h * DH:(h + 1) * DH,
                                    qb * 128:(qb + 1) * 128],
                              sfmf[:, ysl * 256 + 128:ysl * 256 + 256])
                P.wait("sync", "st", P.val("st"))

        def p_partial_proj(w_ap):
            lw = None
            for c in range(4):
                wbuf = skf if c < 2 else sqf
                lw = P.dma("sync", wbuf[:, (c % 2) * 2048:(c % 2 + 1) * 2048],
                           w_ap[c * 128:(c + 1) * 128, :], "ld")
            P.wait("tensor", "ld", lw)
            yr = Ring(P, 2, dma_sems=["r16", "r17"])
            bank_guard = {}
            for t in range(T // 128):
                sl = yr.acquire("sync")
                ld = yr.dma("sync", sl,
                            sb_sc[:, sl * 512:sl * 512 + 512]
                            .rearrange("p (c n) -> p c n", n=128),
                            yfm_d[:, t * 128:(t + 1) * 128]
                            .rearrange("(c p) n -> p c n", p=128))
                P.wait("tensor", yr.dma_sems[sl], P.val(yr.dma_sems[sl]))
                lastmm = None
                for n in range(4):
                    bank = pb[n % 4]
                    P.wait("tensor", "sf_vector", bank_guard.get(n % 4))
                    mm = None
                    for kc in range(4):
                        wbuf = skf if kc < 2 else sqf
                        mm = P.emit("tensor", lambda E, b=bank, sl=sl, kc=kc,
                                    n=n, wb=wbuf,
                                    first=(kc == 0), last=(kc == 3):
                                    E.matmul(b[:, :],
                                             sb_sc[:, sl * 512 + kc * 128:
                                                   sl * 512 + (kc + 1) * 128],
                                             wb[:, (kc % 2) * 2048 + n * 512:
                                                (kc % 2) * 2048 + (n + 1) * 512],
                                             start=first, stop=last),
                                    inc="mm" if kc == 3 else None)
                    lastmm = mm
                    P.wait("vector", "mm", mm)
                    so = f32_ring.acquire("vector")
                    dst = sfmf[:, so * 512:(so + 1) * 512]
                    ve = P.emit("vector", lambda E, d=dst, b=bank:
                                E.tensor_copy(d, b[:, :]))
                    bank_guard[n % 4] = ve
                    P.wait("sync", "sf_vector", ve)
                    f32_ring.dma("sync", so,
                                 pp_ci[t * 128:(t + 1) * 128,
                                       n * 512:(n + 1) * 512], dst)
                yr.consumed(sl, "mm", lastmm)

        def p_residual(src_ap, dst_ap, with_bz):
            if with_bz:
                lb = P.dma("sync", sb_xn[:, :], bzrep[:, :], "ld")
                P.wait("vector", "ld", lb)
            for t in range(NTIL):
                t0 = t * 128
                l1 = P.dma("sync", sb_sc[:, 0:2048], pp_co[t0:t0 + 128, :], "ld")
                l2 = P.dma("sync", sb_xt[:, :], src_ap[t0:t0 + 128, :], "ld")
                P.wait("vector", "ld", l2)
                buf = sb_sc2[:, 0:2048]
                if with_bz:
                    P.emit("vector", lambda E, buf=buf: E.tensor_tensor(
                        buf, sb_sc[:, 0:2048], sb_xn[:, :], OP.add))
                    ve = P.emit("vector", lambda E, buf=buf: E.tensor_tensor(
                        buf, buf, sb_xt[:, :], OP.add))
                else:
                    ve = P.emit("vector", lambda E, buf=buf: E.tensor_tensor(
                        buf, sb_sc[:, 0:2048], sb_xt[:, :], OP.add))
                P.wait("sync", "sf_vector", ve)
                st = P.dma("sync", dst_ap[t0:t0 + 128, :], buf, "st")
                P.wait("vector", "st", st)

        def p_attn_b_h():
            lw = None
            for kc in range(16):
                hbuf = sb_q if kc < 8 else sb_k
                lbuf = sb_v if kc < 8 else sb_p
                lw = P.dma("sync", hbuf[:, (kc % 8) * 1024:(kc % 8 + 1) * 1024],
                           wa_h[kc * 128:(kc + 1) * 128, :], "ld")
                lw = P.dma("sync", lbuf[:, (kc % 8) * 1024:(kc % 8 + 1) * 1024],
                           wa_l[kc * 128:(kc + 1) * 128, :], "ld")
            P.wait("tensor", "ld", lw)
            xguard = [None]
            bank_guard = {}
            for tg in range(NTG):
                tloc = tg * 512
                P.wait("sync", "mm", xguard[0])
                lx = None
                for kc in range(16):
                    lx = P.dma("sync", sbf[:, kc * 512:(kc + 1) * 512],
                               xn_ci[kc * 128:(kc + 1) * 128, tloc:tloc + 512],
                               "ld")
                    lx = P.dma("sync",
                               sbf[:, 8192 + kc * 512:8192 + (kc + 1) * 512],
                               xn_ci[D + kc * 128:D + (kc + 1) * 128,
                                     tloc:tloc + 512], "ld")
                P.wait("tensor", "ld", lx)
                lastmm = None
                for m in range(8):
                    bi = m % 2
                    bank = pb[bi]
                    P.wait("tensor", "sf_vector", bank_guard.get(bi))
                    mm = None
                    for kc in range(16):
                        hbuf = sb_q if kc < 8 else sb_k
                        lbuf = sb_v if kc < 8 else sb_p
                        mm = mm3(bank[:, :],
                                 hbuf[:, (kc % 8) * 1024 + m * 128:
                                      (kc % 8) * 1024 + (m + 1) * 128],
                                 lbuf[:, (kc % 8) * 1024 + m * 128:
                                      (kc % 8) * 1024 + (m + 1) * 128],
                                 sbf[:, kc * 512:(kc + 1) * 512],
                                 sbf[:, 8192 + kc * 512:8192 + (kc + 1) * 512],
                                 first=(kc == 0), last=(kc == 15),
                                 inc="mm" if kc == 15 else None)
                    lastmm = mm
                    P.wait("vector", "mm", mm)
                    so = f32_ring.acquire("vector")
                    dst = sfmf[:, so * 512:(so + 1) * 512]
                    ve = P.emit("vector", lambda E, d=dst, b=bank, m=m:
                                E.tensor_scalar_add(d, b[:, :],
                                                    sb_bi[:, BA + m:BA + m + 1]))
                    bank_guard[bi] = ve
                    P.wait("sync", "sf_vector", ve)
                    f32_ring.dma("sync", so,
                                 h_ci[m * 128:(m + 1) * 128, tloc:tloc + 512],
                                 dst)
                xguard[0] = lastmm

        def p_attn_b_proj():
            lw = None
            for c in range(4):
                lw = P.dma("sync", sqf[:, c * 512:(c + 1) * 512],
                           wuq[c * 128:(c + 1) * 128, :], "ld")
                lw = P.dma("sync", skf[:, c * 512:(c + 1) * 512],
                           wuk[c * 128:(c + 1) * 128, :], "ld")
                lw = P.dma("sync", svf[:, c * 512:(c + 1) * 512],
                           wuv[c * 128:(c + 1) * 128, :], "ld")
            for kc in range(8):
                lw = P.dma("sync", spf[:, kc * 256:(kc + 1) * 256],
                           wc[kc * 128:(kc + 1) * 128, :], "ld")
                lw = P.dma("sync", sqf[:, 2048 + kc * 256:2048 + (kc + 1) * 256],
                           wcs[kc * 128:(kc + 1) * 128, :], "ld")
                lw = P.dma("sync", skf[:, 2048 + kc * 64:2048 + (kc + 1) * 64],
                           wb[kc * 128:(kc + 1) * 128, :], "ld")
                lw = P.dma("sync", skf[:, 2560 + kc * 64:2560 + (kc + 1) * 64],
                           wbs[kc * 128:(kc + 1) * 128, :], "ld")
            P.wait("tensor", "ld", lw)
            s192 = (DH + DHR) ** (-0.5)
            xguard = [None]
            bank_guard = {}
            for tg in range(FT):
                rank, tloc = tg // NTG, (tg % NTG) * 512
                half = (tg % 2) * 4096
                P.wait("sync", "mm", xguard[0] if tg >= 2 else None)
                lx = None
                for kc in range(8):
                    lx = P.dma("sync",
                               sb_s[:, half + kc * 512:half + (kc + 1) * 512],
                               h_co[rank * 2 * DL + kc * 128:
                                    rank * 2 * DL + (kc + 1) * 128,
                                    tloc:tloc + 512], "ld")
                P.wait("tensor", "ld", lx)
                lastmm = None
                for wi, (wbuf, dstd) in enumerate(
                        ((sqf, qh_d), (skf, kh_d), (svf, vt_d))):
                    kcs = range(4, 8) if wi == 0 else range(0, 4)
                    for m in range(HPC):
                        bi = m % 2
                        bank = pb[bi]
                        P.wait("tensor", "sf_vector", bank_guard.get(bi))
                        mm = None
                        for j, kc in enumerate(kcs):
                            mm = P.emit("tensor", lambda E, b=bank, wb=wbuf,
                                        m=m, j=j, kc=kc, half=half,
                                        first=(j == 0), last=(j == 3):
                                        E.matmul(b[:, :],
                                                 wb[:, j * 512 + m * 128:
                                                    j * 512 + (m + 1) * 128],
                                                 sb_s[:, half + kc * 512:
                                                      half + (kc + 1) * 512],
                                                 start=first, stop=last),
                                        inc="mm" if j == 3 else None)
                        lastmm = mm
                        P.wait("vector", "mm", mm)
                        so = f32_ring.acquire("vector")
                        dst = sfmf[:, so * 512:(so + 1) * 512]
                        sc = s192 if wi == 0 else 1.0
                        ve = P.emit("vector", lambda E, d=dst, b=bank, s=sc:
                                    E.tensor_scalar_mul(d, b[:, :], s))
                        bank_guard[bi] = ve
                        if wi < 2:
                            P.wait("sync", "sf_vector", ve)
                            f32_ring.dma("sync", so,
                                         dstd[m, :, tg * 512:(tg + 1) * 512],
                                         dst)
                        else:
                            P.wait("tensor", "sf_vector", ve)
                            tvs = None
                            for c in range(4):
                                tvs = t_evac(dst[:, c * 128:(c + 1) * 128],
                                             sptf[:, (tg % 2) * 512 + c * 128:
                                                  (tg % 2) * 512 + (c + 1) * 128],
                                             f32=True)
                            f32_ring.consumed(so, "mm", P.val("mm"))
                            P.wait("sync", "ve", tvs)
                            st = P.dma("sync",
                                       vt_d[m, tg * 512:(tg + 1) * 512, :]
                                       .rearrange("(c p) n -> p c n", p=128),
                                       sptf[:, (tg % 2) * 512:
                                            (tg % 2) * 512 + 512]
                                       .rearrange("p (c n) -> p c n", n=128),
                                       "st")
                            P.wait("vector", "st", st)
                for m2 in range(2):
                    for sw in range(2):
                        bank = pb[2 + sw]
                        P.wait("tensor", "sf_vector", bank_guard.get(2 + sw))
                        wsl = (spf, 0) if sw == 0 else (sqf, 2048)
                        mm = None
                        for kc in range(8):
                            mm = P.emit("tensor", lambda E, b=bank, wb=wsl[0],
                                        wo0=wsl[1], m2=m2, kc=kc, half=half,
                                        first=(kc == 0), last=(kc == 7):
                                        E.matmul(b[:, :],
                                                 wb[:, wo0 + kc * 256 + m2 * 128:
                                                    wo0 + kc * 256 + (m2 + 1) * 128],
                                                 sb_s[:, half + kc * 512:
                                                      half + (kc + 1) * 512],
                                                 start=first, stop=last),
                                        inc="mm" if kc == 7 else None)
                        lastmm = mm
                        P.wait("vector", "mm", mm)
                        dstf = sb_sc[:, 0:512] if sw == 0 else sb_sc[:, 512:1024]
                        bia = (BC_ + m2) if sw == 0 else (BCS + m2)
                        ve = P.emit("vector", lambda E, d=dstf, b=bank, bia=bia:
                                    E.tensor_scalar_add(
                                        d, b[:, :], sb_bi[:, bia:bia + 1]))
                        bank_guard[2 + sw] = ve
                    lr = P.dma("sync", sb_rp[:, 0:512],
                               cosqr[m2 * 128:(m2 + 1) * 128,
                                     tg * 512:(tg + 1) * 512], "ld")
                    lr = P.dma("sync", sb_rp[:, 512:1024],
                               sinqr[m2 * 128:(m2 + 1) * 128,
                                     tg * 512:(tg + 1) * 512], "ld")
                    P.wait("vector", "ld", lr)
                    P.emit("vector", lambda E: E.tensor_tensor(
                        sb_sc[:, 1024:1536], sb_sc[:, 0:512], sb_rp[:, 0:512],
                        OP.mult))
                    P.emit("vector", lambda E: E.tensor_tensor(
                        sb_sc[:, 1536:2048], sb_sc[:, 512:1024],
                        sb_rp[:, 512:1024], OP.mult))
                    P.emit("vector", lambda E: E.tensor_tensor(
                        sb_sc[:, 1024:1536], sb_sc[:, 1024:1536],
                        sb_sc[:, 1536:2048], OP.add))
                    so = f32_ring.acquire("vector")
                    dst = sfmf[:, so * 512:(so + 1) * 512]
                    ve = P.emit("vector", lambda E, d=dst:
                                E.tensor_scalar_mul(d, sb_sc[:, 1024:1536],
                                                    s192))
                    P.wait("sync", "sf_vector", ve)
                    f32_ring.dma("sync", so,
                                 qr_d[m2 * 128:(m2 + 1) * 128,
                                      tg * 512:(tg + 1) * 512], dst)
                for sw in range(2):
                    bank = pb[4 + sw]
                    P.wait("tensor", "sf_vector", bank_guard.get(4 + sw))
                    wofs = 2048 if sw == 0 else 2560
                    mm = None
                    for kc in range(8):
                        mm = P.emit("tensor", lambda E, b=bank, wofs=wofs,
                                    kc=kc, half=half,
                                    first=(kc == 0), last=(kc == 7):
                                    E.matmul(b[0:64, :],
                                             skf[:, wofs + kc * 64:
                                                 wofs + (kc + 1) * 64],
                                             sb_s[:, half + kc * 512:
                                                  half + (kc + 1) * 512],
                                             start=first, stop=last),
                                    inc="mm" if kc == 7 else None)
                    lastmm = mm
                    P.wait("vector", "mm", mm)
                    dstf = sb_sc[0:64, 0:512] if sw == 0 else sb_sc[0:64, 512:1024]
                    bia = BB_ if sw == 0 else BBS
                    ve = P.emit("vector", lambda E, d=dstf, b=bank, bia=bia:
                                E.tensor_scalar_add(d, b[0:64, :],
                                                    sb_bi[0:64, bia:bia + 1]))
                    bank_guard[4 + sw] = ve
                lr = P.dma("sync", sb_rp[0:64, 0:512],
                           coskr[:, tg * 512:(tg + 1) * 512], "ld")
                lr = P.dma("sync", sb_rp[0:64, 512:1024],
                           sinkr[:, tg * 512:(tg + 1) * 512], "ld")
                P.wait("vector", "ld", lr)
                P.emit("vector", lambda E: E.tensor_tensor(
                    sb_sc[0:64, 1024:1536], sb_sc[0:64, 0:512],
                    sb_rp[0:64, 0:512], OP.mult))
                P.emit("vector", lambda E: E.tensor_tensor(
                    sb_sc[0:64, 1536:2048], sb_sc[0:64, 512:1024],
                    sb_rp[0:64, 512:1024], OP.mult))
                so = f32_ring.acquire("vector")
                dst = sfmf[0:64, so * 512:(so + 1) * 512]
                ve = P.emit("vector", lambda E, d=dst: E.tensor_tensor(
                    d, sb_sc[0:64, 1024:1536], sb_sc[0:64, 1536:2048], OP.add))
                P.wait("sync", "sf_vector", ve)
                f32_ring.dma("sync", so, kr_d[:, tg * 512:(tg + 1) * 512], dst)
                xguard[0] = lastmm

        def p_routing():
            lw = None
            for kc in range(16):
                lw = P.dma("sync", sb_zb[:, kc * 8:(kc + 1) * 8],
                           confw_h[kc * 128:(kc + 1) * 128, :], "ld")
                lw = P.dma("sync", sb_zb[:, 128 + kc * 8:128 + (kc + 1) * 8],
                           confw_l[kc * 128:(kc + 1) * 128, :], "ld")
            P.wait("tensor", "ld", lw)
            xguard = [None]
            for tg in range(NTG):
                tloc = tg * 512
                P.wait("sync", "mm", xguard[0])
                lx = None
                for kc in range(16):
                    lx = P.dma("sync", sbf[:, kc * 512:(kc + 1) * 512],
                               xnfm_d[kc * 128:(kc + 1) * 128, tloc:tloc + 512],
                               "ld")
                    lx = P.dma("sync",
                               sbf[:, 8192 + kc * 512:8192 + (kc + 1) * 512],
                               xnfm_d[D + kc * 128:D + (kc + 1) * 128,
                                      tloc:tloc + 512], "ld")
                P.wait("tensor", "ld", lx)
                mm = None
                combos = [(0, 0), (0, 8192), (128, 0)]
                for ci, (wofs_, xofs_) in enumerate(combos):
                    for kc in range(16):
                        lastf = (ci == 2 and kc == 15)
                        mm = P.emit("tensor", lambda E, kc=kc, w0=wofs_,
                                    x0=xofs_, first=(ci == 0 and kc == 0),
                                    last=lastf:
                                    E.matmul(pb[0][0:NE, :],
                                             sb_zb[:, w0 + kc * 8:
                                                   w0 + (kc + 1) * 8],
                                             sbf[:, x0 + kc * 512:
                                                 x0 + (kc + 1) * 512],
                                             start=first, stop=last),
                                    inc="mm" if lastf else None)
                xguard[0] = mm
                P.wait("vector", "mm", mm)
                ve = P.emit("vector", lambda E, tg=tg: E.tensor_scalar_add(
                    sb_sc[0:NE, tg * 512:(tg + 1) * 512], pb[0][0:NE, :],
                    sb_bi[0:NE, CB_:CB_ + 1]))
                P.wait("tensor", "sf_vector", ve)
            ve = P.val("sf_vector")
            P.wait("scalar", "sf_vector", ve)
            se = P.emit("scalar", lambda E: E.activation(
                sb_sc[0:NE, 0:TOK], sb_sc[0:NE, 0:TOK], AF.Sigmoid))
            P.wait("vector", "sf_scalar", se)
            ve = P.emit("vector", lambda E: E.tensor_scalar_mul(
                sb_sc[0:NE, 0:TOK], sb_sc[0:NE, 0:TOK],
                sb_bi[0:NE, WE_:WE_ + 1]))
            P.wait("tensor", "sf_vector", ve)
            for t in range(NTIL):
                mm = P.emit("tensor", lambda E, t=t: E.transpose(
                    pb[1][:, 0:NE], sb_sc[0:NE, t * 128:(t + 1) * 128],
                    sb_cf[0:NE, 0:NE]), inc="mm")
                P.wait("vector", "mm", mm)
                ve = P.emit("vector", lambda E, t=t: E.tensor_copy(
                    sb_sc2[:, t * 8:(t + 1) * 8], pb[1][:, 0:NE]))
                P.wait("tensor", "sf_vector", ve)
            for t in range(NTIL):
                bid = sb_sc2[:, t * 8:(t + 1) * 8]
                top8 = sb_st[:, 96:104]
                m1 = sb_st[:, 104:105]
                m2 = sb_st[:, 105:106]
                d12 = sb_st[:, 106:107]
                w1 = sb_st[:, 107:108]
                w2 = sb_st[:, 108:109]
                eq1 = sb_st[:, 112:120]
                eq2 = sb_st[:, 120:128]
                ne1 = sb_st[:, 128:136]
                acc = sb_st[:, 136:144]
                P.emit("vector", lambda E, bid=bid: E.max(out=top8, in_=bid))
                P.emit("vector", lambda E: E.tensor_copy(m1, top8[:, 0:1]))
                P.emit("vector", lambda E: E.tensor_copy(m2, top8[:, 1:2]))
                ve = P.emit("vector", lambda E: E.tensor_scalar(
                    d12, m1, m2, None, OP.subtract))
                P.wait("scalar", "sf_vector", ve)
                se = P.emit("scalar", lambda E:
                            E.activation(w1, d12, AF.Sigmoid))
                P.wait("vector", "sf_scalar", se)
                P.emit("vector", lambda E: E.tensor_scalar(
                    w2, w1, -1.0, 1.0, OP.mult, OP.add))
                P.emit("vector", lambda E, bid=bid: E.tensor_scalar(
                    eq1, bid, m1, None, OP.is_equal))
                P.emit("vector", lambda E, bid=bid: E.tensor_scalar(
                    eq2, bid, m2, None, OP.is_equal))
                P.emit("vector", lambda E: E.tensor_scalar(
                    ne1, eq1, -1.0, 1.0, OP.mult, OP.add))
                P.emit("vector", lambda E: E.tensor_scalar(acc, eq1, w1, None,
                                                           OP.mult))
                P.emit("vector", lambda E: E.tensor_tensor(eq2, eq2, ne1,
                                                           OP.mult))
                P.emit("vector", lambda E: E.tensor_scalar(eq2, eq2, w2, None,
                                                           OP.mult))
                ve = P.emit("vector", lambda E: E.tensor_tensor(
                    acc, acc, eq2, OP.add))
                P.wait("tensor", "sf_vector", ve)
                mm = P.emit("tensor", lambda E: E.transpose(
                    pb[1][0:NE, 0:128], acc, sb_cf[:, 0:128]), inc="mm")
                P.wait("vector", "mm", mm)
                ve = P.emit("vector", lambda E, t=t: E.tensor_copy(
                    sb_sc[0:NE, 2048 + t * 128:2048 + (t + 1) * 128],
                    pb[1][0:NE, 0:128]))
                P.wait("tensor", "sf_vector", ve)
            P.wait("sync", "sf_vector", P.val("sf_vector"))
            P.dma("sync", wt_d[:, :], sb_sc[0:NE, 2048:2048 + TOK], "st")

        def emit_wbc(tg):
            for e_ in range(NE):
                lwv = P.dma("sync", sb_rp[0:1, 0:512],
                            wt_d[e_:e_ + 1, tg * 512:(tg + 1) * 512], "ld")
                P.wait("tensor", "ld", lwv)
                mm = P.emit("tensor", lambda E: E.matmul(
                    pb[2][:, :], sb_or[0:1, 0:128], sb_rp[0:1, 0:512],
                    start=True, stop=True), inc="mm")
                P.wait("sync", "mm", mm)
                P.wait("vector", "mm", mm)
                ve = P.emit("vector", lambda E, e_=e_: E.tensor_copy(
                    sb_sc[:, e_ * 512:(e_ + 1) * 512], pb[2][:, :]))
                P.wait("tensor", "sf_vector", ve)
            return P.val("sf_vector")

        def p_ffn_experts():
            whr = Ring(P, 4)
            hxr = Ring(P, 2, dma_sems=["r22", "r23"])
            wldr = Ring(P, 2)
            lorar = Ring(P, 2)
            for tg in range(NTG):
                tloc = tg * 512
                P.wait("sync", "mm", P.val("mm"))
                lx = None
                for kc in range(16):
                    lx = P.dma("sync", sbf[:, kc * 512:(kc + 1) * 512],
                               xnfm_d[kc * 128:(kc + 1) * 128, tloc:tloc + 512],
                               "ld")
                    lx = P.dma("sync",
                               sbf[:, 8192 + kc * 512:8192 + (kc + 1) * 512],
                               xnfm_d[D + kc * 128:D + (kc + 1) * 128,
                                      tloc:tloc + 512], "ld")
                emit_wbc(tg)
                for gi, (ah, al, xaofs) in enumerate(
                        ((gA_h, gA_l, 0), (uA_h, uA_l, 4096))):
                    P.wait("sync", "mm", P.val("mm"))
                    la = None
                    for kc in range(16):
                        hb = sb_wst if kc < 8 else sb_ws2
                        lb = sb_lw if kc < 8 else sb_zb
                        la = P.dma("sync",
                                   hb[:, (kc % 8) * 512:(kc % 8 + 1) * 512],
                                   ah[kc * 128:(kc + 1) * 128, :], "ld")
                        la = P.dma("sync",
                                   lb[:, (kc % 8) * 512:(kc % 8 + 1) * 512],
                                   al[kc * 128:(kc + 1) * 128, :], "ld")
                    P.wait("tensor", "ld", max(lx, la))
                    for til in range(4):
                        mm = None
                        for kc in range(16):
                            hb = sb_wst if kc < 8 else sb_ws2
                            lb = sb_lw if kc < 8 else sb_zb
                            mm = mm3(pb[2][:, :],
                                     sbf[:, kc * 512 + til * 128:
                                         kc * 512 + (til + 1) * 128],
                                     sbf[:, 8192 + kc * 512 + til * 128:
                                         8192 + kc * 512 + (til + 1) * 128],
                                     hb[:, (kc % 8) * 512:(kc % 8 + 1) * 512],
                                     lb[:, (kc % 8) * 512:(kc % 8 + 1) * 512],
                                     first=(kc == 0), last=(kc == 15),
                                     inc="mm" if kc == 15 else None)
                        P.wait("scalar", "mm", mm)
                        se = P.emit("scalar", lambda E:
                                    E.mul(sb_sm[:, :], pb[2][:, :], SCALING))
                        P.wait("vector", "sf_scalar", se)
                        vh = P.emit("vector", lambda E: E.tensor_copy(
                            sb_fm[:, 0:512], sb_sm[:, :]))
                        vl = P.emit("vector", lambda E: E.scalar_tensor_tensor(
                            sb_fm[:, 512:1024], sb_fm[:, 0:512], -1.0,
                            sb_sm[:, :], OP.mult, OP.add))
                        P.wait("tensor", "sf_vector", vl)
                        P.wait("tensor", "sf_scalar", se)
                        P.wait("scalar", "sf_vector", vl)
                        for c in range(4):
                            t_evac(sb_fm[:, c * 128:(c + 1) * 128],
                                   sb_p[:, xaofs + c * 512 + til * 128:
                                        xaofs + c * 512 + (til + 1) * 128])
                            t_evac(sb_fm[:, 512 + c * 128:512 + (c + 1) * 128],
                                   sb_p[:, xaofs + 2048 + c * 512 + til * 128:
                                        xaofs + 2048 + c * 512 + (til + 1) * 128])
                t_guard()
                for pr in range(4):
                    P.emit("tensor", lambda E, pr=pr: E.matmul(
                        pb[2 + pr][:, :], sb_or[0:1, 0:128],
                        sb_zf[0:1, 0:512], start=True, stop=False))
                P.wait("sync", "mm", P.val("mm"))
                for ic in range(NIC):
                    half = wldr.acquire("sync")
                    wofs = half * 2048
                    l1 = P.dma("sync", sb_wst[:, wofs:wofs + 2048],
                               bgate_h[ic * 128:(ic + 1) * 128, :], "ld")
                    l1 = P.dma("sync", sb_ws2[:, wofs:wofs + 2048],
                               bgate_l[ic * 128:(ic + 1) * 128, :], "ld")
                    l1 = P.dma("sync", sb_lw[:, wofs:wofs + 2048],
                               bup_h[ic * 128:(ic + 1) * 128, :], "ld")
                    l1 = P.dma("sync", sb_zb[:, wofs:wofs + 2048],
                               bup_l[ic * 128:(ic + 1) * 128, :], "ld")
                    lh = lorar.acquire("sync")
                    lofs = lh * 3072
                    l1 = P.dma("sync", sb_q[:, lofs:lofs + 512],
                               gB_h[ic * 128:(ic + 1) * 128, :], "ld")
                    l1 = P.dma("sync", sb_q[:, lofs + 512:lofs + 1024],
                               gB_l[ic * 128:(ic + 1) * 128, :], "ld")
                    l1 = P.dma("sync", sb_q[:, lofs + 1024:lofs + 1536],
                               uB_h[ic * 128:(ic + 1) * 128, :], "ld")
                    l1 = P.dma("sync", sb_q[:, lofs + 1536:lofs + 2048],
                               uB_l[ic * 128:(ic + 1) * 128, :], "ld")
                    l1 = P.dma("sync",
                               sqf[:, (lofs + 2048) // 2:(lofs + 2048) // 2 + 512],
                               dA[ic * 128:(ic + 1) * 128, :], "ld")
                    P.wait("tensor", "ld", l1)
                    for bsel in range(2):
                        hb = sb_wst if bsel == 0 else sb_lw
                        lb = sb_ws2 if bsel == 0 else sb_zb
                        bank = pb[0] if bsel == 0 else pb[1]
                        mm = None
                        for kc in range(16):
                            mm = mm3(bank[0:128, 0:512],
                                     hb[:, wofs + kc * 128:wofs + (kc + 1) * 128],
                                     lb[:, wofs + kc * 128:wofs + (kc + 1) * 128],
                                     sbf[:, kc * 512:(kc + 1) * 512],
                                     sbf[:, 8192 + kc * 512:
                                         8192 + (kc + 1) * 512],
                                     first=(kc == 0), last=(kc == 15),
                                     inc="mm" if kc == 15 else None)
                        P.wait("scalar", "mm", mm)
                        dstf = sb_rp[:, 0:512] if bsel == 0 else sb_rp[:, 512:1024]
                        se = P.emit("scalar", lambda E, d=dstf, b=bank:
                                    E.copy(d, b[0:128, 0:512]))
                        if bsel == 1:
                            P.wait("vector", "sf_scalar", se)
                    wldr.consumed(half, "mm", P.val("mm"))
                    sebase = P.val("sf_scalar")
                    for e_ in range(NE):
                        eb = (e_ % 2) * 64
                        ech = (e_ // 2)
                        mm = mm3(pb6[0:128, 0:512],
                                 sb_q[eb:eb + 64, lofs + ech * 128:
                                      lofs + (ech + 1) * 128],
                                 sb_q[eb:eb + 64, lofs + 512 + ech * 128:
                                      lofs + 512 + (ech + 1) * 128],
                                 sb_p[eb:eb + 64, ech * 512:(ech + 1) * 512],
                                 sb_p[eb:eb + 64, 2048 + ech * 512:
                                      2048 + (ech + 1) * 512],
                                 first=True, last=True, inc="mm")
                        P.wait("vector", "mm", mm)
                        P.wait("vector", "sf_scalar", sebase)
                        ve = P.emit("vector", lambda E: E.tensor_tensor(
                            sb_sc2[:, 0:512], sb_rp[:, 0:512],
                            pb6[0:128, 0:512], OP.add))
                        P.wait("scalar", "sf_vector", ve)
                        se = P.emit("scalar", lambda E: E.activation(
                            sb_sc2[:, 512:1024], sb_sc2[:, 0:512], AF.Silu))
                        P.wait("tensor", "sf_vector", ve)
                        mm = mm3(pb6[0:128, 0:512],
                                 sb_q[eb:eb + 64, lofs + 1024 + ech * 128:
                                      lofs + 1024 + (ech + 1) * 128],
                                 sb_q[eb:eb + 64, lofs + 1536 + ech * 128:
                                      lofs + 1536 + (ech + 1) * 128],
                                 sb_p[eb:eb + 64, 4096 + ech * 512:
                                      4096 + (ech + 1) * 512],
                                 sb_p[eb:eb + 64, 6144 + ech * 512:
                                      6144 + (ech + 1) * 512],
                                 first=True, last=True, inc="mm")
                        P.wait("vector", "mm", mm)
                        ve = P.emit("vector", lambda E: E.tensor_tensor(
                            sb_sc2[:, 1024:1536], sb_rp[:, 512:1024],
                            pb6[0:128, 0:512], OP.add))
                        P.wait("tensor", "sf_vector", ve)
                        P.wait("vector", "sf_scalar", se)
                        P.emit("vector", lambda E: E.tensor_tensor(
                            sb_sc2[:, 1536:2048], sb_sc2[:, 512:1024],
                            sb_sc2[:, 1024:1536], OP.mult))
                        so = whr.acquire("vector")
                        whs = svf[:, so * 512:(so + 1) * 512]
                        ve = P.emit("vector", lambda E, w=whs, e_=e_:
                                    E.tensor_tensor(
                                        w, sb_sc2[:, 1536:2048],
                                        sb_sc[:, e_ * 512:(e_ + 1) * 512],
                                        OP.mult))
                        P.wait("gpsimd", "sf_vector", ve)
                        if e_ == 0:
                            ge = P.emit("gpsimd", lambda E, w=whs:
                                        E.tensor_copy(sb_sm[:, :], w))
                        else:
                            ge = P.emit("gpsimd", lambda E, w=whs:
                                        E.tensor_tensor(sb_sm[:, :],
                                                        sb_sm[:, :], w, OP.add))
                        P.wait("tensor", "sf_gpsimd", ge)
                        mm = P.emit("tensor", lambda E, e_=e_, eb=eb, w=whs,
                                    lofs=lofs:
                                    E.matmul(pb[2 + e_ // 2][eb:eb + 64, :],
                                             sqf[:, (lofs + 2048) // 2 + e_ * 64:
                                                 (lofs + 2048) // 2 + (e_ + 1) * 64],
                                             w, start=False, stop=False),
                                    inc="mm")
                        whr.consumed(so, "mm", mm)
                        whr.consumed(so, "sf_gpsimd", ge)
                    lorar.consumed(lh, "mm", P.val("mm"))
                    P.wait("scalar", "sf_gpsimd", P.val("sf_gpsimd"))
                    so = hxr.acquire("scalar")
                    hof = 1024 + so * 1024
                    se = P.emit("scalar", lambda E, hof=hof:
                                E.copy(sb_fm[:, hof:hof + 512], sb_sm[:, :]))
                    P.wait("vector", "sf_scalar", se)
                    ve = P.emit("vector", lambda E, hof=hof:
                                E.scalar_tensor_tensor(
                                    sb_fm[:, hof + 512:hof + 1024],
                                    sb_fm[:, hof:hof + 512], -1.0,
                                    sb_sm[:, :], OP.mult, OP.add))
                        
                    P.wait("sync", "sf_vector", ve)
                    hxr.dma("sync", so, hmix_d[tg, ic, :, :],
                            sb_fm[:, hof:hof + 1024])
                    P.wait("gpsimd", "sf_vector", ve)
                t_release()
                mmz = None
                for pr in range(4):
                    mmz = P.emit("tensor", lambda E, pr=pr: E.matmul(
                        pb[2 + pr][:, :], sb_or[0:1, 0:128],
                        sb_zf[0:1, 0:512], start=False, stop=True), inc="mm")
                P.wait("vector", "mm", mmz)
                P.wait("vector", "sf_gpsimd", P.val("sf_gpsimd"))
                for pr in range(4):
                    ve = P.emit("vector", lambda E, pr=pr:
                                E.tensor_scalar_mul(
                                    svf[:, pr * 512:(pr + 1) * 512],
                                    pb[2 + pr][:, :], SCALING))
                    P.wait("sync", "sf_vector", ve)
                    st = P.dma("sync", z_d[tg, :, pr * 512:(pr + 1) * 512],
                               svf[:, pr * 512:(pr + 1) * 512], "st")
                    P.wait("vector", "st", st)
                    P.wait("tensor", "sf_vector", ve)

        def p_ffn_down(src_ap, dst_ap):
            banks = list(pb) + [pb6, pb7]
            hr = Ring(P, 3)
            br = Ring(P, 2)
            ldb = None
            for c in range(4):
                ldb = P.dma("sync", sb_s[:, c * 2048:(c + 1) * 2048],
                            dB[c * 128:(c + 1) * 128, :], "ld")
            for tg in range(NTG):
                P.wait("sync", "mm", P.val("mm"))
                lz = P.dma("sync", slwf[:, 0:2048], z_d[tg, :, :], "ld")
                for dh in range(2):
                    lastmm = None
                    for ic in range(NIC):
                        hs = hr.acquire("sync")
                        l1 = P.dma("sync", sb_wst[:, hs * 1024:hs * 1024 + 1024],
                                   hmix_d[tg, ic, :, :], "ld")
                        bs = br.acquire("sync")
                        l2 = P.dma("sync", sb_ws2[:, bs * 2048:bs * 2048 + 1024],
                                   bdown_h[ic * 128:(ic + 1) * 128,
                                           dh * 1024:dh * 1024 + 1024], "ld")
                        l2 = P.dma("sync",
                                   sb_ws2[:, bs * 2048 + 1024:bs * 2048 + 2048],
                                   bdown_l[ic * 128:(ic + 1) * 128,
                                           dh * 1024:dh * 1024 + 1024], "ld")
                        hop(P, "sync", "tensor",
                            [("ld", max(l1, l2, lz, ldb))])
                        mm = None
                        for til in range(4):
                            for dc in range(2):
                                mm = mm3(banks[til * 2 + dc][:, :],
                                         sb_wst[:, hs * 1024 + til * 128:
                                                hs * 1024 + (til + 1) * 128],
                                         sb_wst[:, hs * 1024 + 512 + til * 128:
                                                hs * 1024 + 512 + (til + 1) * 128],
                                         sb_ws2[:, bs * 2048 + dc * 512:
                                                bs * 2048 + (dc + 1) * 512],
                                         sb_ws2[:, bs * 2048 + 1024 + dc * 512:
                                                bs * 2048 + 1024 + (dc + 1) * 512],
                                         first=(ic == 0), last=False,
                                         inc="mm" if (til == 3 and dc == 1)
                                         else None)
                        hr.consumed(hs, "mm", mm)
                        br.consumed(bs, "mm", mm)
                        lastmm = mm
                    for e_ in range(NE):
                        eb = (e_ % 2) * 64
                        ech = e_ // 2
                        for til in range(4):
                            for dc in range(2):
                                lastf = (e_ == NE - 1)
                                lastmm = P.emit(
                                    "tensor", lambda E, eb=eb, ech=ech,
                                    til=til, dc=dc, dh=dh, last=lastf:
                                    E.matmul(
                                        banks[til * 2 + dc][:, :],
                                        slwf[eb:eb + 64,
                                             ech * 512 + til * 128:
                                             ech * 512 + (til + 1) * 128],
                                        sb_s[eb:eb + 64,
                                             ech * 2048 + dh * 1024 + dc * 512:
                                             ech * 2048 + dh * 1024 +
                                             dc * 512 + 512],
                                        start=False, stop=last),
                                    inc="mm" if (e_ == NE - 1 and til == 3 and
                                                 dc == 1) else None)
                    P.wait("vector", "mm", P.val("mm"))
                    for til in range(4):
                        t0 = tg * 512 + til * 128
                        lo = P.dma("sync", sb_xt[:, 0:1024],
                                   src_ap[t0:t0 + 128,
                                          dh * 1024:dh * 1024 + 1024], "ld")
                        P.wait("vector", "ld", lo)
                        ve = None
                        for dc in range(2):
                            ve = P.emit("vector", lambda E, til=til, dc=dc:
                                        E.tensor_tensor(
                                            sb_sc[:, til * 1024 + dc * 512:
                                                  til * 1024 + (dc + 1) * 512],
                                            banks[til * 2 + dc][:, :],
                                            sb_xt[:, dc * 512:(dc + 1) * 512],
                                            OP.add))
                        P.wait("sync", "sf_vector", ve)
                        st = P.dma("sync",
                                   dst_ap[t0:t0 + 128,
                                          dh * 1024:dh * 1024 + 1024],
                                   sb_sc[:, til * 1024:til * 1024 + 1024], "st")
                        P.wait("vector", "st", st)
                    P.wait("tensor", "sf_vector", P.val("sf_vector"))

        pcount = [0]

        def phase(fn, *a, **kw):
            pcount[0] += 1
            if pcount[0] > PHASE_LIMIT:
                return
            fn(*a, **kw)
            P.barrier()

        phase(p_consts)
        phase(p_cast_weights)

        def sub_attn_a(src, dst):
            phase(p_ln_fm, src, xn_ci)
            phase(p_coll, "AllGather", xn_ci, xn_co)
            phase(p_attn_a_qkv)
            phase(p_attn_heads, "a")
            phase(p_partial_proj, wo)
            phase(p_coll, "ReduceScatter", pp_ci, pp_co)
            phase(p_residual, src, dst, False)

        def sub_ffn(src, dst):
            phase(p_ln_fm, src, xnfm_d)
            phase(p_routing)
            phase(p_ffn_experts)
            phase(p_ffn_down, src, dst)

        def sub_attn_b(src, dst):
            phase(p_ln_fm, src, xn_ci)
            phase(p_attn_b_h)
            phase(p_coll, "AllGather", h_ci, h_co)
            phase(p_attn_b_proj)
            phase(p_attn_heads, "b")
            phase(p_partial_proj, wz)
            phase(p_coll, "ReduceScatter", pp_ci, pp_co)
            phase(p_residual, src, dst, True)

        subs = [
            lambda d: sub_attn_a(x0, d),
            lambda d: sub_ffn(xres, d),
            lambda d: sub_attn_b(xres, d),
            lambda d: sub_ffn(xres, d),
        ]
        for i in range(N_SUB):
            dst = yout if i == N_SUB - 1 else xres
            subs[i](dst)

        P.wait("gpsimd", "st", P.val("st"))
        P.wait("gpsimd", "ld", P.val("ld"))
        for i in range(24):
            P.wait("gpsimd", f"r{i}", P.val(f"r{i}"))

        with nc.Block() as block:
            for ename in ENGINES:
                def make(ename):
                    def body(E):
                        for fn in P.ops[ename]:
                            fn(E)
                    return body
                getattr(block, ename)(make(ename))

    return nc



def _rope_tables(t, d):
    freq = np.arange(1, t + 1, dtype=np.float64)[:, None]
    pos = np.repeat(np.arange(d // 2, dtype=np.float64), 2)[None]
    theta = np.exp(-pos / d * math.log(10000.0))
    ang = freq * theta
    return np.cos(ang).astype(np.float32), np.sin(ang).astype(np.float32)


def _swap_pairs(a, axis):
    idx = np.arange(a.shape[axis])
    idx = idx.reshape(-1, 2)[:, ::-1].reshape(-1)
    return np.take(a, idx, axis=axis)


def kernel(**inputs):
    dm = _dims()
    HPC, TOK = dm["HPC"], dm["TOK"]
    f32 = np.float32

    Wqkv = np.asarray(inputs["L0_attn_a_W_qkv"], f32)
    Wo = np.asarray(inputs["L0_attn_a_W_o"], f32)
    x_in = np.asarray(inputs["L0_attn_a_x"], f32)
    WA = np.asarray(inputs["L0_attn_b_WA"], f32)
    bA = np.asarray(inputs["L0_attn_b_bA"], f32)
    WB = np.asarray(inputs["L0_attn_b_WB"], f32)
    bB = np.asarray(inputs["L0_attn_b_bB"], f32)
    WC = np.asarray(inputs["L0_attn_b_WC"], f32)
    bC = np.asarray(inputs["L0_attn_b_bC"], f32)
    WUQ = np.asarray(inputs["L0_attn_b_WUQ"], f32)
    WUK = np.asarray(inputs["L0_attn_b_WUK"], f32)
    WUV = np.asarray(inputs["L0_attn_b_WUV"], f32)
    WZ = np.asarray(inputs["L0_attn_b_WZ"], f32)
    bZ = np.asarray(inputs["L0_attn_b_bZ"], f32)
    wealth = np.asarray(inputs["L0_ffn_expert_wealth"], f32)
    confW = np.asarray(inputs["L0_ffn_conf_W"], f32)
    confb = np.asarray(inputs["L0_ffn_conf_b"], f32)
    bgate = np.asarray(inputs["L0_ffn_base_gate"], f32)
    bup = np.asarray(inputs["L0_ffn_base_up"], f32)
    bdown = np.asarray(inputs["L0_ffn_base_down"], f32)
    gateA = np.asarray(inputs["L0_ffn_gate_A"], f32)
    gateB = np.asarray(inputs["L0_ffn_gate_B"], f32)
    upA = np.asarray(inputs["L0_ffn_up_A"], f32)
    upB = np.asarray(inputs["L0_ffn_up_B"], f32)
    downA = np.asarray(inputs["L0_ffn_down_A"], f32)
    downB = np.asarray(inputs["L0_ffn_down_B"], f32)

    C = lambda a: np.ascontiguousarray(a, dtype=f32)
    identb = np.eye(128, dtype=BFD)
    identf = np.eye(128, dtype=f32)
    onesf = np.ones((1, 128), f32)
    diagm = np.where(np.arange(128)[None, :] > np.arange(128)[:, None],
                     NEG, 0.0).astype(f32)
    cos_k, sin_k = _rope_tables(T, DHR)
    cos_q, sin_q = _rope_tables(T, 2 * DL)
    sgn_k = np.where(np.arange(DHR) % 2 == 0, -1.0, 1.0).astype(f32)
    sgn_q = np.where(np.arange(2 * DL) % 2 == 0, -1.0, 1.0).astype(f32)
    sin_k = sin_k * sgn_k[None, :]
    sin_q = sin_q * sgn_q[None, :]

    slopes = 2.0 ** (-8.0 / np.arange(1, NH + 1))
    ii = np.arange(128)[:, None]
    jj = np.arange(512)[None, :]

    in_maps = []
    for c in range(8):
        b, g = c // 4, c % 4
        heads = slopes[g * HPC:(g + 1) * HPC]
        al5 = (heads[:, None, None] * (jj - ii)[None]).astype(f32)
        jd = jj[:, :128]
        ald = (heads[:, None, None] * (jd - ii)[None]).astype(f32)
        ald = ald + np.where(jd > ii, NEG, 0.0)[None]
        ald = ald.astype(f32)
        alc = np.zeros((128, HPC * 64), f32)
        for hh in range(HPC):
            alc[:, hh * 64:(hh + 1) * 64] = (
                -heads[hh] * 128.0 * np.arange(64, dtype=f32))[None, :]
        wcs_full = _swap_pairs(WC, axis=1)
        m = {
            "x0": C(x_in[b, g * TOK:(g + 1) * TOK, :]),
            "wq": C(Wqkv[:, g * 512:(g + 1) * 512]),
            "wk": C(Wqkv[:, D + g * 512:D + (g + 1) * 512]),
            "wv": C(Wqkv[:, 2 * D + g * 512:2 * D + (g + 1) * 512]),
            "wo": C(Wo[g * 512:(g + 1) * 512, :]),
            "wa": C(WA), "ba": C(bA.reshape(8, 128).T),
            "wb": C(WB), "wbs": C(_swap_pairs(WB, axis=1)),
            "bb": C(bB.reshape(DHR, 1)),
            "bbs": C(_swap_pairs(bB, axis=0).reshape(DHR, 1)),
            "wc": C(WC[:, g * 256:(g + 1) * 256]),
            "wcs": C(wcs_full[:, g * 256:(g + 1) * 256]),
            "bc": C(bC[g * 256:(g + 1) * 256].reshape(2, 128).T),
            "bcs": C(_swap_pairs(bC, axis=0)[g * 256:(g + 1) * 256].reshape(2, 128).T),
            "wuq": C(WUQ[:, g * 512:(g + 1) * 512]),
            "wuk": C(WUK[:, g * 512:(g + 1) * 512]),
            "wuv": C(WUV[:, g * 512:(g + 1) * 512]),
            "wz": C(WZ[g * 512:(g + 1) * 512, :]),
            "bzrep": C(np.broadcast_to(bZ[None, :], (128, D))),
            "wealth": C(wealth.reshape(NE, 1)),
            "confw": C(confW.T), "confb": C(confb.reshape(NE, 1)),
            "bgate": C(bgate.reshape(D // 128, 128, DI // 128, 128)
                      .transpose(2, 1, 0, 3).reshape(DI // 128 * 128, D)),
            "bup": C(bup.reshape(D // 128, 128, DI // 128, 128)
                     .transpose(2, 1, 0, 3).reshape(DI // 128 * 128, D)),
            "bdown": C(bdown),
            "gA": C(gateA.transpose(1, 0, 2).reshape(D, NE * RK)),
            "uA": C(upA.transpose(1, 0, 2).reshape(D, NE * RK)),
            "gB": C(gateB.reshape(NE // 2, 2, RK, DI // 128, 128)
                    .transpose(3, 1, 2, 0, 4).reshape(DI // 128 * 128,
                                                      NE // 2 * 128)),
            "uB": C(upB.reshape(NE // 2, 2, RK, DI // 128, 128)
                    .transpose(3, 1, 2, 0, 4).reshape(DI // 128 * 128,
                                                      NE // 2 * 128)),
            "dA": C(downA.transpose(1, 0, 2).reshape(DI, NE * RK)),
            "dB": C(downB.reshape(NE * RK, D)),
            "identb": identb, "identf": identf, "onesf": onesf,
            "alibi5": al5, "alibid": ald, "alibic": alc, "diagm": diagm,
            "coskr": C(cos_k.T), "sinkr": C(sin_k.T),
            "cosqr": C(cos_q.T[g * 256:(g + 1) * 256, :]),
            "sinqr": C(sin_q.T[g * 256:(g + 1) * 256, :]),
        }
        in_maps.append(m)

    nc = build_nc()
    res = run_bass_kernel_spmd(nc, in_maps, core_ids=list(range(8)))
    out = np.empty((B, T, D), dtype=f32)
    for c in range(8):
        b, g = c // 4, c % 4
        out[b, g * TOK:(g + 1) * TOK, :] = res.results[c]["y"]
    return out
